# version 13
# baseline (speedup 1.0000x reference)
import math
import numpy as np

import concourse.bass as bass
import concourse.mybir as mybir
from concourse.bass_utils import run_bass_kernel_spmd

F32 = mybir.dt.float32
BF16 = mybir.dt.bfloat16
I32 = mybir.dt.int32
AF = mybir.ActivationFunctionType
OP = mybir.AluOpType

D, FF, T = 1024, 4096, 2048
NCH, NF, NTC, NTB = 8, 32, 16, 4
NH, DH, NL = 16, 64, 64
MAGIC = float(np.float32(3 * 2 ** 14))
EPS_LN = 1e-5
REG = 1e-6
ENGINES = ("pe", "dve", "act", "pool", "sp")


def _build_pwl():
    X = None
    try:
        import jax
        import jax.numpy as jnp
        with jax.default_device(jax.devices("cpu")[0]):
            Xj = jnp.linspace(-4.0, 4.0, 33, dtype=jnp.float32)
            Yj = 0.5 * Xj * (1.0 + jax.lax.erf(Xj / jnp.sqrt(jnp.float32(2.0))))
            X = np.asarray(Xj, np.float32)
            Y = np.asarray(Yj, np.float32)
    except Exception:
        X = None
    if X is None:
        X = np.linspace(-4.0, 4.0, 33).astype(np.float32)
        Y = np.float32([0.5 * float(v) * (1.0 + math.erf(float(v) / math.sqrt(2.0)))
                        for v in X])
    X64, Y64 = X.astype(np.float64), Y.astype(np.float64)
    s = (Y64[1:] - Y64[:-1]) / (X64[1:] - X64[:-1])
    cp = np.empty(33, np.float64)
    cp[0] = s[0]
    cp[1:32] = s[1:] - s[:-1]
    cp[32] = 1.0 - s[31]
    XK = np.concatenate([X64[:32], [4.0]])
    kconst = float(np.float32(Y64[0] - np.sum(cp * XK)))
    terms = [(float(np.float32(cp[j])), float(np.float32(cp[j] * XK[j])),
              bool(cp[j] > 0)) for j in range(33)]
    knots = [float(np.float32(XK[j])) for j in range(33)]
    return terms, kconst, knots


PWL_TERMS, PWL_K, PWL_KNOTS = _build_pwl()

M4 = 196608.0


def _ref_t2i(in0, in1, s0, s1, imm2):
    return np.maximum(in0, s0) * s1 + np.maximum(in0, s0 + 1.0) * imm2


def _ref_t2a(in0, in1, s0, s1, imm2):
    return (in1 + np.maximum(in0, s0) * s1) + np.maximum(in0, s0 + 1.0) * imm2


def _ref_t1ak(in0, in1, s0, s1, imm2):
    return (in1 + np.maximum(in0, s0) * s1) + imm2


def _install_pwl_ops():
    from concourse import dve_ops as _dvo
    from concourse.dve_op import DveOp
    from concourse.dve_spec import Spec, Src0, Src1, C0, C1, C2, One, maxx
    from concourse.dve_spec import lower as _lower
    if any(op.name == "PWL_T2A" for op in _dvo.OPS):
        return {op.name: op for op in _dvo.OPS
                if op.name in ("PWL_T2I", "PWL_T2A", "PWL_T1AK")}
    defs = [
        ("PWL_T2I",
         Spec(body=maxx(Src0, C0) * C1 + maxx(Src0, C0 + One) * C2),
         _ref_t2i, "append"),
        ("PWL_T2A",
         Spec(body=(Src1 + maxx(Src0, C0) * C1)
              + maxx(Src0, C0 + One) * C2),
         _ref_t2a, "append"),
        ("PWL_T1AK",
         Spec(body=(Src1 + maxx(Src0, C0) * C1) + C2),
         _ref_t1ak, 0),
    ]
    out = {}
    for name, spec, ref, slot in defs:
        sha = {"v3": _lower(spec, ver="v3", perf_en=False, name=name).sha("v3")}
        op = DveOp(name, spec, subdim=False, uops_sha=sha, reference=ref)
        if slot == "append":
            _dvo.OPS.append(op)
            row = _dvo._CUSTOM_DVE_ROW_BASE + len(_dvo.OPS) - 1
        else:
            old = _dvo.OPS[slot]
            _dvo.OPS[slot] = op
            _dvo.CUSTOM_DVE_SPECS.pop(old.name, None)
            row = _dvo._CUSTOM_DVE_ROW_BASE + slot
        assert row < 0x20
        _dvo.CUSTOM_DVE_SPECS[name] = spec
        _dvo.CUSTOM_DVE_REFS[name] = ref
        _dvo._SUB_OPCODE_FOR_NAME[name] = row
        out[name] = op
    return out


_PWL_OPS = _install_pwl_ops()

_PWL4 = [(float(np.float32(4.0) * np.float32(PWL_KNOTS[_j])),
          float(np.float32(PWL_TERMS[_j][0]) / 4.0)) for _j in range(33)]
for _j in range(32):
    assert _PWL4[_j + 1][0] == _PWL4[_j][0] + 1.0, (_j, _PWL4[_j], _PWL4[_j + 1])


class Buf:
    __slots__ = ("w", "r", "name")

    def __init__(self, name=""):
        self.w, self.r, self.name = None, [], name


class Group:
    __slots__ = ("eng", "emit", "deps", "dma", "need_inc", "cum", "gid", "ndma",
                 "slot", "slot_prev", "phase")

    def __init__(self, eng, emit, deps, dma, ndma):
        self.eng, self.emit, self.deps = eng, emit, deps
        self.dma, self.ndma = dma, ndma
        self.need_inc, self.cum, self.gid = False, 0, -1


class Sched:
    def __init__(self):
        self.groups = []
        self.by_eng = {e: [] for e in ENGINES}
        self._bar = {}
        self.phase = "init"

    def add(self, eng, emit, reads=(), writes=(), deps=(), dma=False, ndma=0):
        ds = set()
        for b in reads:
            if b.w is not None:
                ds.add(b.w)
        for b in writes:
            if b.w is not None:
                ds.add(b.w)
            ds.update(b.r)
        for gg in deps:
            if gg is not None:
                ds.add(gg)
        ds.update(self._bar.values())
        grp = Group(eng, emit, ds, dma, ndma)
        grp.phase = self.phase
        grp.gid = len(self.groups)
        self.groups.append(grp)
        self.by_eng[eng].append(grp)
        for b in reads:
            b.r.append(grp)
        for b in writes:
            b.w, b.r = grp, []
        return grp

    def barrier(self):
        self._bar = {e: self.by_eng[e][-1] for e in ENGINES if self.by_eng[e]}


def build_nc():
    import os
    LIM = int(os.environ.get("KLIM", "99"))
    SUB = int(os.environ.get("KSUB", "9"))
    nc = bass.Bass("TRN2", target_bir_lowering=False,
                   detect_race_conditions=False)
    sched = Sched()
    B = {}

    def buf(name):
        if name not in B:
            B[name] = Buf(name)
        return B[name]

    x_in = nc.dram_tensor("x", [D, T], F32, kind="ExternalInput")
    w1_d = nc.dram_tensor("w1", [D, FF], F32, kind="ExternalInput")
    w2_d = nc.dram_tensor("w2", [FF, D], F32, kind="ExternalInput")
    wq_d = nc.dram_tensor("wqt", [D, D], F32, kind="ExternalInput")
    wk_d = nc.dram_tensor("wkt", [D, D], F32, kind="ExternalInput")
    wv_d = nc.dram_tensor("wvt", [D, D], F32, kind="ExternalInput")
    wo_d = nc.dram_tensor("wot", [D, D], F32, kind="ExternalInput")
    b1_d = nc.dram_tensor("b1", [128, NF], F32, kind="ExternalInput")
    b2_d = nc.dram_tensor("b2", [128, NCH], F32, kind="ExternalInput")
    bq_d = nc.dram_tensor("bq", [128, NCH], F32, kind="ExternalInput")
    bk_d = nc.dram_tensor("bk", [128, NCH], F32, kind="ExternalInput")
    bv_d = nc.dram_tensor("bv", [128, NCH], F32, kind="ExternalInput")
    bo_d = nc.dram_tensor("bo", [128, NCH], F32, kind="ExternalInput")
    bkrow_d = nc.dram_tensor("bkrow", [1, D], F32, kind="ExternalInput")
    bvrow_d = nc.dram_tensor("bvrow", [1, D], F32, kind="ExternalInput")
    lidx_d = nc.dram_tensor("lidx", [1, NL], I32, kind="ExternalInput")
    hind_d = nc.dram_tensor("hind", [2, 128], F32, kind="ExternalInput")
    temp_d = nc.dram_tensor("temp", [1, 1], F32, kind="ExternalInput")
    coff_d = nc.dram_tensor("coff", [1, 1], F32, kind="ExternalInput")
    out_d = nc.dram_tensor("out", [D, T], F32, kind="ExternalOutput")
    dbg_d = nc.dram_tensor("dbg", [128, 8192], F32, kind="ExternalOutput")

    x1_d = nc.dram_tensor("x1scr", [D, T], F32)
    yq_d = nc.dram_tensor("yqscr", [FF, T], BF16)
    w1b_d = nc.dram_tensor("w1bscr", [128, NCH * FF], BF16)
    w2b_d = nc.dram_tensor("w2bscr", [128, NCH * FF], BF16)
    cc1_in = nc.dram_tensor("cc1in", [NL, D], F32)
    cc1_out = nc.dram_tensor("cc1out", [NL, D], F32)
    cc2_in = nc.dram_tensor("cc2in", [NL, NH * 65], F32)
    cc2_out = nc.dram_tensor("cc2out", [NL, NH * 65], F32)
    RG = [[0, 1], [2, 3], [4, 5], [6, 7]]

    import contextlib
    ctx = contextlib.ExitStack()

    def sbt(name, shape, dt):
        return ctx.enter_context(nc.sbuf_tensor(name, shape, dt))

    AR = sbt("AR", [128, 49152], F32)
    ARB = AR[:, :].bitcast(BF16)

    b1sb = sbt("b1sb", [128, NF], F32)
    b1q4 = sbt("b1q4", [128, NF], F32)
    b2sb = sbt("b2sb", [128, NCH], F32)
    bqsb = sbt("bqsb", [128, NCH], F32)
    bksb = sbt("bksb", [128, NCH], F32)
    bosb = sbt("bosb", [128, NCH], F32)
    sm = sbt("sm", [128, 256], F32)
    ident = sbt("ident", [128, 128], F32)
    identb = sbt("identb", [128, 128], BF16)
    identdual = sbt("identdual", [128, 64], F32)
    iepsdual = sbt("iepsdual", [128, 64], F32)
    halfind = sbt("halfind", [2, 128], F32)
    onesb = sbt("onesb", [128, 4], BF16)
    onesrb = sbt("onesrb", [1, 128], BF16)
    Esb = sbt("Esb", [128, NTC * NL], BF16)
    idxsb = sbt("idxsb", [1, NL], I32)
    Lfmb = sbt("Lfmb", [128, 8 * DH], BF16)
    Lfmze = sbt("Lfmze", [128, 8 * DH], BF16)
    Lfmzo = sbt("Lfmzo", [128, 8 * DH], BF16)
    ones2 = sbt("ones2", [128, 64], BF16)
    Lnpair = sbt("Lnpair", [128, 8], F32)
    elcol = sbt("elcol", [64, NH], F32)
    lncol = sbt("lncol", [64, NH], F32)
    lnrows = sbt("lnrows", [16, 64], F32)
    lnpr2 = sbt("lnpr2", [2, 128], F32)
    knh = sbt("knh", [128, NTC * NH], F32)
    ybsb = sbt("ybsb", [128, 8 * 65], BF16)
    trsc = sbt("trsc", [64, 64], BF16)
    indsb = sbt("indsb", [128, 2], F32)

    psum = ctx.enter_context(nc.psum_tensor("ps", [128, 4096], F32))
    sems = {e: ctx.enter_context(nc.semaphore("s_" + e)) for e in ENGINES}
    NSLOT = 10
    dslot = {q: [ctx.enter_context(nc.semaphore("d_%s%d" % (q, i)))
                 for i in range(NSLOT)] for q in ("sp", "pool")}
    ccsem1 = ctx.enter_context(nc.semaphore("cc1s"))
    ccsem2 = ctx.enter_context(nc.semaphore("cc2s"))

    XRES, XHAT = 0, 16384
    TMPA, TMPB, TMPC = 24576, 26624, 28672
    ROWS = 30720
    WBLK, YQIN = 34816, 38912
    WSTG = 43008
    BFTA, BFTC = 47104, 48128

    def f32(off, n, p0=0, p1=128):
        return AR[p0:p1, off:off + n]

    def bft(off_f32, nb, p0=0, p1=128):
        return ARB[p0:p1, 2 * off_f32:2 * off_f32 + nb]

    tmpa = f32(TMPA, 2048)
    tmpb = f32(TMPB, 2048)
    tmpc = f32(TMPC, 2048)
    bfta = bft(BFTA, 2048)
    bftc = bft(BFTC, 2048)

    iota_c = sm[:, 8:9]
    rcol = sm[:, 1:2]
    negr = sm[:, 2:3]
    rcol2 = sm[:, 3:4]
    coffcol = sm[:, 5:6]
    onesrow = sm[0:1, 128:256]

    nc.const_aps.register(F32, float(EPS_LN), sm[:, 11:12])
    nc.const_aps.register(F32, 0.0, sm[:, 12:13])
    nc.const_aps.register(F32, 1.0, sm[:, 13:14])
    nc.const_aps.register(F32, M4, sm[:, 6:7])
    nc.const_aps.register(F32, -M4, sm[:, 7:8])
    nc.const_aps.register(F32, MAGIC, sm[:, 14:15])
    nc.const_aps.register(F32, -MAGIC, sm[:, 15:16])

    bank = [buf("pbank%d" % i) for i in range(8)]

    def g(eng, emit, reads=(), writes=(), deps=()):
        return sched.add(eng, emit, reads, writes, deps)

    def dmag(eng, emit, n, reads=(), writes=(), deps=()):
        return sched.add(eng, emit, reads, writes, deps, dma=True, ndma=n)

    xres_b = [buf("xres%d" % c) for c in range(NCH)]
    xhat_b = [buf("xhat%d" % c) for c in range(NCH)]

    dmag("sp", lambda e: [e.dma_start(f32(XRES + c * T, T),
                                      x_in[c * 128:(c + 1) * 128, :])
                          for c in range(NCH)], NCH, writes=xres_b)
    dmag("sp", lambda e: [e.dma_start(b1sb[:, :], b1_d[:, :]),
                          e.dma_start(b2sb[:, :], b2_d[:, :]),
                          e.dma_start(bqsb[:, :], bq_d[:, :]),
                          e.dma_start(bksb[:, :], bk_d[:, :]),
                          e.dma_start(bosb[:, :], bo_d[:, :]),
                          e.dma_start(idxsb[:, :], lidx_d[:, :]),
                          e.dma_start(sm[0:1, 0:1], temp_d[:, :]),
                          e.dma_start(sm[0:1, 4:5], coff_d[:, :]),
                          e.dma_start(halfind[:, :], hind_d[:, :])],
         9, writes=[buf("consts"), buf("hind")])
    g("dve", lambda e: e.tensor_scalar(b1q4[:, :], b1sb[:, :], 4.0, None,
                                       OP.mult),
      reads=[buf("consts")], writes=[buf("b1q4")])

    def ph_iota(e):
        e.iota(iota_c, [[0, 1]], channel_multiplier=1,
               allow_small_or_imprecise_dtypes=True)
        return e.iota(ident[:, :], [[1, 128]], channel_multiplier=0,
                      allow_small_or_imprecise_dtypes=True)
    g("pool", ph_iota, writes=[buf("iota"), buf("identraw")])

    def ph_ones(e):
        e.memset(sm[:, 11:12], EPS_LN)
        e.memset(sm[:, 12:13], 0.0)
        e.memset(sm[:, 13:14], 1.0)
        e.memset(sm[:, 6:7], M4)
        e.memset(sm[:, 7:8], -M4)
        e.memset(sm[:, 14:15], MAGIC)
        e.memset(sm[:, 15:16], -MAGIC)
        e.memset(onesrow, 1.0)
        e.memset(onesb[:, :], 1.0)
        e.memset(ones2[:, :], 1.0)
        e.memset(Lfmze[:, :], 0.0)
        e.memset(Lfmzo[:, :], 0.0)
        return e.memset(onesrb[:, :], 1.0)
    g("dve", ph_ones, writes=[buf("ones")])

    def ph_ident(e):
        e.tensor_scalar(ident[:, :], ident[:, :], iota_c, None, OP.is_equal)
        e.tensor_scalar(indsb[:, 0:1], iota_c, 64.0, None, OP.is_lt)
        e.tensor_copy(identb[:, :], ident[:, :])
        e.tensor_scalar(indsb[:, 1:2], iota_c, 63.5, None, OP.is_gt)
        e.tensor_copy(identdual[0:64, :], ident[0:64, 0:64])
        return e.tensor_scalar(iepsdual[0:64, :], ident[0:64, 0:64], REG, None,
                               OP.mult)
    g("dve", ph_ident, reads=[buf("iota"), buf("identraw")],
      writes=[buf("ident"), buf("idual_t"), buf("ind")])
    dmag("pool", lambda e: [e.dma_start(identdual[64:128, :], identdual[0:64, :]),
                            e.dma_start(iepsdual[64:128, :], iepsdual[0:64, :])],
         2, reads=[buf("idual_t")], writes=[buf("idual")])

    g("pe", lambda e: e.matmul(psum[:, 0:1], onesrow, sm[0:1, 0:1],
                               start=True, stop=True),
      reads=[buf("consts"), buf("ones")], writes=[bank[0]])

    def ph_tder(e):
        e.reciprocal(rcol, psum[:, 0:1])
        e.tensor_scalar(sm[:, 10:11], psum[:, 0:1], -1.0, None, OP.mult)
        e.tensor_copy(sm[:, 0:1], psum[:, 0:1])
        e.tensor_scalar(rcol2, rcol, 2.0, None, OP.mult)
        return e.reciprocal(negr, sm[:, 10:11])
    gtd = g("dve", ph_tder, reads=[bank[0]], writes=[buf("temps")])

    g("pe", lambda e: e.matmul(psum[:, 2:3], onesrow, sm[0:1, 4:5],
                               start=True, stop=True),
      reads=[buf("consts"), buf("ones")], writes=[bank[0]], deps=[gtd])
    g("act", lambda e: e.activation(coffcol, psum[:, 2:3], AF.Identity,
                                    bias=0.0, scale=1.0),
      reads=[bank[0]], writes=[buf("coffcol")])

    def ph_idx(e):
        e.tensor_copy(sm[0:1, 16:16 + NL], idxsb[0:1, :])
        e.memset(sm[:, 9:10], 0.0)
        return e.tensor_scalar(sm[0:1, 16:16 + NL], sm[0:1, 16:16 + NL],
                               0.0, float(2 * T - 1), OP.max, OP.min)
    g("dve", ph_idx, reads=[buf("consts")], writes=[buf("idxrow")])
    g("pe", lambda e: e.matmul(psum[:, 512:512 + NL], onesrow,
                               sm[0:1, 16:16 + NL], start=True, stop=True),
      reads=[buf("idxrow"), buf("ones")], writes=[bank[1]])

    def ph_E(e):
        last = None
        for tc in range(NTC):
            e.tensor_scalar(tmpa[:, tc * NL:(tc + 1) * NL],
                            psum[:, 512:512 + NL],
                            float(-128 * tc), coffcol, OP.add, OP.subtract)
        for tc in range(NTC):
            last = e.tensor_scalar(Esb[:, tc * NL:(tc + 1) * NL],
                                   tmpa[:, tc * NL:(tc + 1) * NL], iota_c,
                                   None, OP.is_equal)
        return last
    g("dve", ph_E, reads=[bank[1], buf("iota"), buf("coffcol")],
      writes=[buf("E")])
    sched.barrier()

    def ln_stats(xb_fn, xbufs, tag):
        sqsl = lambda dc: (bfta if dc < 4 else bftc)[:, (dc % 4) * 512:
                                                     (dc % 4) * 512 + 512]
        for nb in range(NTB):
            g("pe", (lambda nb=nb: lambda e: [e.matmul(
                psum[0:1, 3072:3584], onesb[:, 0:1],
                xb_fn(dc)[:, nb * 512:nb * 512 + 512],
                start=(dc == 0), stop=(dc == NCH - 1))
                for dc in range(NCH)][-1])(),
              reads=list(xbufs) + [buf("ones")], writes=[bank[6]])
            g("dve", (lambda nb=nb: lambda e: e.tensor_copy(
                f32(ROWS + nb * 512, 512, 0, 1), psum[0:1, 3072:3584]))(),
              reads=[bank[6]], writes=[buf("sx" + tag)])
            gsq = g("dve", (lambda nb=nb: lambda e: [e.tensor_tensor(
                sqsl(dc),
                xb_fn(dc)[:, nb * 512:nb * 512 + 512],
                xb_fn(dc)[:, nb * 512:nb * 512 + 512], OP.mult)
                for dc in range(NCH)][-1])(),
                reads=list(xbufs), writes=[buf("sq" + tag)])
            g("pe", (lambda nb=nb: lambda e: [e.matmul(
                psum[0:1, 3584:4096], onesb[:, 0:1], sqsl(dc),
                start=(dc == 0), stop=(dc == NCH - 1))
                for dc in range(NCH)][-1])(),
              reads=[buf("sq" + tag), buf("ones")], writes=[bank[7]],
              deps=[gsq])
            g("dve", (lambda nb=nb: lambda e: e.tensor_copy(
                f32(ROWS + 2048 + nb * 512, 512, 0, 1), psum[0:1, 3584:4096]))(),
              reads=[bank[7]], writes=[buf("sx2" + tag)])

        def rowmath(e):
            mean = f32(ROWS, T, 0, 1)
            ex2 = f32(ROWS + 2048, T, 0, 1)
            e.tensor_scalar(mean, mean, 1.0 / D, None, OP.mult)
            e.tensor_scalar(ex2, ex2, 1.0 / D, None, OP.mult)
            e.tensor_tensor(tmpa[0:1, :], mean, mean, OP.mult)
            return e.tensor_tensor(ex2, ex2, tmpa[0:1, :], OP.subtract)
        g("dve", rowmath, reads=[buf("sx" + tag), buf("sx2" + tag)],
          writes=[buf("var" + tag), buf("sx" + tag)])
        g("act", lambda e: e.activation(f32(ROWS + 2048, T, 0, 1),
                                        f32(ROWS + 2048, T, 0, 1), AF.Sqrt,
                                        bias=EPS_LN, scale=1.0),
          reads=[buf("var" + tag)], writes=[buf("sd" + tag)])

        def rowmath2(e):
            mean = f32(ROWS, T, 0, 1)
            sd = f32(ROWS + 2048, T, 0, 1)
            e.reciprocal(tmpa[0:1, :], sd)
            e.tensor_tensor(tmpb[0:1, :], mean, tmpa[0:1, :], OP.mult)
            e.tensor_copy(mean, tmpa[0:1, :])
            return e.tensor_copy(sd, tmpb[0:1, :])
        return g("dve", rowmath2, reads=[buf("sd" + tag), buf("sx" + tag)],
                 writes=[buf("rows" + tag)])

    def ln_apply(src_fn, srcbufs, dst_fn, dstbufs, tag, gstat):
        for nb in range(NTB):
            gb1 = g("pe", (lambda nb=nb: lambda e: e.matmul(
                psum[:, 2048:2560], onesrow, f32(ROWS + nb * 512, 512, 0, 1),
                start=True, stop=True))(),
                reads=[buf("rows" + tag), buf("ones")], writes=[bank[4]],
                deps=[gstat])
            gb2 = g("pe", (lambda nb=nb: lambda e: e.matmul(
                psum[:, 2560:3072], onesrow,
                f32(ROWS + 2048 + nb * 512, 512, 0, 1),
                start=True, stop=True))(),
                reads=[buf("rows" + tag), buf("ones")], writes=[bank[5]],
                deps=[gstat])
            for dc in range(NCH):
                gm = g("dve", (lambda nb=nb, dc=dc: lambda e: e.tensor_tensor(
                    tmpa[:, nb * 512:nb * 512 + 512],
                    src_fn(dc)[:, nb * 512:nb * 512 + 512],
                    psum[:, 2048:2560], OP.mult))(),
                    reads=[srcbufs[dc], bank[4]], writes=[buf("lnt" + tag)],
                    deps=[gb1])
                g("dve", (lambda nb=nb, dc=dc: lambda e: e.tensor_tensor(
                    dst_fn(dc)[:, nb * 512:nb * 512 + 512],
                    tmpa[:, nb * 512:nb * 512 + 512],
                    psum[:, 2560:3072], OP.subtract))(),
                  reads=[buf("lnt" + tag), bank[5]], writes=[dstbufs[dc]],
                  deps=[gb2, gm])

    def ffn_mm1(xhat_fn, xhatbufs, tag, first):
        wb = buf("wblk" + tag)
        for fb in range(4):
            for kc in range(NCH):
                st = buf("wstg%d" % (kc % 2))
                if first:
                    gl = dmag("sp", (lambda kc=kc, fb=fb: lambda e: [e.dma_start(
                        f32(WSTG + (kc % 2) * 2048, 1024),
                        w1_d[kc * 128:(kc + 1) * 128,
                             fb * 1024:(fb + 1) * 1024])])(), 1, writes=[st])
                    gc = g("dve", (lambda kc=kc: lambda e: e.tensor_copy(
                        bft(WBLK, 8192)[:, kc * 1024:(kc + 1) * 1024],
                        f32(WSTG + (kc % 2) * 2048, 1024)))(),
                        reads=[st], writes=[wb], deps=[gl])
                    dmag("pool", (lambda kc=kc, fb=fb: lambda e: [e.dma_start(
                        w1b_d[:, kc * FF + fb * 1024:kc * FF + (fb + 1) * 1024],
                        bft(WBLK, 8192)[:, kc * 1024:(kc + 1) * 1024])])(),
                        1, reads=[wb], deps=[gc])
                else:
                    dmag("sp", (lambda kc=kc, fb=fb: lambda e: [e.dma_start(
                        bft(WBLK, 8192)[:, kc * 1024:(kc + 1) * 1024],
                        w1b_d[:, kc * FF + fb * 1024:kc * FF + (fb + 1) * 1024])])(),
                        1, writes=[wb])
            for fci in range(8):
                fc = fb * 8 + fci
                pb = (fc % 2) * 4
                pbufs = [bank[pb + i] for i in range(4)]

                def mm1(e, fci=fci, pb=pb):
                    last = None
                    for kc in range(NCH):
                        for nb in range(NTB):
                            last = e.matmul(
                                psum[:, (pb + nb) * 512:(pb + nb) * 512 + 512],
                                bft(WBLK, 8192)[:, kc * 1024 + fci * 128:
                                                kc * 1024 + fci * 128 + 128],
                                xhat_fn(kc)[:, nb * 512:nb * 512 + 512],
                                start=(kc == 0), stop=(kc == NCH - 1))
                    return last
                g("pe", mm1, reads=[wb] + list(xhatbufs), writes=pbufs)
                g("act", (lambda fc=fc, pb=pb: lambda e: e.activation(
                    tmpa[:, :], psum[:, pb * 512:pb * 512 + 2048], AF.Identity,
                    bias=b1q4[:, fc:fc + 1], scale=4.0))(),
                  reads=pbufs + [buf("b1q4")], writes=[buf("h" + tag)])
                g("dve", lambda e: e.tensor_scalar(
                    tmpb[:, :], tmpa[:, :], M4, M4, OP.add, OP.subtract),
                  reads=[buf("h" + tag)], writes=[buf("hq" + tag)])

                def pwl(e):
                    T2I, T2A, T1AK = (_PWL_OPS["PWL_T2I"], _PWL_OPS["PWL_T2A"],
                                      _PWL_OPS["PWL_T1AK"])
                    e._custom_dve(T2I, out=tmpc[:, :], in0=tmpb[:, :],
                                  s0=_PWL4[0][0], s1=_PWL4[0][1],
                                  imm2=_PWL4[1][1])
                    for j in range(2, 32, 2):
                        e._custom_dve(T2A, out=tmpc[:, :], in0=tmpb[:, :],
                                      in1=tmpc[:, :], s0=_PWL4[j][0],
                                      s1=_PWL4[j][1], imm2=_PWL4[j + 1][1])
                    e._custom_dve(T1AK, out=tmpc[:, :], in0=tmpb[:, :],
                                  in1=tmpc[:, :], s0=_PWL4[32][0],
                                  s1=_PWL4[32][1], imm2=PWL_K)
                    return e.tensor_scalar(bfta[:, :], tmpc[:, :], MAGIC, MAGIC,
                                           OP.add, OP.subtract)
                gp = g("dve", pwl, reads=[buf("hq" + tag)],
                       writes=[buf("yqo" + tag), buf("h" + tag),
                               buf("sq" + tag)])
                dmag("pool", (lambda fc=fc: lambda e: [e.dma_start(
                    yq_d[fc * 128:(fc + 1) * 128, :], bfta[:, :])])(), 1,
                    reads=[buf("yqo" + tag)],
                    writes=[buf("yqd%d" % fb)], deps=[gp])

    def ffn_mm2(res_fn, resbufs, cast_fn, castbufs, dma_out, tag, first):
        w2stA = bft(WSTG + 1024, 1024)
        w2stB = bft(WSTG + 1536, 1024)
        u2 = [f32(WSTG + 3072, 512), f32(WSTG + 3584, 512)]
        u2b_ = [buf("u2a"), buf("u2b")]
        if first:
            for dc in range(NCH):
                gl = dmag("sp", (lambda dc=dc: lambda e: [e.dma_start(
                    f32(WSTG, 4096).rearrange("p (c n) -> p c n", n=128),
                    w2_d[:, dc * 128:(dc + 1) * 128].rearrange(
                        "(c p) n -> p c n", p=128))])(), 1,
                    writes=[buf("wstgA"), buf("wstg0"), buf("wstg1"),
                            buf("w2q0"), buf("w2q1")] + u2b_)
                gc = g("dve", (lambda dc=dc: lambda e: [
                    e.tensor_copy(bft(YQIN, 2048), f32(WSTG, 2048)),
                    e.tensor_copy(bft(YQIN + 1024, 2048),
                                  f32(WSTG + 2048, 2048))][-1])(),
                    reads=[buf("wstgA"), buf("wstg0"), buf("wstg1"),
                           buf("w2q0"), buf("w2q1"), buf("u2a"), buf("u2b")],
                    writes=[buf("w2cst"), buf("yqq0"), buf("yqq1")],
                    deps=[gl])
                dmag("pool", (lambda dc=dc: lambda e: [
                    e.dma_start(w2b_d[:, dc * 4096:dc * 4096 + 2048],
                                bft(YQIN, 2048)),
                    e.dma_start(w2b_d[:, dc * 4096 + 2048:dc * 4096 + 4096],
                                bft(YQIN + 1024, 2048))])(), 2,
                    reads=[buf("w2cst"), buf("yqq0"), buf("yqq1"),
                           buf("yqguard")],
                    writes=[buf("w2bd%d" % dc)])
        for tb in range(NTB):
            for qtr in range(4):
                par = (tb * 4 + qtr) % 2
                yqb = buf("yqq%d" % par)
                gyq = dmag("pool", (lambda tb=tb, qtr=qtr, par=par:
                                    lambda e: [e.dma_start(
                    bft(YQIN + par * 2048, 4096)[:, k * 512:k * 512 + 512],
                    yq_d[(qtr * 8 + k) * 128:(qtr * 8 + k + 1) * 128,
                         tb * 512:tb * 512 + 512]) for k in range(8)])(),
                    8, reads=[buf("yqd%d" % qtr)],
                    writes=[yqb, buf("yqguard")])
                for dc in range(NCH):
                    wdb = buf("w2q%d" % (dc % 2))
                    wst = w2stA if dc % 2 == 0 else w2stB
                    gw = dmag("sp", (lambda dc=dc, qtr=qtr, wst=wst:
                                     lambda e: [e.dma_start(
                        wst[:, :],
                        w2b_d[:, dc * 4096 + qtr * 1024:
                              dc * 4096 + (qtr + 1) * 1024])])(), 1,
                        reads=[buf("w2bd%d" % dc)], writes=[wdb])

                    def mm2(e, dc=dc, qtr=qtr, par=par, wst=wst):
                        last = None
                        for k in range(8):
                            last = e.matmul(
                                psum[:, dc * 512:dc * 512 + 512],
                                wst[:, k * 128:k * 128 + 128],
                                bft(YQIN + par * 2048, 4096)
                                [:, k * 512:k * 512 + 512],
                                start=(qtr == 0 and k == 0),
                                stop=(qtr == 3 and k == 7))
                        return last
                    g("pe", mm2, reads=[wdb, yqb], writes=[bank[dc]],
                      deps=[gw, gyq])
            for dp in range(0, 8, 2):
                g("act", (lambda dp=dp: lambda e: [
                    e.activation(u2[0][:, :],
                                 psum[:, dp * 512:dp * 512 + 512],
                                 AF.Identity, bias=b2sb[:, dp:dp + 1],
                                 scale=1.0),
                    e.activation(u2[1][:, :],
                                 psum[:, (dp + 1) * 512:(dp + 1) * 512 + 512],
                                 AF.Identity, bias=b2sb[:, dp + 1:dp + 2],
                                 scale=1.0)][-1])(),
                  reads=[bank[dp], bank[dp + 1], buf("consts")],
                  writes=u2b_)
                g("act", lambda e: [
                    e.activation(u2[0][:, :], u2[0][:, :], AF.Identity,
                                 bias=MAGIC, scale=1.0),
                    e.activation(u2[1][:, :], u2[1][:, :], AF.Identity,
                                 bias=MAGIC, scale=1.0),
                    e.activation(u2[0][:, :], u2[0][:, :], AF.Identity,
                                 bias=-MAGIC, scale=1.0),
                    e.activation(u2[1][:, :], u2[1][:, :], AF.Identity,
                                 bias=-MAGIC, scale=1.0)][-1],
                  reads=u2b_, writes=u2b_)
                gy = g("dve", (lambda tb=tb, dp=dp: lambda e: [
                    e.scalar_tensor_tensor(
                        res_fn(dp + i)[:, tb * 512:tb * 512 + 512],
                        u2[i][:, :], 0.5,
                        res_fn(dp + i)[:, tb * 512:tb * 512 + 512],
                        OP.mult, OP.add) for i in range(2)][-1])(),
                    reads=u2b_ + [resbufs[dp], resbufs[dp + 1]],
                    writes=[resbufs[dp], resbufs[dp + 1]])
                if dma_out is not None:
                    dmag("pool", (lambda tb=tb, dp=dp: lambda e: [e.dma_start(
                        dma_out[(dp + i) * 128:(dp + i + 1) * 128,
                                tb * 512:tb * 512 + 512],
                        res_fn(dp + i)[:, tb * 512:tb * 512 + 512])
                        for i in range(2)])(),
                        2, reads=[resbufs[dp], resbufs[dp + 1]], deps=[gy])

    awb = bft(TMPC, 8192)
    bkr = f32(TMPA, D, 0, 1)
    bvr = f32(TMPA + 1024, D, 0, 1)
    ktm_bufs = [buf("ktm%d" % i) for i in range(NTC)]
    kfm_bufs = [buf("kfm%d" % i) for i in range(NCH)]
    vtm_bufs = [buf("vtm%d" % i) for i in range(NTC)]
    qfm_bufs = [buf("qfm%d" % i) for i in range(NCH)]
    Lsb = f32(WSTG, 1024, 0, 64)
    Lbb = bft(WSTG + 1024, 1024, 0, 64)
    lsq = f32(WSTG + 1536, 1024, 0, 64)
    cc2st = f32(WSTG + 2560, NH * 65, 0, 64)
    asb = f32(TMPB, 1056)
    norm16 = f32(WBLK, 2048, 0, 16)
    rnb16 = bft(WBLK + 2048, 2048, 0, 16)

    awk_ap = bft(WBLK, 8192)

    def load_attn_w(dram_h, tagc, wap, extra_w=(), ceng="pool"):
        wbuf = buf("aw" + tagc)
        for kc in range(NCH):
            st = buf("wstg%d" % (kc % 2))
            gl = dmag("sp", (lambda kc=kc: lambda e: [e.dma_start(
                f32(WSTG + (kc % 2) * 2048, 1024),
                dram_h[kc * 128:(kc + 1) * 128, :])])(), 1, writes=[st])
            g(ceng, (lambda kc=kc: lambda e: e.tensor_copy(
                wap[:, kc * 1024:(kc + 1) * 1024],
                f32(WSTG + (kc % 2) * 2048, 1024)))(),
              reads=[st], writes=[wbuf] + list(extra_w), deps=[gl])
        return wbuf

    def proj_tm(dst_off, dstbufs, wbuf, wap, brow_ap, tagp, extra_w=()):
        bias_sb = f32(BFTA, 1024)
        for half in range(2):
            gbb_ = g("pe", (lambda half=half: lambda e: e.matmul(
                psum[:, (4 + half) * 512:(4 + half) * 512 + 512], onesrow,
                brow_ap[0:1, half * 512:half * 512 + 512],
                start=True, stop=True))(),
              reads=[buf("brow"), buf("ones")], writes=[bank[4 + half]])
            g("dve", (lambda half=half: lambda e: e.tensor_copy(
                f32(BFTA + half * 512, 512),
                psum[:, (4 + half) * 512:(4 + half) * 512 + 512]))(),
              reads=[bank[4 + half]], writes=[buf("biasb" + tagp)],
              deps=[gbb_])
        for tm in range(NTC):
            pb = (tm % 2) * 2
            pbufs = [bank[pb], bank[pb + 1]]

            def mm(e, tm=tm, pb=pb):
                last = None
                for kc in range(NCH):
                    for half in range(2):
                        last = e.matmul(
                            psum[:, (pb + half) * 512:(pb + half) * 512 + 512],
                            bft(XHAT + kc * 1024, 2048)[:, tm * 128:tm * 128 + 128],
                            wap[:, kc * 1024 + half * 512:
                                kc * 1024 + half * 512 + 512],
                            start=(kc == 0), stop=(kc == NCH - 1))
                return last
            g("pe", mm, reads=[wbuf] + list(xhat_b), writes=pbufs)
            for half in range(2):
                g("dve", (lambda tm=tm, half=half, pb=pb: lambda e:
                          e.tensor_tensor(
                    bft(dst_off + tm * 512, 1024)[:, half * 512:half * 512 + 512],
                    psum[:, (pb + half) * 512:(pb + half) * 512 + 512],
                    f32(BFTA + half * 512, 512),
                    OP.add))(),
                  reads=[bank[pb + half], buf("biasb" + tagp)],
                  writes=[dstbufs[tm]] + list(extra_w))

    def proj_fm(dst_off, dstbufs, wbuf, wap, bias_sb, tagp, extra_w=()):
        for oc in range(NCH):
            pb = (oc % 2) * 4
            pbufs = [bank[pb + i] for i in range(4)]

            def mm(e, oc=oc, pb=pb):
                last = None
                for kc in range(NCH):
                    for nb in range(NTB):
                        last = e.matmul(
                            psum[:, (pb + nb) * 512:(pb + nb) * 512 + 512],
                            wap[:, kc * 1024 + oc * 128:kc * 1024 + oc * 128 + 128],
                            bft(XHAT + kc * 1024, 2048)[:, nb * 512:nb * 512 + 512],
                            start=(kc == 0), stop=(kc == NCH - 1))
                return last
            g("pe", mm, reads=[wbuf] + list(xhat_b), writes=pbufs)
            g("act", (lambda oc=oc, pb=pb: lambda e: e.activation(
                bft(dst_off + oc * 1024, 2048),
                psum[:, pb * 512:pb * 512 + 2048],
                AF.Identity, bias=bias_sb[:, oc:oc + 1], scale=1.0))(),
              reads=pbufs + [buf("consts")],
              writes=[dstbufs[oc]] + list(extra_w))

    if LIM < 1:
        return _finish(locals())
    sched.phase = "LN1"
    xb_fn = lambda dc: bft(XHAT + dc * 1024, 2048)
    xb1_bufs = [buf("xb1_%d" % dc) for dc in range(NCH)]
    for dc in range(NCH):
        g("dve", (lambda dc=dc: lambda e: e.tensor_copy(
            xb_fn(dc), f32(XRES + dc * T, T)))(),
          reads=[xres_b[dc]], writes=[xb1_bufs[dc]])
    gst1 = ln_stats(xb_fn, xb1_bufs, "L1")
    ln_apply(lambda dc: f32(XRES + dc * T, T), xres_b, xb_fn, xhat_b,
             "L1", gst1)
    sched.barrier()
    if LIM < 2:
        return _finish(locals())
    sched.phase = "FFN1"
    ffn_mm1(xb_fn, xhat_b, "F1", True)
    ffn_mm2(lambda dc: f32(XRES + dc * T, T), xres_b, None, None, x1_d,
            "F1", True)
    wkbuf = load_attn_w(wk_d, "k", awk_ap, extra_w=[buf("wblkF1")])
    sched.barrier()
    if LIM < 4:
        return _finish(locals())
    sched.phase = "LN2"
    xb2_bufs = [buf("xb2_%d" % dc) for dc in range(NCH)]
    for dc in range(NCH):
        g("dve", (lambda dc=dc: lambda e: e.tensor_copy(
            xb_fn(dc), f32(XRES + dc * T, T)))(),
          reads=[xres_b[dc]], writes=[xb2_bufs[dc]])
    gst2 = ln_stats(xb_fn, xb2_bufs, "L2")
    ln_apply(lambda dc: f32(XRES + dc * T, T), xres_b, xb_fn, xhat_b,
             "L2", gst2)
    sched.barrier()

    if LIM < 5:
        return _finish(locals())
    sched.phase = "A1ktm"
    dmag("sp", lambda e: [e.dma_start(bkr, bkrow_d[:, :]),
                          e.dma_start(bvr, bvrow_d[:, :])], 2,
         writes=[buf("brow")])
    wvbuf = load_attn_w(wv_d, "v", awb,
                        extra_w=[buf("yqaccF1"), buf("rowsL2")])
    proj_tm(XRES, ktm_bufs, wkbuf, awk_ap, bkr, "ktm")
    sched.barrier()

    if LIM < 6:
        return _finish(locals())
    sched.phase = "A2Lpart"
    for tc in range(NTC):
        g("pe", (lambda tc=tc: lambda e: [
            e.matmul(psum[0:64, 3072 + half * 512:3072 + half * 512 + 512],
                     Esb[:, tc * NL:(tc + 1) * NL],
                     bft(XRES + tc * 512, 1024)[:, half * 512:half * 512 + 512],
                     start=(tc == 0), stop=(tc == NTC - 1))
            for half in range(2)][-1])(),
          reads=[buf("E"), ktm_bufs[tc]], writes=[bank[6], bank[7]])
    g("dve", lambda e: e.tensor_copy(Lsb, psum[0:64, 3072:4096]),
      reads=[bank[6], bank[7]], writes=[buf("Lpart")])
    dmag("pool", lambda e: [e.dma_start(cc1_in[:, :], Lsb)], 1,
         reads=[buf("Lpart")])

    def cc1(e):
        ins = e.collective_compute(
            "AllReduce", OP.add, replica_groups=RG,
            ins=[cc1_in[:, :].opt()], outs=[cc1_out[:, :].opt()])
        ins.then_inc(ccsem1, 1)
        e.wait_ge(ccsem1, 1)
        return e.engine_nop()
    gcc1 = g("pool", cc1, deps=[sched.by_eng["pool"][-1]])
    sched.barrier()

    if LIM < 7:
        return _finish(locals())
    sched.phase = "A4kfm"
    proj_fm(XRES + 8192, kfm_bufs, wkbuf, awk_ap, bksb, "kfm")
    sched.barrier()

    if LIM < 8:
        return _finish(locals())
    sched.phase = "A3vtm"
    proj_tm(WBLK, vtm_bufs, wvbuf, awb, bvr, "vtm", extra_w=[wkbuf])
    sched.barrier()

    if LIM < 9:
        return _finish(locals())
    sched.phase = "A5L"
    wqbuf = load_attn_w(wq_d, "q", awb, extra_w=[buf("awv")])
    gLl = dmag("pool", lambda e: [e.dma_start(Lsb, cc1_out[:, :])],
               1, writes=[buf("L")], deps=[gcc1])

    for tc in range(NTC):
        so = (tc % 2) * 1024
        gsq = g("dve", (lambda tc=tc, so=so: lambda e: [e.tensor_tensor(
            tmpa[:, so + dc * 128:so + dc * 128 + 128],
            bft(XRES + 8192 + dc * 1024, 2048)[:, tc * 128:tc * 128 + 128],
            bft(XRES + 8192 + dc * 1024, 2048)[:, tc * 128:tc * 128 + 128],
            OP.mult) for dc in range(NCH)][-1])(),
            reads=list(kfm_bufs), writes=[buf("ksq%d" % (tc % 2))])
        pk = 4 + tc % 2
        g("pe", (lambda tc=tc, so=so, pk=pk: lambda e: [e.matmul(
            psum[:, pk * 512 + 2 * dc:pk * 512 + 2 * dc + 2],
            tmpa[:, so + dc * 128:so + dc * 128 + 128], indsb[:, :],
            start=True, stop=True) for dc in range(NCH)][-1])(),
          reads=[buf("ksq%d" % (tc % 2)), buf("ind")], writes=[bank[pk]],
          deps=[gsq])
        g("dve", (lambda tc=tc, pk=pk: lambda e: e.tensor_scalar(
            knh[:, tc * NH:(tc + 1) * NH], psum[:, pk * 512:pk * 512 + NH],
            0.5, None, OP.mult))(),
          reads=[bank[pk]], writes=[buf("knh")])

    def lder(e):
        e.tensor_copy(Lbb[:, :], Lsb)
        e.tensor_tensor(lsq, Lbb[:, :], Lbb[:, :], OP.mult)
        e.tensor_reduce(lncol[:, :], lsq.rearrange("p (h d) -> p h d", d=64),
                        mybir.AxisListType.X, OP.add)
        return e.engine_nop()
    gld = g("dve", lder, reads=[buf("L")], writes=[buf("Lb"), buf("LnC")])
    g("act", lambda e: e.activation(elcol[:, :], lncol[:, :], AF.Exp,
                                    bias=0.0, scale=negr[0:64, :]),
      reads=[buf("LnC"), buf("temps")], writes=[buf("eL")])
    gt1 = g("pe", lambda e: e.transpose(psum[0:16, 1536:1536 + 64],
                                        lncol[:, :], ident[0:64, 0:64]),
            reads=[buf("LnC"), buf("ident")], writes=[bank[3]])
    g("dve", lambda e: e.tensor_copy(lnrows[:, :], psum[0:16, 1536:1536 + 64]),
      reads=[bank[3]], writes=[buf("lnrows")])

    def lnpair_dma(e):
        ops = []
        for h in range(NH):
            p = h // 2
            dst = Lnpair[0:64, p:p + 1] if h % 2 == 0 else Lnpair[64:128, p:p + 1]
            ops.append(e.dma_start(dst, lncol[0:64, h:h + 1],
                                   allow_slow_non_contiguous=True))
        return ops
    dmag("pool", lnpair_dma, NH, reads=[buf("LnC")], writes=[buf("lnpair")])

    for h in range(NH):
        gt = g("pe", (lambda h=h: lambda e: e.transpose(
            psum[0:64, 1024 + (h % 2) * 64:1024 + (h % 2) * 64 + 64],
            Lsb[:, h * 64:(h + 1) * 64], ident[0:64, 0:64]))(),
            reads=[buf("L"), buf("ident")], writes=[bank[2]])
        if h % 2 == 0:
            g("dve", (lambda h=h: lambda e: e.tensor_copy(
                Lfmb[0:64, (h // 2) * 64:(h // 2) * 64 + 64],
                psum[0:64, 1024:1024 + 64]))(),
              reads=[bank[2]], writes=[buf("lfm")], deps=[gt])
        else:
            gcv = g("dve", (lambda h=h: lambda e: e.tensor_copy(
                trsc[:, :], psum[0:64, 1088:1088 + 64]))(),
                reads=[bank[2]], writes=[buf("lfmtmp")], deps=[gt])
            dmag("pool", (lambda h=h: lambda e: [e.dma_start(
                Lfmb[64:128, (h // 2) * 64:(h // 2) * 64 + 64], trsc[:, :])])(),
                1, reads=[buf("lfmtmp")], writes=[buf("lfm")])
    g("dve", lambda e: [
        e.tensor_copy(Lfmze[0:64, :], Lfmb[0:64, :]),
        e.tensor_copy(Lfmzo[64:128, :], Lfmb[64:128, :])][-1],
      reads=[buf("lfm")], writes=[buf("lfmz")])
    sched.barrier()

    if LIM < 10:
        return _finish(locals())
    sched.phase = "A9qfm"
    proj_fm(XRES, qfm_bufs, wqbuf, awb, bqsb, "qfm", extra_w=list(ktm_bufs))
    sched.barrier()
    sched.phase = "A6W"
    for pp in range(0, 8, 2):
        gw1 = g("pe", (lambda pp=pp: lambda e: [
            e.matmul(psum[0 if half == 0 else 64:64 if half == 0 else 128,
                          q * 128:q * 128 + 64],
                     Lfmb[0 if half == 0 else 64:64 if half == 0 else 128,
                          (pp + q) * 64:(pp + q) * 64 + 64],
                     Lfmb[0 if half == 0 else 64:64 if half == 0 else 128,
                          (pp + q) * 64:(pp + q) * 64 + 64],
                     start=True, stop=True)
            for q in range(2) for half in range(2)][-1])(),
            reads=[buf("lfm")], writes=[bank[0]])
        gpr = dmag("pool", (lambda pp=pp: lambda e: [
            e.dma_start(lnpr2[i:i + 1, q * 64:q * 64 + 64],
                        lnrows[2 * (pp + q) + i:2 * (pp + q) + i + 1, :])
            for q in range(2) for i in range(2)])(), 4,
            reads=[buf("lnrows")], writes=[buf("lnpr2")])
        gw2 = g("pe", (lambda pp=pp: lambda e: [
            e.matmul(psum[:, 256 + q * 64:256 + q * 64 + 64], halfind[:, :],
                     lnpr2[:, q * 64:q * 64 + 64], start=True, stop=True)
            for q in range(2)][-1])(),
            reads=[buf("lnpr2"), buf("ones")], writes=[bank[0]],
            deps=[gpr, gw1])

        def wsq(e, pp=pp):
            for q in range(2):
                e.tensor_scalar(tmpa[:, q * 64:q * 64 + 64],
                                psum[:, q * 128:q * 128 + 64], -2.0,
                                Lnpair[:, pp + q:pp + q + 1],
                                OP.mult, OP.add)
            for q in range(2):
                e.tensor_tensor(tmpa[:, q * 64:q * 64 + 64],
                                tmpa[:, q * 64:q * 64 + 64],
                                psum[:, 256 + q * 64:256 + q * 64 + 64],
                                OP.add)
            last = None
            for q in range(2):
                last = e.tensor_scalar(tmpa[:, q * 64:q * 64 + 64],
                                       tmpa[:, q * 64:q * 64 + 64], 0.0,
                                       negr, OP.max, OP.mult)
            return last
        g("dve", wsq, reads=[bank[0], buf("lnpair"), buf("temps")],
          writes=[buf("wsq")])
        gw4 = g("act", (lambda pp=pp: lambda e: [
            e.activation(asb[:, (pp + q) * 132:(pp + q) * 132 + 64],
                         tmpa[:, q * 64:q * 64 + 64], AF.Exp,
                         bias=0.0, scale=1.0) for q in range(2)][-1])(),
            reads=[buf("wsq")],
            writes=[buf("asbW%d" % pp), buf("asbW%d" % (pp + 1))])
        g("dve", (lambda pp=pp: lambda e: [
            e.tensor_tensor(asb[:, (pp + q) * 132:(pp + q) * 132 + 64],
                            asb[:, (pp + q) * 132:(pp + q) * 132 + 64],
                            iepsdual[:, :], OP.add) for q in range(2)][-1])(),
          reads=[buf("asbW%d" % pp), buf("asbW%d" % (pp + 1)), buf("idual")],
          writes=[buf("asbW%d" % pp), buf("asbW%d" % (pp + 1))])

    if LIM < 11:
        return _finish(locals())
    sched.phase = "A7phiK"
    wobuf = load_attn_w(wo_d, "o", awb, extra_w=[buf("awq")])
    for tc in range(NTC):
        pc = (tc % 2) * 2
        cb = [bank[pc], bank[pc + 1]]

        def cmm(e, tc=tc, pc=pc):
            last = None
            for dc in range(NCH):
                last = e.matmul(
                    psum[0:128, pc * 512 + 2 * dc * 64:
                         pc * 512 + 2 * dc * 64 + 64],
                    bft(XRES + 8192 + dc * 1024, 2048)
                    [:, tc * 128:tc * 128 + 128],
                    Lfmze[:, dc * 64:dc * 64 + 64],
                    start=True, stop=True)
                last = e.matmul(
                    psum[0:128, pc * 512 + (2 * dc + 1) * 64:
                         pc * 512 + (2 * dc + 1) * 64 + 64],
                    bft(XRES + 8192 + dc * 1024, 2048)
                    [:, tc * 128:tc * 128 + 128],
                    Lfmzo[:, dc * 64:dc * 64 + 64],
                    start=True, stop=True)
            return last
        g("pe", cmm, reads=[buf("lfmz")] + kfm_bufs, writes=cb)
        if SUB < 1:
            continue
        gsb = g("dve", (lambda tc=tc, pc=pc: lambda e: e.tensor_tensor(
            tmpa[:, 0:1024].rearrange("p (h d) -> p h d", d=64),
            psum[:, pc * 512:pc * 512 + 1024].rearrange(
                "p (h d) -> p h d", d=64),
            knh[:, tc * NH:(tc + 1) * NH].rearrange(
                "p (h o) -> p h o", o=1).broadcast_to([128, NH, 64]),
            OP.subtract))(),
            reads=cb + [buf("knh")], writes=[buf("ck")])
        gex = g("act", (lambda tc=tc: lambda e: e.activation(
            bftc[:, 0:1024], tmpa[:, 0:1024], AF.Exp, bias=0.0,
            scale=rcol2))(),
            reads=[buf("ck"), buf("temps")], writes=[buf("phik")])
        if SUB < 2:
            continue
        for h in range(NH):
            g("pe", (lambda tc=tc, h=h: lambda e: [
                e.matmul(psum[0:64, 3072 + h * 64:3072 + h * 64 + 64],
                         bftc[:, h * 64:h * 64 + 64],
                         bft(WBLK + tc * 512, 1024)[:, h * 64:h * 64 + 64],
                         start=(tc == 0 and h % 8 == 0),
                         stop=(tc == NTC - 1 and h % 8 == 7)),
                e.matmul(psum[0:64, 2560 + h:2560 + h + 1],
                         bftc[:, h * 64:h * 64 + 64], onesb[:, 0:1],
                         start=(tc == 0 and h == 0),
                         stop=(tc == NTC - 1 and h == NH - 1))][-1])(),
              reads=[buf("phik"), vtm_bufs[tc], buf("ones")],
              writes=[bank[6], bank[7], bank[5]])

    if SUB < 3:
        return _finish(locals())

    def zstage(e):
        e.tensor_tensor(cc2st[:, 0:1024].rearrange("p (h d) -> p h d", d=64),
                        psum[0:64, 3072:4096].rearrange(
                            "p (h d) -> p h d", d=64),
                        elcol[:, :].rearrange("p (h o) -> p h o", o=1)
                        .broadcast_to([64, NH, 64]), OP.mult)
        return e.tensor_tensor(cc2st[:, 1024:1040], psum[0:64, 2560:2576],
                               elcol[:, :], OP.mult)
    g("dve", zstage, reads=[bank[5], bank[6], bank[7], buf("eL")],
      writes=[buf("cc2st")])
    dmag("pool", lambda e: [e.dma_start(cc2_in[:, :], cc2st)], 1,
         reads=[buf("cc2st")])

    if SUB < 4:
        return _finish(locals())

    def cc2(e):
        ins = e.collective_compute(
            "AllReduce", OP.add, replica_groups=RG,
            ins=[cc2_in[:, :].opt()], outs=[cc2_out[:, :].opt()])
        ins.then_inc(ccsem2, 1)
        e.wait_ge(ccsem2, 1)
        return e.engine_nop()
    gcc2 = g("pool", cc2, deps=[sched.by_eng["pool"][-1]])

    def zload(e):
        ops = []
        for h in range(NH):
            p0 = (h % 2) * 64
            ops.append(e.dma_start(
                asb[p0:p0 + 64, (h // 2) * 132 + 64:(h // 2) * 132 + 128],
                cc2_out[0:64, h * 64:(h + 1) * 64]))
            ops.append(e.dma_start(
                asb[p0:p0 + 64, (h // 2) * 132 + 128:(h // 2) * 132 + 129],
                cc2_out[0:64, 1024 + h:1024 + h + 1],
                allow_slow_non_contiguous=True))
        return ops
    dmag("pool", zload, 32, writes=[buf("asbZ")], deps=[gcc2])
    sched.barrier()

    if LIM < 12:
        return _finish(locals())
    sched.phase = "A8solve"
    asb_buf = buf("asball")
    sel = f32(TMPB + 1920, 128)
    for k in range(NL):
        extra = ([buf("asbZ")] + [buf("asbW%d" % p) for p in range(8)]
                 if k == 0 else [asb_buf])
        gsel = g("dve", (lambda k=k: lambda e: [
            e.tensor_scalar(sel[:, 0:64], iota_c.broadcast_to([128, 64]),
                            float(k), None, OP.is_equal),
            e.tensor_scalar(sel[:, 64:128], iota_c.broadcast_to([128, 64]),
                            float(64 + k), None, OP.is_equal)][-1])(),
            reads=[buf("iota")], writes=[buf("sel")])
        gbc = g("pe", (lambda k=k: lambda e: [
            e.matmul(psum[:, 0:512], sel, asb[:, 0:512],
                     start=True, stop=True),
            e.matmul(psum[:, 512:1024], sel, asb[:, 512:1024],
                     start=True, stop=True),
            e.matmul(psum[:, 1024:1056], sel, asb[:, 1024:1056],
                     start=True, stop=True)][-1])(),
            reads=extra + [buf("sel")], writes=[bank[0], bank[1], bank[2]],
            deps=[gsel])

        def solstep(e, k=k):
            for q in range(2):
                e.reciprocal(sm[:, 32 + q * 4:36 + q * 4],
                             psum[:, k:1056:132][:, q * 4:q * 4 + 4])
            for q in range(2):
                e.tensor_tensor(sm[:, 40 + q * 4:44 + q * 4],
                                identdual[:, k:k + 1].broadcast_to([128, 4]),
                                asb[:, k::132][:, q * 4:q * 4 + 4],
                                OP.subtract)
            for q in range(2):
                e.tensor_tensor(sm[:, 48 + q * 4:52 + q * 4],
                                sm[:, 40 + q * 4:44 + q * 4],
                                sm[:, 32 + q * 4:36 + q * 4], OP.mult)
            for q in range(2):
                e.tensor_tensor(
                    f32(TMPA + q * 528, 528).rearrange(
                        "p (b n) -> p b n", n=132),
                    psum[:, q * 528:q * 528 + 528].rearrange(
                        "p (b n) -> p b n", n=132),
                    sm[:, 48 + q * 4:52 + q * 4].rearrange(
                        "p (b o) -> p b o", o=1)
                    .broadcast_to([128, 4, 132]), OP.mult)
            last = None
            for q in range(2):
                last = e.tensor_tensor(asb[:, q * 528:q * 528 + 528],
                                       asb[:, q * 528:q * 528 + 528],
                                       f32(TMPA + q * 528, 528), OP.add)
            return last
        g("dve", solstep, reads=[bank[0], bank[1], bank[2], buf("idual")],
          writes=[asb_buf])
        if k % 8 == 7:
            p = k // 8
            cqb = [bank[3], bank[4], bank[5], bank[6]]

            def cqmm_e(e, p=p):
                last = None
                for nb in range(NTB):
                    last = e.matmul(
                        psum[0:64, 1536 + nb * 512:1536 + nb * 512 + 512],
                        Lfmb[0:64, p * 64:p * 64 + 64],
                        bft(XRES + p * 1024, 2048)
                        [0:64, nb * 512:nb * 512 + 512],
                        start=True, stop=True)
                    last = e.matmul(
                        psum[64:128, 1536 + nb * 512:1536 + nb * 512 + 512],
                        Lfmb[64:128, p * 64:p * 64 + 64],
                        bft(XRES + p * 1024, 2048)
                        [64:128, nb * 512:nb * 512 + 512],
                        start=True, stop=True)
                return last
            g("pe", cqmm_e, reads=[buf("lfm"), qfm_bufs[p]], writes=cqb)
            for ch in range(4):
                gts = g("dve", (lambda p=p, ch=ch: lambda e: e.tensor_scalar(
                    f32(BFTC + (ch % 2) * 512, 512),
                    psum[:, 1536 + ch * 512:1536 + (ch + 1) * 512], 2.0,
                    Lnpair[:, p:p + 1], OP.mult, OP.subtract))(),
                    reads=[cqb[ch], buf("lnpair")],
                    writes=[buf("cqch%d" % (ch % 2))])
                g("act", (lambda p=p, ch=ch: lambda e: e.activation(
                    bft(XRES + 8192 + p * 1024 + ch * 256, 512),
                    f32(BFTC + (ch % 2) * 512, 512), AF.Exp,
                    bias=0.0, scale=rcol))(),
                  reads=[buf("cqch%d" % (ch % 2)), buf("temps")],
                  writes=[buf("phiq%d" % p)], deps=[gts])
    def ycopy(e):
        e.memset(sm[:, 9:10], 0.0)
        return e.tensor_copy(
            ybsb[:, :].rearrange("p (b n) -> p b n", n=65),
            asb[:, :].rearrange("p (b n) -> p b n", n=132)[:, :, 64:129])
    g("dve", ycopy, reads=[asb_buf], writes=[buf("yb")])
    sched.barrier()

    if LIM < 13:
        return _finish(locals())
    if LIM < 14:
        return _finish(locals())
    sched.phase = "A10phiQ"
    for p in range(8):
        phiqp = bft(XRES + 8192 + p * 1024, 2048)
        for nb in range(NTB):
            g("pe", (lambda p=p, nb=nb, phiqp=phiqp: lambda e: [
                e.matmul(psum[0:64, (4 + nb) * 512:(4 + nb) * 512 + 512],
                         ybsb[0:64, p * 65:p * 65 + 64],
                         phiqp[0:64, nb * 512:nb * 512 + 512],
                         start=True, stop=True),
                e.matmul(psum[64:128, (4 + nb) * 512:(4 + nb) * 512 + 512],
                         ybsb[64:128, p * 65:p * 65 + 64],
                         phiqp[64:128, nb * 512:nb * 512 + 512],
                         start=True, stop=True),
                e.matmul(psum[0:1, nb * 512:nb * 512 + 512],
                         ybsb[0:64, p * 65 + 64:p * 65 + 65],
                         phiqp[0:64, nb * 512:nb * 512 + 512],
                         start=True, stop=True),
                e.matmul(psum[64:65, nb * 512:nb * 512 + 512],
                         ybsb[64:128, p * 65 + 64:p * 65 + 65],
                         phiqp[64:128, nb * 512:nb * 512 + 512],
                         start=True, stop=True)][-1])(),
              reads=[buf("yb"), buf("phiq%d" % p)],
              writes=[bank[4 + nb], bank[nb]])
        g("dve", (lambda p=p: lambda e: e.tensor_copy(
            bft(XRES + 8192 + p * 1024, 2048), psum[:, 2048:4096]))(),
          reads=[bank[4], bank[5], bank[6], bank[7]],
          writes=[buf("attnb%d" % p), buf("phiq%d" % p)])
        gnv = g("dve", (lambda p=p: lambda e: [
            e.tensor_copy(AR[0:1, TMPB:TMPB + 2048], psum[0:1, 0:2048]),
            e.tensor_copy(AR[64:65, TMPB:TMPB + 2048], psum[64:65, 0:2048])][-1])(),
            reads=[bank[0], bank[1], bank[2], bank[3]], writes=[buf("nv")])
        dmag("pool", (lambda p=p: lambda e: [
            e.dma_start(norm16[2 * p:2 * p + 1, :], AR[0:1, TMPB:TMPB + 2048]),
            e.dma_start(norm16[2 * p + 1:2 * p + 2, :],
                        AR[64:65, TMPB:TMPB + 2048])])(), 2,
            reads=[buf("nv")], writes=[buf("n16_%d" % p)])

    def rncomp(e):
        e.tensor_scalar(norm16[:, :], norm16[:, :], 1e-10, None, OP.max)
        e.reciprocal(norm16[:, :], norm16[:, :])
        return e.tensor_copy(rnb16[:, :], norm16[:, :])
    g("dve", rncomp, reads=[buf("n16_%d" % p) for p in range(8)],
      writes=[buf("rn")])
    for p in range(8):
        dmag("pool", (lambda p=p: lambda e: [
            e.dma_start(ARB[0:1, 2 * TMPB:2 * TMPB + 2048],
                        rnb16[2 * p:2 * p + 1, :]),
            e.dma_start(ARB[64:65, 2 * TMPB:2 * TMPB + 2048],
                        rnb16[2 * p + 1:2 * p + 2, :])])(), 2,
            reads=[buf("rn")], writes=[buf("rnp0")])
        for nb in range(NTB):
            gbb = g("pe", (lambda nb=nb: lambda e: [
                e.matmul(psum[0:64, nb * 512:nb * 512 + 512],
                         onesrb[0:1, 0:64],
                         ARB[0:1, 2 * TMPB + nb * 512:2 * TMPB + nb * 512 + 512],
                         start=True, stop=True),
                e.matmul(psum[64:128, nb * 512:nb * 512 + 512],
                         ones2[64:65, :],
                         ARB[64:65, 2 * TMPB + nb * 512:
                             2 * TMPB + nb * 512 + 512],
                         start=True, stop=True)][-1])(),
                reads=[buf("rnp0"), buf("ones")], writes=[bank[nb]])
            gcp = g("dve", (lambda nb=nb: lambda e: e.tensor_copy(
                bftc[:, nb * 512:nb * 512 + 512],
                psum[:, nb * 512:nb * 512 + 512]))(),
                reads=[bank[nb]], writes=[buf("rnbc")], deps=[gbb])
            g("dve", (lambda p=p, nb=nb: lambda e: e.tensor_tensor(
                bft(XRES + 8192 + p * 1024, 2048)[:, nb * 512:nb * 512 + 512],
                bft(XRES + 8192 + p * 1024, 2048)[:, nb * 512:nb * 512 + 512],
                bftc[:, nb * 512:nb * 512 + 512], OP.mult))(),
              reads=[buf("attnb%d" % p), buf("rnbc")],
              writes=[buf("attnb%d" % p)], deps=[gcp])
    sched.barrier()

    if LIM < 15:
        return _finish(locals())
    sched.phase = "A11wo"
    x2_fn = lambda dc: (f32(XHAT + dc * 2048, 2048) if dc < 4 else
                        f32(XRES + (dc - 4) * 2048, 2048))
    x2_bufs = [buf("x2_%d" % dc) for dc in range(NCH)]
    x2b_fn = lambda dc: bft(WBLK + dc * 1024, 2048)
    x2b_bufs = [buf("x2b_%d" % dc) for dc in range(NCH)]
    for oc in range(NCH):
        pb = (oc % 2) * 4
        pbufs = [bank[pb + i] for i in range(4)]

        def mmo(e, oc=oc, pb=pb):
            last = None
            for kc in range(NCH):
                for nb in range(NTB):
                    last = e.matmul(
                        psum[:, (pb + nb) * 512:(pb + nb) * 512 + 512],
                        awb[:, kc * 1024 + oc * 128:kc * 1024 + oc * 128 + 128],
                        bft(XRES + 8192 + kc * 1024, 2048)
                        [:, nb * 512:nb * 512 + 512],
                        start=(kc == 0), stop=(kc == NCH - 1))
            return last
        g("pe", mmo, reads=[wobuf] + [buf("attnb%d" % p) for p in range(8)],
          writes=pbufs)
        gx1 = dmag("sp", (lambda oc=oc: lambda e: [e.dma_start(
            tmpb[:, :], x1_d[oc * 128:(oc + 1) * 128, :])])(), 1,
            writes=[buf("x1t")])
        g("act", (lambda oc=oc, pb=pb: lambda e: e.activation(
            tmpa[:, :], psum[:, pb * 512:pb * 512 + 2048],
            AF.Identity, bias=bosb[:, oc:oc + 1], scale=1.0))(),
          reads=pbufs + [buf("consts")], writes=[buf("at")])
        g("dve", (lambda oc=oc: lambda e: e.tensor_tensor(
            x2_fn(oc), tmpa[:, :], tmpb[:, :], OP.add))(),
          reads=[buf("at"), buf("x1t")], writes=[x2_bufs[oc]])
        g("dve", (lambda oc=oc: lambda e: e.tensor_copy(
            x2b_fn(oc), x2_fn(oc)))(),
          reads=[x2_bufs[oc]], writes=[x2b_bufs[oc]])
    sched.barrier()

    if LIM < 16:
        return _finish(locals())
    sched.phase = "LN3"
    gst3 = ln_stats(x2b_fn, x2b_bufs, "L3")
    xhat3_fn = lambda dc: bft(XRES + 8192 + dc * 1024, 2048)
    xhat3_b = [buf("xh3_%d" % dc) for dc in range(NCH)]
    ln_apply(x2_fn, x2_bufs, xhat3_fn, xhat3_b, "L3", gst3)
    sched.barrier()
    sched.phase = "FFN3"
    ffn_mm1(xhat3_fn, xhat3_b, "F3", False)
    ffn_mm2(x2_fn, x2_bufs, None, None, out_d, "F3", False)
    sched.barrier()

    return _finish(locals())


def _finish(L):
    nc = L["nc"]; sched = L["sched"]; sems = L["sems"]; dslot = L["dslot"]
    ctx = L["ctx"]; ENGINES_ = ENGINES
    if L["LIM"] < 16:
        f32 = L["f32"]; XRES = L["XRES"]; out_d = L["out_d"]
        dmagf = L["dmag"]
        dmagf("pool", lambda e: [e.dma_start(
            out_d[c * 128:(c + 1) * 128, :], f32(XRES + c * T, T))
            for c in range(NCH)], NCH)
        dbg_d = L["dbg_d"]
        import os as _os
        _skip_dbg = _os.environ.get("KDBG", "1") == "0"
        dumps = []
        col = [0]
        def dump(ap, w):
            c0 = col[0]
            dumps.append((ap, c0, w))
            col[0] += w
        dump(L["Lfmb"][:, :].bitcast(F32), 256)
        dump(L["Lnpair"][:, :], 8)
        dump(L["knh"][:, :], 256)
        dump(L["asb"], 1056)
        dump(L["ybsb"][:, :].bitcast(F32), 260)
        dump(L["elcol"][:, :], 16)
        dump(L["lncol"][:, :], 16)
        dump(L["AR"][:, L["WSTG"]:L["WSTG"] + 1024], 1024)
        dump(L["AR"][:, L["WBLK"]:L["WBLK"] + 2048], 2048)
        dump(L["Esb"][:, :].bitcast(F32), 512)
        dump(L["sm"][:, :], 256)
        def dofl(e):
            return [e.dma_start(dbg_d[0:a.shape[0], c0:c0 + w], a)
                    for (a, c0, w) in dumps]
        if not _skip_dbg:
            dmagf("pool", dofl, len(dumps))
    for grp in sched.groups:
        for d in grp.deps:
            if d.dma or d.eng != grp.eng:
                d.need_inc = True
    NSLOT_ = 10
    for e in ENGINES:
        cum = 0
        slot_tot = [0] * NSLOT_
        didx = 0
        for grp in sched.by_eng[e]:
            if grp.dma:
                s = didx % NSLOT_
                grp.slot = s
                grp.slot_prev = slot_tot[s]
                slot_tot[s] += 16 * grp.ndma
                grp.cum = slot_tot[s]
                didx += 1
            else:
                if grp.need_inc:
                    cum += 1
                grp.cum = cum

    with nc.Block() as block:
        def mkrun(eng_name):
            def body(eng):
                waited = {}
                for grp in sched.by_eng[eng_name]:
                    for d in sorted(grp.deps, key=lambda gg: gg.gid):
                        if d.dma:
                            sem, tgt = dslot[d.eng][d.slot], d.cum
                        elif d.eng != eng_name:
                            sem, tgt = sems[d.eng], d.cum
                        else:
                            continue
                        key = id(sem)
                        if waited.get(key, -1) < tgt:
                            eng.wait_ge(sem, tgt)
                            waited[key] = tgt
                    if grp.dma:
                        ssem = dslot[eng_name][grp.slot]
                        if grp.slot_prev > 0 and waited.get(id(ssem), -1) < grp.slot_prev:
                            eng.wait_ge(ssem, grp.slot_prev)
                            waited[id(ssem)] = grp.slot_prev
                        with nc.named_scope(grp.phase):
                            for ins in grp.emit(eng):
                                ins.then_inc(ssem, 16)
                    else:
                        with nc.named_scope(grp.phase):
                            last = grp.emit(eng)
                        if grp.need_inc:
                            last.then_inc(sems[eng_name], 1)
                if eng_name == "pool":
                    for i, s in enumerate(dslot["pool"]):
                        tot = 0
                        for x in sched.by_eng["pool"]:
                            if x.dma and x.slot == i:
                                tot = x.cum
                        if tot > 0:
                            eng.wait_ge(s, tot)
            return body
        block.tensor(mkrun("pe"))
        block.vector(mkrun("dve"))
        block.scalar(mkrun("act"))
        block.gpsimd(mkrun("pool"))
        block.sync(mkrun("sp"))
    ctx.close()
    from concourse.library_overlay import lower_extended_insts
    lower_extended_insts(nc)
    return nc


_NC_CACHE = None
_LAST_RES = None


def kernel(**inputs):
    global _NC_CACHE
    x = np.ascontiguousarray(np.asarray(inputs["L0_attn_query"], np.float32))
    Wq = np.asarray(inputs["L0_attn_Wq"], np.float32)
    Wk = np.asarray(inputs["L0_attn_Wk"], np.float32)
    Wv = np.asarray(inputs["L0_attn_Wv"], np.float32)
    Wo = np.asarray(inputs["L0_attn_Wo"], np.float32)
    bq = np.asarray(inputs["L0_attn_bq"], np.float32)
    bk = np.asarray(inputs["L0_attn_bk"], np.float32)
    bv = np.asarray(inputs["L0_attn_bv"], np.float32)
    bo = np.asarray(inputs["L0_attn_bo"], np.float32)
    temp = np.asarray(inputs["L0_attn_temperature"], np.float32).reshape(1, 1)
    lidx = np.asarray(inputs["L0_attn_landmark_idx"], np.int32).reshape(1, NL)
    w1 = np.asarray(inputs["L0_ffn_w1"], np.float32)
    b1 = np.asarray(inputs["L0_ffn_b1"], np.float32)
    w2 = np.asarray(inputs["L0_ffn_w2"], np.float32)
    b2 = np.asarray(inputs["L0_ffn_b2"], np.float32)

    def colmajor(b):
        return np.ascontiguousarray(b.reshape(-1, 128).T)

    base = {
        "w1": np.ascontiguousarray(w1),
        "w2": np.ascontiguousarray(w2),
        "wqt": np.ascontiguousarray(Wq.T),
        "wkt": np.ascontiguousarray(Wk.T),
        "wvt": np.ascontiguousarray(Wv.T),
        "wot": np.ascontiguousarray(Wo.T),
        "b1": colmajor(b1),
        "b2": colmajor(b2),
        "bq": colmajor(bq),
        "bk": colmajor(bk),
        "bv": colmajor(bv),
        "bo": colmajor(bo),
        "bkrow": np.ascontiguousarray(bk.reshape(1, -1)),
        "bvrow": np.ascontiguousarray(bv.reshape(1, -1)),
        "lidx": lidx,
        "temp": temp,
        "hind": np.float32([[1.0] * 64 + [0.0] * 64,
                            [0.0] * 64 + [1.0] * 64]),
    }
    if _NC_CACHE is None:
        _NC_CACHE = build_nc()
    nc = _NC_CACHE
    in_maps = []
    for c in range(8):
        b, h = c // 2, c % 2
        m = dict(base)
        m["x"] = np.ascontiguousarray(x[b, h * T:(h + 1) * T, :].T)
        m["coff"] = np.float32([[h * T]])
        in_maps.append(m)
    res = run_bass_kernel_spmd(nc, in_maps, core_ids=list(range(8)))
    global _LAST_RES
    _LAST_RES = res
    out = np.empty((4, 2 * T, D), np.float32)
    for c in range(8):
        b, h = c // 2, c % 2
        out[b, h * T:(h + 1) * T, :] = np.asarray(res.results[c]["out"]).T
    return out


# version 14
# speedup vs baseline: 1.0246x; 1.0246x over previous
import math
import numpy as np

import concourse.bass as bass
import concourse.mybir as mybir
from concourse.bass_utils import run_bass_kernel_spmd

F32 = mybir.dt.float32
BF16 = mybir.dt.bfloat16
I32 = mybir.dt.int32
AF = mybir.ActivationFunctionType
OP = mybir.AluOpType

D, FF, T = 1024, 4096, 2048
NCH, NF, NTC, NTB = 8, 32, 16, 4
NH, DH, NL = 16, 64, 64
MAGIC = float(np.float32(3 * 2 ** 14))
EPS_LN = 1e-5
REG = 1e-6
ENGINES = ("pe", "dve", "act", "pool", "sp")


def _build_pwl():
    X = None
    try:
        import jax
        import jax.numpy as jnp
        with jax.default_device(jax.devices("cpu")[0]):
            Xj = jnp.linspace(-4.0, 4.0, 33, dtype=jnp.float32)
            Yj = 0.5 * Xj * (1.0 + jax.lax.erf(Xj / jnp.sqrt(jnp.float32(2.0))))
            X = np.asarray(Xj, np.float32)
            Y = np.asarray(Yj, np.float32)
    except Exception:
        X = None
    if X is None:
        X = np.linspace(-4.0, 4.0, 33).astype(np.float32)
        Y = np.float32([0.5 * float(v) * (1.0 + math.erf(float(v) / math.sqrt(2.0)))
                        for v in X])
    X64, Y64 = X.astype(np.float64), Y.astype(np.float64)
    s = (Y64[1:] - Y64[:-1]) / (X64[1:] - X64[:-1])
    cp = np.empty(33, np.float64)
    cp[0] = s[0]
    cp[1:32] = s[1:] - s[:-1]
    cp[32] = 1.0 - s[31]
    XK = np.concatenate([X64[:32], [4.0]])
    kconst = float(np.float32(Y64[0] - np.sum(cp * XK)))
    terms = [(float(np.float32(cp[j])), float(np.float32(cp[j] * XK[j])),
              bool(cp[j] > 0)) for j in range(33)]
    knots = [float(np.float32(XK[j])) for j in range(33)]
    return terms, kconst, knots


PWL_TERMS, PWL_K, PWL_KNOTS = _build_pwl()

M4 = 196608.0


def _ref_t2i(in0, in1, s0, s1, imm2):
    return np.maximum(in0, s0) * s1 + np.maximum(in0, s0 + 1.0) * imm2


def _ref_t2a(in0, in1, s0, s1, imm2):
    return (in1 + np.maximum(in0, s0) * s1) + np.maximum(in0, s0 + 1.0) * imm2


def _ref_t1ak(in0, in1, s0, s1, imm2):
    return (in1 + np.maximum(in0, s0) * s1) + imm2


def _install_pwl_ops():
    from concourse import dve_ops as _dvo
    from concourse.dve_op import DveOp
    from concourse.dve_spec import Spec, Src0, Src1, C0, C1, C2, One, maxx
    from concourse.dve_spec import lower as _lower
    if any(op.name == "PWL_T2A" for op in _dvo.OPS):
        return {op.name: op for op in _dvo.OPS
                if op.name in ("PWL_T2I", "PWL_T2A", "PWL_T1AK")}
    defs = [
        ("PWL_T2I",
         Spec(body=maxx(Src0, C0) * C1 + maxx(Src0, C0 + One) * C2),
         _ref_t2i, "append"),
        ("PWL_T2A",
         Spec(body=(Src1 + maxx(Src0, C0) * C1)
              + maxx(Src0, C0 + One) * C2),
         _ref_t2a, "append"),
        ("PWL_T1AK",
         Spec(body=(Src1 + maxx(Src0, C0) * C1) + C2),
         _ref_t1ak, 0),
    ]
    out = {}
    for name, spec, ref, slot in defs:
        sha = {"v3": _lower(spec, ver="v3", perf_en=False, name=name).sha("v3")}
        op = DveOp(name, spec, subdim=False, uops_sha=sha, reference=ref)
        if slot == "append":
            _dvo.OPS.append(op)
            row = _dvo._CUSTOM_DVE_ROW_BASE + len(_dvo.OPS) - 1
        else:
            old = _dvo.OPS[slot]
            _dvo.OPS[slot] = op
            _dvo.CUSTOM_DVE_SPECS.pop(old.name, None)
            row = _dvo._CUSTOM_DVE_ROW_BASE + slot
        assert row < 0x20
        _dvo.CUSTOM_DVE_SPECS[name] = spec
        _dvo.CUSTOM_DVE_REFS[name] = ref
        _dvo._SUB_OPCODE_FOR_NAME[name] = row
        out[name] = op
    return out


_PWL_OPS = _install_pwl_ops()

_PWL4 = [(float(np.float32(4.0) * np.float32(PWL_KNOTS[_j])),
          float(np.float32(PWL_TERMS[_j][0]) / 4.0)) for _j in range(33)]
for _j in range(32):
    assert _PWL4[_j + 1][0] == _PWL4[_j][0] + 1.0, (_j, _PWL4[_j], _PWL4[_j + 1])


class Buf:
    __slots__ = ("w", "r", "name")

    def __init__(self, name=""):
        self.w, self.r, self.name = None, [], name


class Group:
    __slots__ = ("eng", "emit", "deps", "dma", "need_inc", "cum", "gid", "ndma",
                 "slot", "slot_prev", "phase")

    def __init__(self, eng, emit, deps, dma, ndma):
        self.eng, self.emit, self.deps = eng, emit, deps
        self.dma, self.ndma = dma, ndma
        self.need_inc, self.cum, self.gid = False, 0, -1


class Sched:
    def __init__(self):
        self.groups = []
        self.by_eng = {e: [] for e in ENGINES}
        self._bar = {}
        self.phase = "init"

    def add(self, eng, emit, reads=(), writes=(), deps=(), dma=False, ndma=0):
        ds = set()
        for b in reads:
            if b.w is not None:
                ds.add(b.w)
        for b in writes:
            if b.w is not None:
                ds.add(b.w)
            ds.update(b.r)
        for gg in deps:
            if gg is not None:
                ds.add(gg)
        ds.update(self._bar.values())
        grp = Group(eng, emit, ds, dma, ndma)
        grp.phase = self.phase
        grp.gid = len(self.groups)
        self.groups.append(grp)
        self.by_eng[eng].append(grp)
        for b in reads:
            b.r.append(grp)
        for b in writes:
            b.w, b.r = grp, []
        return grp

    def barrier(self):
        self._bar = {e: self.by_eng[e][-1] for e in ENGINES if self.by_eng[e]}


def build_nc():
    import os
    LIM = int(os.environ.get("KLIM", "99"))
    SUB = int(os.environ.get("KSUB", "9"))
    nc = bass.Bass("TRN2", target_bir_lowering=False,
                   detect_race_conditions=False)
    sched = Sched()
    B = {}

    def buf(name):
        if name not in B:
            B[name] = Buf(name)
        return B[name]

    x_in = nc.dram_tensor("x", [D, T], F32, kind="ExternalInput")
    w1_d = nc.dram_tensor("w1", [D, FF], F32, kind="ExternalInput")
    w2_d = nc.dram_tensor("w2", [FF, D], F32, kind="ExternalInput")
    wq_d = nc.dram_tensor("wqt", [D, D], F32, kind="ExternalInput")
    wk_d = nc.dram_tensor("wkt", [D, D], F32, kind="ExternalInput")
    wv_d = nc.dram_tensor("wvt", [D, D], F32, kind="ExternalInput")
    wo_d = nc.dram_tensor("wot", [D, D], F32, kind="ExternalInput")
    b1_d = nc.dram_tensor("b1", [128, NF], F32, kind="ExternalInput")
    b2_d = nc.dram_tensor("b2", [128, NCH], F32, kind="ExternalInput")
    bq_d = nc.dram_tensor("bq", [128, NCH], F32, kind="ExternalInput")
    bk_d = nc.dram_tensor("bk", [128, NCH], F32, kind="ExternalInput")
    bv_d = nc.dram_tensor("bv", [128, NCH], F32, kind="ExternalInput")
    bo_d = nc.dram_tensor("bo", [128, NCH], F32, kind="ExternalInput")
    bkrow_d = nc.dram_tensor("bkrow", [1, D], F32, kind="ExternalInput")
    bvrow_d = nc.dram_tensor("bvrow", [1, D], F32, kind="ExternalInput")
    lidx_d = nc.dram_tensor("lidx", [1, NL], I32, kind="ExternalInput")
    hind_d = nc.dram_tensor("hind", [2, 128], F32, kind="ExternalInput")
    temp_d = nc.dram_tensor("temp", [1, 1], F32, kind="ExternalInput")
    coff_d = nc.dram_tensor("coff", [1, 1], F32, kind="ExternalInput")
    out_d = nc.dram_tensor("out", [D, T], F32, kind="ExternalOutput")
    dbg_d = nc.dram_tensor("dbg", [128, 8192], F32, kind="ExternalOutput")

    x1_d = nc.dram_tensor("x1scr", [D, T], F32)
    yq_d = nc.dram_tensor("yqscr", [FF, T], BF16)
    w1b_d = nc.dram_tensor("w1bscr", [128, NCH * FF], BF16)
    w2b_d = nc.dram_tensor("w2bscr", [128, NCH * FF], BF16)
    cc1_in = nc.dram_tensor("cc1in", [NL, D], F32)
    cc1_out = nc.dram_tensor("cc1out", [NL, D], F32)
    cc2_in = nc.dram_tensor("cc2in", [NL, NH * 65], F32)
    cc2_out = nc.dram_tensor("cc2out", [NL, NH * 65], F32)
    RG = [[0, 1], [2, 3], [4, 5], [6, 7]]

    import contextlib
    ctx = contextlib.ExitStack()

    def sbt(name, shape, dt):
        return ctx.enter_context(nc.sbuf_tensor(name, shape, dt))

    AR = sbt("AR", [128, 49152], F32)
    ARB = AR[:, :].bitcast(BF16)

    b1sb = sbt("b1sb", [128, NF], F32)
    b1q4 = sbt("b1q4", [128, NF], F32)
    b2sb = sbt("b2sb", [128, NCH], F32)
    bqsb = sbt("bqsb", [128, NCH], F32)
    bksb = sbt("bksb", [128, NCH], F32)
    bosb = sbt("bosb", [128, NCH], F32)
    sm = sbt("sm", [128, 256], F32)
    ident = sbt("ident", [128, 128], F32)
    identb = sbt("identb", [128, 128], BF16)
    identdual = sbt("identdual", [128, 64], F32)
    iepsdual = sbt("iepsdual", [128, 64], F32)
    halfind = sbt("halfind", [2, 128], F32)
    onesb = sbt("onesb", [128, 4], BF16)
    onesrb = sbt("onesrb", [1, 128], BF16)
    Esb = sbt("Esb", [128, NTC * NL], BF16)
    idxsb = sbt("idxsb", [1, NL], I32)
    Lfmb = sbt("Lfmb", [128, 8 * DH], BF16)
    Lfmze = sbt("Lfmze", [128, 8 * DH], BF16)
    Lfmzo = sbt("Lfmzo", [128, 8 * DH], BF16)
    ones2 = sbt("ones2", [128, 64], BF16)
    Lnpair = sbt("Lnpair", [128, 8], F32)
    elcol = sbt("elcol", [64, NH], F32)
    lncol = sbt("lncol", [64, NH], F32)
    lnrows = sbt("lnrows", [16, 64], F32)
    lnpr2 = sbt("lnpr2", [2, 128], F32)
    knh = sbt("knh", [128, NTC * NH], F32)
    ybsb = sbt("ybsb", [128, 8 * 65], BF16)
    trsc = sbt("trsc", [64, 64], BF16)
    indsb = sbt("indsb", [128, 2], F32)

    psum = ctx.enter_context(nc.psum_tensor("ps", [128, 4096], F32))
    sems = {e: ctx.enter_context(nc.semaphore("s_" + e)) for e in ENGINES}
    NSLOT = 10
    dslot = {q: [ctx.enter_context(nc.semaphore("d_%s%d" % (q, i)))
                 for i in range(NSLOT)] for q in ("sp", "pool")}
    ccsem1 = ctx.enter_context(nc.semaphore("cc1s"))
    ccsem2 = ctx.enter_context(nc.semaphore("cc2s"))

    XRES, XHAT = 0, 16384
    TMPA, TMPB, TMPC = 24576, 26624, 28672
    ROWS = 30720
    WBLK, YQIN = 34816, 38912
    WSTG = 43008
    BFTA, BFTC = 47104, 48128

    def f32(off, n, p0=0, p1=128):
        return AR[p0:p1, off:off + n]

    def bft(off_f32, nb, p0=0, p1=128):
        return ARB[p0:p1, 2 * off_f32:2 * off_f32 + nb]

    tmpa = f32(TMPA, 2048)
    tmpb = f32(TMPB, 2048)
    tmpc = f32(TMPC, 2048)
    bfta = bft(BFTA, 2048)
    bftc = bft(BFTC, 2048)

    iota_c = sm[:, 8:9]
    rcol = sm[:, 1:2]
    negr = sm[:, 2:3]
    rcol2 = sm[:, 3:4]
    coffcol = sm[:, 5:6]
    onesrow = sm[0:1, 128:256]

    nc.const_aps.register(F32, float(EPS_LN), sm[:, 11:12])
    nc.const_aps.register(F32, 0.0, sm[:, 12:13])
    nc.const_aps.register(F32, 1.0, sm[:, 13:14])
    nc.const_aps.register(F32, M4, sm[:, 6:7])
    nc.const_aps.register(F32, -M4, sm[:, 7:8])
    nc.const_aps.register(F32, MAGIC, sm[:, 14:15])
    nc.const_aps.register(F32, -MAGIC, sm[:, 15:16])

    bank = [buf("pbank%d" % i) for i in range(8)]

    def g(eng, emit, reads=(), writes=(), deps=()):
        return sched.add(eng, emit, reads, writes, deps)

    def dmag(eng, emit, n, reads=(), writes=(), deps=()):
        return sched.add(eng, emit, reads, writes, deps, dma=True, ndma=n)

    xres_b = [buf("xres%d" % c) for c in range(NCH)]
    xhat_b = [buf("xhat%d" % c) for c in range(NCH)]

    dmag("sp", lambda e: [e.dma_start(f32(XRES + c * T, T),
                                      x_in[c * 128:(c + 1) * 128, :])
                          for c in range(NCH)], NCH, writes=xres_b)
    dmag("sp", lambda e: [e.dma_start(b1sb[:, :], b1_d[:, :]),
                          e.dma_start(b2sb[:, :], b2_d[:, :]),
                          e.dma_start(bqsb[:, :], bq_d[:, :]),
                          e.dma_start(bksb[:, :], bk_d[:, :]),
                          e.dma_start(bosb[:, :], bo_d[:, :]),
                          e.dma_start(idxsb[:, :], lidx_d[:, :]),
                          e.dma_start(sm[0:1, 0:1], temp_d[:, :]),
                          e.dma_start(sm[0:1, 4:5], coff_d[:, :]),
                          e.dma_start(halfind[:, :], hind_d[:, :])],
         9, writes=[buf("consts"), buf("hind")])
    g("dve", lambda e: e.tensor_scalar(b1q4[:, :], b1sb[:, :], 4.0, None,
                                       OP.mult),
      reads=[buf("consts")], writes=[buf("b1q4")])

    def ph_iota(e):
        e.iota(iota_c, [[0, 1]], channel_multiplier=1,
               allow_small_or_imprecise_dtypes=True)
        return e.iota(ident[:, :], [[1, 128]], channel_multiplier=0,
                      allow_small_or_imprecise_dtypes=True)
    g("pool", ph_iota, writes=[buf("iota"), buf("identraw")])

    def ph_ones(e):
        e.memset(sm[:, 11:12], EPS_LN)
        e.memset(sm[:, 12:13], 0.0)
        e.memset(sm[:, 13:14], 1.0)
        e.memset(sm[:, 6:7], M4)
        e.memset(sm[:, 7:8], -M4)
        e.memset(sm[:, 14:15], MAGIC)
        e.memset(sm[:, 15:16], -MAGIC)
        e.memset(onesrow, 1.0)
        e.memset(onesb[:, :], 1.0)
        e.memset(ones2[:, :], 1.0)
        e.memset(Lfmze[:, :], 0.0)
        e.memset(Lfmzo[:, :], 0.0)
        return e.memset(onesrb[:, :], 1.0)
    g("dve", ph_ones, writes=[buf("ones")])

    def ph_ident(e):
        e.tensor_scalar(ident[:, :], ident[:, :], iota_c, None, OP.is_equal)
        e.tensor_scalar(indsb[:, 0:1], iota_c, 64.0, None, OP.is_lt)
        e.tensor_copy(identb[:, :], ident[:, :])
        e.tensor_scalar(indsb[:, 1:2], iota_c, 63.5, None, OP.is_gt)
        e.tensor_copy(identdual[0:64, :], ident[0:64, 0:64])
        return e.tensor_scalar(iepsdual[0:64, :], ident[0:64, 0:64], REG, None,
                               OP.mult)
    g("dve", ph_ident, reads=[buf("iota"), buf("identraw")],
      writes=[buf("ident"), buf("idual_t"), buf("ind")])
    dmag("pool", lambda e: [e.dma_start(identdual[64:128, :], identdual[0:64, :]),
                            e.dma_start(iepsdual[64:128, :], iepsdual[0:64, :])],
         2, reads=[buf("idual_t")], writes=[buf("idual")])

    g("pe", lambda e: e.matmul(psum[:, 0:1], onesrow, sm[0:1, 0:1],
                               start=True, stop=True),
      reads=[buf("consts"), buf("ones")], writes=[bank[0]])

    def ph_tder(e):
        e.reciprocal(rcol, psum[:, 0:1])
        e.tensor_scalar(sm[:, 10:11], psum[:, 0:1], -1.0, None, OP.mult)
        e.tensor_copy(sm[:, 0:1], psum[:, 0:1])
        e.tensor_scalar(rcol2, rcol, 2.0, None, OP.mult)
        return e.reciprocal(negr, sm[:, 10:11])
    gtd = g("dve", ph_tder, reads=[bank[0]], writes=[buf("temps")])

    g("pe", lambda e: e.matmul(psum[:, 2:3], onesrow, sm[0:1, 4:5],
                               start=True, stop=True),
      reads=[buf("consts"), buf("ones")], writes=[bank[0]], deps=[gtd])
    g("act", lambda e: e.activation(coffcol, psum[:, 2:3], AF.Identity,
                                    bias=0.0, scale=1.0),
      reads=[bank[0]], writes=[buf("coffcol")])

    def ph_idx(e):
        e.tensor_copy(sm[0:1, 16:16 + NL], idxsb[0:1, :])
        e.memset(sm[:, 9:10], 0.0)
        return e.tensor_scalar(sm[0:1, 16:16 + NL], sm[0:1, 16:16 + NL],
                               0.0, float(2 * T - 1), OP.max, OP.min)
    g("dve", ph_idx, reads=[buf("consts")], writes=[buf("idxrow")])
    g("pe", lambda e: e.matmul(psum[:, 512:512 + NL], onesrow,
                               sm[0:1, 16:16 + NL], start=True, stop=True),
      reads=[buf("idxrow"), buf("ones")], writes=[bank[1]])

    def ph_E(e):
        last = None
        for tc in range(NTC):
            e.tensor_scalar(tmpa[:, tc * NL:(tc + 1) * NL],
                            psum[:, 512:512 + NL],
                            float(-128 * tc), coffcol, OP.add, OP.subtract)
        for tc in range(NTC):
            last = e.tensor_scalar(Esb[:, tc * NL:(tc + 1) * NL],
                                   tmpa[:, tc * NL:(tc + 1) * NL], iota_c,
                                   None, OP.is_equal)
        return last
    g("dve", ph_E, reads=[bank[1], buf("iota"), buf("coffcol")],
      writes=[buf("E")])
    sched.barrier()

    def ln_stats(xb_fn, xbufs, tag):
        sqsl = lambda dc: (bfta if dc < 4 else bftc)[:, (dc % 4) * 512:
                                                     (dc % 4) * 512 + 512]
        for nb in range(NTB):
            g("pe", (lambda nb=nb: lambda e: [e.matmul(
                psum[0:1, 3072:3584], onesb[:, 0:1],
                xb_fn(dc)[:, nb * 512:nb * 512 + 512],
                start=(dc == 0), stop=(dc == NCH - 1))
                for dc in range(NCH)][-1])(),
              reads=list(xbufs) + [buf("ones")], writes=[bank[6]])
            g("dve", (lambda nb=nb: lambda e: e.tensor_copy(
                f32(ROWS + nb * 512, 512, 0, 1), psum[0:1, 3072:3584]))(),
              reads=[bank[6]], writes=[buf("sx" + tag)])
            gsq = g("dve", (lambda nb=nb: lambda e: [e.tensor_tensor(
                sqsl(dc),
                xb_fn(dc)[:, nb * 512:nb * 512 + 512],
                xb_fn(dc)[:, nb * 512:nb * 512 + 512], OP.mult)
                for dc in range(NCH)][-1])(),
                reads=list(xbufs), writes=[buf("sq" + tag)])
            g("pe", (lambda nb=nb: lambda e: [e.matmul(
                psum[0:1, 3584:4096], onesb[:, 0:1], sqsl(dc),
                start=(dc == 0), stop=(dc == NCH - 1))
                for dc in range(NCH)][-1])(),
              reads=[buf("sq" + tag), buf("ones")], writes=[bank[7]],
              deps=[gsq])
            g("dve", (lambda nb=nb: lambda e: e.tensor_copy(
                f32(ROWS + 2048 + nb * 512, 512, 0, 1), psum[0:1, 3584:4096]))(),
              reads=[bank[7]], writes=[buf("sx2" + tag)])

        def rowmath(e):
            mean = f32(ROWS, T, 0, 1)
            ex2 = f32(ROWS + 2048, T, 0, 1)
            e.tensor_scalar(mean, mean, 1.0 / D, None, OP.mult)
            e.tensor_scalar(ex2, ex2, 1.0 / D, None, OP.mult)
            e.tensor_tensor(tmpa[0:1, :], mean, mean, OP.mult)
            return e.tensor_tensor(ex2, ex2, tmpa[0:1, :], OP.subtract)
        g("dve", rowmath, reads=[buf("sx" + tag), buf("sx2" + tag)],
          writes=[buf("var" + tag), buf("sx" + tag)])
        g("act", lambda e: e.activation(f32(ROWS + 2048, T, 0, 1),
                                        f32(ROWS + 2048, T, 0, 1), AF.Sqrt,
                                        bias=EPS_LN, scale=1.0),
          reads=[buf("var" + tag)], writes=[buf("sd" + tag)])

        def rowmath2(e):
            mean = f32(ROWS, T, 0, 1)
            sd = f32(ROWS + 2048, T, 0, 1)
            e.reciprocal(tmpa[0:1, :], sd)
            e.tensor_tensor(tmpb[0:1, :], mean, tmpa[0:1, :], OP.mult)
            e.tensor_copy(mean, tmpa[0:1, :])
            return e.tensor_copy(sd, tmpb[0:1, :])
        return g("dve", rowmath2, reads=[buf("sd" + tag), buf("sx" + tag)],
                 writes=[buf("rows" + tag)])

    def ln_apply(src_fn, srcbufs, dst_fn, dstbufs, tag, gstat):
        for nb in range(NTB):
            gb1 = g("pe", (lambda nb=nb: lambda e: e.matmul(
                psum[:, 2048:2560], onesrow, f32(ROWS + nb * 512, 512, 0, 1),
                start=True, stop=True))(),
                reads=[buf("rows" + tag), buf("ones")], writes=[bank[4]],
                deps=[gstat])
            gb2 = g("pe", (lambda nb=nb: lambda e: e.matmul(
                psum[:, 2560:3072], onesrow,
                f32(ROWS + 2048 + nb * 512, 512, 0, 1),
                start=True, stop=True))(),
                reads=[buf("rows" + tag), buf("ones")], writes=[bank[5]],
                deps=[gstat])
            for dc in range(NCH):
                gm = g("dve", (lambda nb=nb, dc=dc: lambda e: e.tensor_tensor(
                    tmpa[:, nb * 512:nb * 512 + 512],
                    src_fn(dc)[:, nb * 512:nb * 512 + 512],
                    psum[:, 2048:2560], OP.mult))(),
                    reads=[srcbufs[dc], bank[4]], writes=[buf("lnt" + tag)],
                    deps=[gb1])
                g("dve", (lambda nb=nb, dc=dc: lambda e: e.tensor_tensor(
                    dst_fn(dc)[:, nb * 512:nb * 512 + 512],
                    tmpa[:, nb * 512:nb * 512 + 512],
                    psum[:, 2560:3072], OP.subtract))(),
                  reads=[buf("lnt" + tag), bank[5]], writes=[dstbufs[dc]],
                  deps=[gb2, gm])

    def ffn_mm1(xhat_fn, xhatbufs, tag, first):
        wb = buf("wblk" + tag)
        for fb in range(4):
            for kc in range(NCH):
                st = buf("wstg%d" % (kc % 2))
                if first:
                    gl = dmag("sp", (lambda kc=kc, fb=fb: lambda e: [e.dma_start(
                        f32(WSTG + (kc % 2) * 2048, 1024),
                        w1_d[kc * 128:(kc + 1) * 128,
                             fb * 1024:(fb + 1) * 1024])])(), 1, writes=[st])
                    gc = g("dve", (lambda kc=kc: lambda e: e.tensor_copy(
                        bft(WBLK, 8192)[:, kc * 1024:(kc + 1) * 1024],
                        f32(WSTG + (kc % 2) * 2048, 1024)))(),
                        reads=[st], writes=[wb], deps=[gl])
                    dmag("pool", (lambda kc=kc, fb=fb: lambda e: [e.dma_start(
                        w1b_d[:, kc * FF + fb * 1024:kc * FF + (fb + 1) * 1024],
                        bft(WBLK, 8192)[:, kc * 1024:(kc + 1) * 1024])])(),
                        1, reads=[wb], deps=[gc])
                else:
                    dmag("sp", (lambda kc=kc, fb=fb: lambda e: [e.dma_start(
                        bft(WBLK, 8192)[:, kc * 1024:(kc + 1) * 1024],
                        w1b_d[:, kc * FF + fb * 1024:kc * FF + (fb + 1) * 1024])])(),
                        1, writes=[wb])
            for fci in range(8):
                fc = fb * 8 + fci
                pb = (fc % 2) * 4
                pbufs = [bank[pb + i] for i in range(4)]

                def mm1(e, fci=fci, pb=pb):
                    last = None
                    for kc in range(NCH):
                        for nb in range(NTB):
                            last = e.matmul(
                                psum[:, (pb + nb) * 512:(pb + nb) * 512 + 512],
                                bft(WBLK, 8192)[:, kc * 1024 + fci * 128:
                                                kc * 1024 + fci * 128 + 128],
                                xhat_fn(kc)[:, nb * 512:nb * 512 + 512],
                                start=(kc == 0), stop=(kc == NCH - 1))
                    return last
                g("pe", mm1, reads=[wb] + list(xhatbufs), writes=pbufs)
                g("act", (lambda fc=fc, pb=pb: lambda e: e.activation(
                    tmpa[:, :], psum[:, pb * 512:pb * 512 + 2048], AF.Identity,
                    bias=b1q4[:, fc:fc + 1], scale=4.0))(),
                  reads=pbufs + [buf("b1q4")], writes=[buf("h" + tag)])
                g("dve", lambda e: e.tensor_scalar(
                    tmpb[:, :], tmpa[:, :], M4, M4, OP.add, OP.subtract),
                  reads=[buf("h" + tag)], writes=[buf("hq" + tag)])

                def pwl(e):
                    T2I, T2A, T1AK = (_PWL_OPS["PWL_T2I"], _PWL_OPS["PWL_T2A"],
                                      _PWL_OPS["PWL_T1AK"])
                    e._custom_dve(T2I, out=tmpc[:, :], in0=tmpb[:, :],
                                  s0=_PWL4[0][0], s1=_PWL4[0][1],
                                  imm2=_PWL4[1][1])
                    for j in range(2, 32, 2):
                        e._custom_dve(T2A, out=tmpc[:, :], in0=tmpb[:, :],
                                      in1=tmpc[:, :], s0=_PWL4[j][0],
                                      s1=_PWL4[j][1], imm2=_PWL4[j + 1][1])
                    e._custom_dve(T1AK, out=tmpc[:, :], in0=tmpb[:, :],
                                  in1=tmpc[:, :], s0=_PWL4[32][0],
                                  s1=_PWL4[32][1], imm2=PWL_K)
                    return e.tensor_scalar(bfta[:, :], tmpc[:, :], MAGIC, MAGIC,
                                           OP.add, OP.subtract)
                gp = g("dve", pwl, reads=[buf("hq" + tag)],
                       writes=[buf("yqo" + tag), buf("h" + tag),
                               buf("sq" + tag)])
                dmag("pool", (lambda fc=fc: lambda e: [e.dma_start(
                    yq_d[fc * 128:(fc + 1) * 128, :], bfta[:, :])])(), 1,
                    reads=[buf("yqo" + tag)],
                    writes=[buf("yqd%d" % fb)], deps=[gp])

    def ffn_mm2(res_fn, resbufs, cast_fn, castbufs, dma_out, tag, first):
        w2stA = bft(WSTG + 1024, 1024)
        w2stB = bft(WSTG + 1536, 1024)
        u2 = [f32(WSTG + 3072, 512), f32(WSTG + 3584, 512)]
        u2b_ = [buf("u2a"), buf("u2b")]
        if first:
            for dc in range(NCH):
                gl = dmag("sp", (lambda dc=dc: lambda e: [e.dma_start(
                    f32(WSTG, 4096).rearrange("p (c n) -> p c n", n=128),
                    w2_d[:, dc * 128:(dc + 1) * 128].rearrange(
                        "(c p) n -> p c n", p=128))])(), 1,
                    writes=[buf("wstgA"), buf("wstg0"), buf("wstg1"),
                            buf("w2q0"), buf("w2q1")] + u2b_)
                gc = g("dve", (lambda dc=dc: lambda e: [
                    e.tensor_copy(bft(YQIN, 2048), f32(WSTG, 2048)),
                    e.tensor_copy(bft(YQIN + 1024, 2048),
                                  f32(WSTG + 2048, 2048))][-1])(),
                    reads=[buf("wstgA"), buf("wstg0"), buf("wstg1"),
                           buf("w2q0"), buf("w2q1"), buf("u2a"), buf("u2b")],
                    writes=[buf("w2cst"), buf("yqq0"), buf("yqq1")],
                    deps=[gl])
                dmag("pool", (lambda dc=dc: lambda e: [
                    e.dma_start(w2b_d[:, dc * 4096:dc * 4096 + 2048],
                                bft(YQIN, 2048)),
                    e.dma_start(w2b_d[:, dc * 4096 + 2048:dc * 4096 + 4096],
                                bft(YQIN + 1024, 2048))])(), 2,
                    reads=[buf("w2cst"), buf("yqq0"), buf("yqq1"),
                           buf("yqguard")],
                    writes=[buf("w2bd%d" % dc)])
        for tb in range(NTB):
            for qtr in range(4):
                par = (tb * 4 + qtr) % 2
                yqb = buf("yqq%d" % par)
                gyq = dmag("sp", (lambda tb=tb, qtr=qtr, par=par:
                                  lambda e: [e.dma_start(
                    bft(YQIN + par * 2048, 4096)[:, k * 512:k * 512 + 512],
                    yq_d[(qtr * 8 + k) * 128:(qtr * 8 + k + 1) * 128,
                         tb * 512:tb * 512 + 512]) for k in range(8)])(),
                    8, reads=[buf("yqd%d" % qtr)],
                    writes=[yqb, buf("yqguard")])
                for dc in range(NCH):
                    wdb = buf("w2q%d" % (dc % 2))
                    wst = w2stA if dc % 2 == 0 else w2stB
                    gw = dmag("sp", (lambda dc=dc, qtr=qtr, wst=wst:
                                     lambda e: [e.dma_start(
                        wst[:, :],
                        w2b_d[:, dc * 4096 + qtr * 1024:
                              dc * 4096 + (qtr + 1) * 1024])])(), 1,
                        reads=[buf("w2bd%d" % dc)], writes=[wdb])

                    def mm2(e, dc=dc, qtr=qtr, par=par, wst=wst):
                        last = None
                        for k in range(8):
                            last = e.matmul(
                                psum[:, dc * 512:dc * 512 + 512],
                                wst[:, k * 128:k * 128 + 128],
                                bft(YQIN + par * 2048, 4096)
                                [:, k * 512:k * 512 + 512],
                                start=(qtr == 0 and k == 0),
                                stop=(qtr == 3 and k == 7))
                        return last
                    g("pe", mm2, reads=[wdb, yqb], writes=[bank[dc]],
                      deps=[gw, gyq])
            for dp in range(0, 8, 2):
                g("act", (lambda dp=dp: lambda e: [
                    e.activation(u2[0][:, :],
                                 psum[:, dp * 512:dp * 512 + 512],
                                 AF.Identity, bias=b2sb[:, dp:dp + 1],
                                 scale=1.0),
                    e.activation(u2[1][:, :],
                                 psum[:, (dp + 1) * 512:(dp + 1) * 512 + 512],
                                 AF.Identity, bias=b2sb[:, dp + 1:dp + 2],
                                 scale=1.0)][-1])(),
                  reads=[bank[dp], bank[dp + 1], buf("consts")],
                  writes=u2b_)
                g("act", lambda e: [
                    e.activation(u2[0][:, :], u2[0][:, :], AF.Identity,
                                 bias=MAGIC, scale=1.0),
                    e.activation(u2[1][:, :], u2[1][:, :], AF.Identity,
                                 bias=MAGIC, scale=1.0),
                    e.activation(u2[0][:, :], u2[0][:, :], AF.Identity,
                                 bias=-MAGIC, scale=1.0),
                    e.activation(u2[1][:, :], u2[1][:, :], AF.Identity,
                                 bias=-MAGIC, scale=1.0)][-1],
                  reads=u2b_, writes=u2b_)
                gy = g("dve", (lambda tb=tb, dp=dp: lambda e: [
                    e.scalar_tensor_tensor(
                        res_fn(dp + i)[:, tb * 512:tb * 512 + 512],
                        u2[i][:, :], 0.5,
                        res_fn(dp + i)[:, tb * 512:tb * 512 + 512],
                        OP.mult, OP.add) for i in range(2)][-1])(),
                    reads=u2b_ + [resbufs[dp], resbufs[dp + 1]],
                    writes=[resbufs[dp], resbufs[dp + 1]])
                if dma_out is not None:
                    dmag("pool", (lambda tb=tb, dp=dp: lambda e: [e.dma_start(
                        dma_out[(dp + i) * 128:(dp + i + 1) * 128,
                                tb * 512:tb * 512 + 512],
                        res_fn(dp + i)[:, tb * 512:tb * 512 + 512])
                        for i in range(2)])(),
                        2, reads=[resbufs[dp], resbufs[dp + 1]], deps=[gy])

    awb = bft(TMPC, 8192)
    bkr = f32(TMPA, D, 0, 1)
    bvr = f32(TMPA + 1024, D, 0, 1)
    ktm_bufs = [buf("ktm%d" % i) for i in range(NTC)]
    kfm_bufs = [buf("kfm%d" % i) for i in range(NCH)]
    vtm_bufs = [buf("vtm%d" % i) for i in range(NTC)]
    qfm_bufs = [buf("qfm%d" % i) for i in range(NCH)]
    Lsb = f32(WSTG, 1024, 0, 64)
    Lbb = bft(WSTG + 1024, 1024, 0, 64)
    lsq = f32(WSTG + 1536, 1024, 0, 64)
    cc2st = f32(WSTG + 2560, NH * 65, 0, 64)
    asb = f32(TMPB, 1056)
    norm16 = f32(WBLK, 2048, 0, 16)
    rnb16 = bft(WBLK + 2048, 2048, 0, 16)

    awk_ap = bft(WBLK, 8192)

    def load_attn_w(dram_h, tagc, wap, extra_w=(), ceng="pool"):
        wbuf = buf("aw" + tagc)
        for kc in range(NCH):
            st = buf("wstg%d" % (kc % 2))
            gl = dmag("sp", (lambda kc=kc: lambda e: [e.dma_start(
                f32(WSTG + (kc % 2) * 2048, 1024),
                dram_h[kc * 128:(kc + 1) * 128, :])])(), 1, writes=[st])
            g(ceng, (lambda kc=kc: lambda e: e.tensor_copy(
                wap[:, kc * 1024:(kc + 1) * 1024],
                f32(WSTG + (kc % 2) * 2048, 1024)))(),
              reads=[st], writes=[wbuf] + list(extra_w), deps=[gl])
        return wbuf

    def proj_tm(dst_off, dstbufs, wbuf, wap, brow_ap, tagp, extra_w=()):
        bias_sb = f32(BFTA, 1024)
        for half in range(2):
            gbb_ = g("pe", (lambda half=half: lambda e: e.matmul(
                psum[:, (4 + half) * 512:(4 + half) * 512 + 512], onesrow,
                brow_ap[0:1, half * 512:half * 512 + 512],
                start=True, stop=True))(),
              reads=[buf("brow"), buf("ones")], writes=[bank[4 + half]])
            g("dve", (lambda half=half: lambda e: e.tensor_copy(
                f32(BFTA + half * 512, 512),
                psum[:, (4 + half) * 512:(4 + half) * 512 + 512]))(),
              reads=[bank[4 + half]], writes=[buf("biasb" + tagp)],
              deps=[gbb_])
        for tm in range(NTC):
            pb = (tm % 2) * 2
            pbufs = [bank[pb], bank[pb + 1]]

            def mm(e, tm=tm, pb=pb):
                last = None
                for kc in range(NCH):
                    for half in range(2):
                        last = e.matmul(
                            psum[:, (pb + half) * 512:(pb + half) * 512 + 512],
                            bft(XHAT + kc * 1024, 2048)[:, tm * 128:tm * 128 + 128],
                            wap[:, kc * 1024 + half * 512:
                                kc * 1024 + half * 512 + 512],
                            start=(kc == 0), stop=(kc == NCH - 1))
                return last
            g("pe", mm, reads=[wbuf] + list(xhat_b), writes=pbufs)
            for half in range(2):
                g("dve", (lambda tm=tm, half=half, pb=pb: lambda e:
                          e.tensor_tensor(
                    bft(dst_off + tm * 512, 1024)[:, half * 512:half * 512 + 512],
                    psum[:, (pb + half) * 512:(pb + half) * 512 + 512],
                    f32(BFTA + half * 512, 512),
                    OP.add))(),
                  reads=[bank[pb + half], buf("biasb" + tagp)],
                  writes=[dstbufs[tm]] + list(extra_w))

    def proj_fm(dst_off, dstbufs, wbuf, wap, bias_sb, tagp, extra_w=()):
        for oc in range(NCH):
            pb = (oc % 2) * 4
            pbufs = [bank[pb + i] for i in range(4)]

            def mm(e, oc=oc, pb=pb):
                last = None
                for kc in range(NCH):
                    for nb in range(NTB):
                        last = e.matmul(
                            psum[:, (pb + nb) * 512:(pb + nb) * 512 + 512],
                            wap[:, kc * 1024 + oc * 128:kc * 1024 + oc * 128 + 128],
                            bft(XHAT + kc * 1024, 2048)[:, nb * 512:nb * 512 + 512],
                            start=(kc == 0), stop=(kc == NCH - 1))
                return last
            g("pe", mm, reads=[wbuf] + list(xhat_b), writes=pbufs)
            g("act", (lambda oc=oc, pb=pb: lambda e: e.activation(
                bft(dst_off + oc * 1024, 2048),
                psum[:, pb * 512:pb * 512 + 2048],
                AF.Identity, bias=bias_sb[:, oc:oc + 1], scale=1.0))(),
              reads=pbufs + [buf("consts")],
              writes=[dstbufs[oc]] + list(extra_w))

    if LIM < 1:
        return _finish(locals())
    sched.phase = "LN1"
    xb_fn = lambda dc: bft(XHAT + dc * 1024, 2048)
    xb1_bufs = [buf("xb1_%d" % dc) for dc in range(NCH)]
    for dc in range(NCH):
        g("dve", (lambda dc=dc: lambda e: e.tensor_copy(
            xb_fn(dc), f32(XRES + dc * T, T)))(),
          reads=[xres_b[dc]], writes=[xb1_bufs[dc]])
    gst1 = ln_stats(xb_fn, xb1_bufs, "L1")
    ln_apply(lambda dc: f32(XRES + dc * T, T), xres_b, xb_fn, xhat_b,
             "L1", gst1)
    sched.barrier()
    if LIM < 2:
        return _finish(locals())
    sched.phase = "FFN1"
    ffn_mm1(xb_fn, xhat_b, "F1", True)
    ffn_mm2(lambda dc: f32(XRES + dc * T, T), xres_b, None, None, x1_d,
            "F1", True)
    wkbuf = load_attn_w(wk_d, "k", awk_ap, extra_w=[buf("wblkF1")])
    sched.barrier()
    if LIM < 4:
        return _finish(locals())
    sched.phase = "LN2"
    xb2_bufs = [buf("xb2_%d" % dc) for dc in range(NCH)]
    for dc in range(NCH):
        g("dve", (lambda dc=dc: lambda e: e.tensor_copy(
            xb_fn(dc), f32(XRES + dc * T, T)))(),
          reads=[xres_b[dc]], writes=[xb2_bufs[dc]])
    gst2 = ln_stats(xb_fn, xb2_bufs, "L2")
    ln_apply(lambda dc: f32(XRES + dc * T, T), xres_b, xb_fn, xhat_b,
             "L2", gst2)
    sched.barrier()

    if LIM < 5:
        return _finish(locals())
    sched.phase = "A1ktm"
    dmag("sp", lambda e: [e.dma_start(bkr, bkrow_d[:, :]),
                          e.dma_start(bvr, bvrow_d[:, :])], 2,
         writes=[buf("brow")])
    wvbuf = load_attn_w(wv_d, "v", awb,
                        extra_w=[buf("yqaccF1"), buf("rowsL2")])
    proj_tm(XRES, ktm_bufs, wkbuf, awk_ap, bkr, "ktm")
    sched.barrier()

    if LIM < 6:
        return _finish(locals())
    sched.phase = "A2Lpart"
    for tc in range(NTC):
        g("pe", (lambda tc=tc: lambda e: [
            e.matmul(psum[0:64, 3072 + half * 512:3072 + half * 512 + 512],
                     Esb[:, tc * NL:(tc + 1) * NL],
                     bft(XRES + tc * 512, 1024)[:, half * 512:half * 512 + 512],
                     start=(tc == 0), stop=(tc == NTC - 1))
            for half in range(2)][-1])(),
          reads=[buf("E"), ktm_bufs[tc]], writes=[bank[6], bank[7]])
    g("dve", lambda e: e.tensor_copy(Lsb, psum[0:64, 3072:4096]),
      reads=[bank[6], bank[7]], writes=[buf("Lpart")])
    dmag("pool", lambda e: [e.dma_start(cc1_in[:, :], Lsb)], 1,
         reads=[buf("Lpart")])

    def cc1(e):
        ins = e.collective_compute(
            "AllReduce", OP.add, replica_groups=RG,
            ins=[cc1_in[:, :].opt()], outs=[cc1_out[:, :].opt()])
        ins.then_inc(ccsem1, 1)
        e.wait_ge(ccsem1, 1)
        return e.engine_nop()
    gcc1 = g("pool", cc1, deps=[sched.by_eng["pool"][-1]])
    sched.barrier()

    if LIM < 7:
        return _finish(locals())
    sched.phase = "A4kfm"
    proj_fm(XRES + 8192, kfm_bufs, wkbuf, awk_ap, bksb, "kfm")
    sched.barrier()

    if LIM < 8:
        return _finish(locals())
    sched.phase = "A3vtm"
    proj_tm(WBLK, vtm_bufs, wvbuf, awb, bvr, "vtm", extra_w=[wkbuf])
    sched.barrier()

    if LIM < 9:
        return _finish(locals())
    sched.phase = "A5L"
    wqbuf = load_attn_w(wq_d, "q", awb, extra_w=[buf("awv")])
    gLl = dmag("pool", lambda e: [e.dma_start(Lsb, cc1_out[:, :])],
               1, writes=[buf("L")], deps=[gcc1])

    for tc in range(NTC):
        so = (tc % 2) * 1024
        gsq = g("dve", (lambda tc=tc, so=so: lambda e: [e.tensor_tensor(
            tmpa[:, so + dc * 128:so + dc * 128 + 128],
            bft(XRES + 8192 + dc * 1024, 2048)[:, tc * 128:tc * 128 + 128],
            bft(XRES + 8192 + dc * 1024, 2048)[:, tc * 128:tc * 128 + 128],
            OP.mult) for dc in range(NCH)][-1])(),
            reads=list(kfm_bufs), writes=[buf("ksq%d" % (tc % 2))])
        pk = 4 + tc % 2
        g("pe", (lambda tc=tc, so=so, pk=pk: lambda e: [e.matmul(
            psum[:, pk * 512 + 2 * dc:pk * 512 + 2 * dc + 2],
            tmpa[:, so + dc * 128:so + dc * 128 + 128], indsb[:, :],
            start=True, stop=True) for dc in range(NCH)][-1])(),
          reads=[buf("ksq%d" % (tc % 2)), buf("ind")], writes=[bank[pk]],
          deps=[gsq])
        g("dve", (lambda tc=tc, pk=pk: lambda e: e.tensor_scalar(
            knh[:, tc * NH:(tc + 1) * NH], psum[:, pk * 512:pk * 512 + NH],
            0.5, None, OP.mult))(),
          reads=[bank[pk]], writes=[buf("knh")])

    def lder(e):
        e.tensor_copy(Lbb[:, :], Lsb)
        e.tensor_tensor(lsq, Lbb[:, :], Lbb[:, :], OP.mult)
        e.tensor_reduce(lncol[:, :], lsq.rearrange("p (h d) -> p h d", d=64),
                        mybir.AxisListType.X, OP.add)
        return e.engine_nop()
    gld = g("dve", lder, reads=[buf("L")], writes=[buf("Lb"), buf("LnC")])
    g("act", lambda e: e.activation(elcol[:, :], lncol[:, :], AF.Exp,
                                    bias=0.0, scale=negr[0:64, :]),
      reads=[buf("LnC"), buf("temps")], writes=[buf("eL")])
    gt1 = g("pe", lambda e: e.transpose(psum[0:16, 1536:1536 + 64],
                                        lncol[:, :], ident[0:64, 0:64]),
            reads=[buf("LnC"), buf("ident")], writes=[bank[3]])
    g("dve", lambda e: e.tensor_copy(lnrows[:, :], psum[0:16, 1536:1536 + 64]),
      reads=[bank[3]], writes=[buf("lnrows")])

    def lnpair_dma(e):
        ops = []
        for h in range(NH):
            p = h // 2
            dst = Lnpair[0:64, p:p + 1] if h % 2 == 0 else Lnpair[64:128, p:p + 1]
            ops.append(e.dma_start(dst, lncol[0:64, h:h + 1],
                                   allow_slow_non_contiguous=True))
        return ops
    dmag("pool", lnpair_dma, NH, reads=[buf("LnC")], writes=[buf("lnpair")])

    for h in range(NH):
        gt = g("pe", (lambda h=h: lambda e: e.transpose(
            psum[0:64, 1024 + (h % 2) * 64:1024 + (h % 2) * 64 + 64],
            Lsb[:, h * 64:(h + 1) * 64], ident[0:64, 0:64]))(),
            reads=[buf("L"), buf("ident")], writes=[bank[2]])
        if h % 2 == 0:
            g("dve", (lambda h=h: lambda e: e.tensor_copy(
                Lfmb[0:64, (h // 2) * 64:(h // 2) * 64 + 64],
                psum[0:64, 1024:1024 + 64]))(),
              reads=[bank[2]], writes=[buf("lfm")], deps=[gt])
        else:
            gcv = g("dve", (lambda h=h: lambda e: e.tensor_copy(
                trsc[:, :], psum[0:64, 1088:1088 + 64]))(),
                reads=[bank[2]], writes=[buf("lfmtmp")], deps=[gt])
            dmag("pool", (lambda h=h: lambda e: [e.dma_start(
                Lfmb[64:128, (h // 2) * 64:(h // 2) * 64 + 64], trsc[:, :])])(),
                1, reads=[buf("lfmtmp")], writes=[buf("lfm")])
    g("dve", lambda e: [
        e.tensor_copy(Lfmze[0:64, :], Lfmb[0:64, :]),
        e.tensor_copy(Lfmzo[64:128, :], Lfmb[64:128, :])][-1],
      reads=[buf("lfm")], writes=[buf("lfmz")])
    sched.barrier()

    if LIM < 10:
        return _finish(locals())
    sched.phase = "A9qfm"
    proj_fm(XRES, qfm_bufs, wqbuf, awb, bqsb, "qfm", extra_w=list(ktm_bufs))
    sched.barrier()
    sched.phase = "A6W"
    for pp in range(0, 8, 2):
        gw1 = g("pe", (lambda pp=pp: lambda e: [
            e.matmul(psum[0 if half == 0 else 64:64 if half == 0 else 128,
                          q * 128:q * 128 + 64],
                     Lfmb[0 if half == 0 else 64:64 if half == 0 else 128,
                          (pp + q) * 64:(pp + q) * 64 + 64],
                     Lfmb[0 if half == 0 else 64:64 if half == 0 else 128,
                          (pp + q) * 64:(pp + q) * 64 + 64],
                     start=True, stop=True)
            for q in range(2) for half in range(2)][-1])(),
            reads=[buf("lfm")], writes=[bank[0]])
        gpr = dmag("pool", (lambda pp=pp: lambda e: [
            e.dma_start(lnpr2[i:i + 1, q * 64:q * 64 + 64],
                        lnrows[2 * (pp + q) + i:2 * (pp + q) + i + 1, :])
            for q in range(2) for i in range(2)])(), 4,
            reads=[buf("lnrows")], writes=[buf("lnpr2")])
        gw2 = g("pe", (lambda pp=pp: lambda e: [
            e.matmul(psum[:, 256 + q * 64:256 + q * 64 + 64], halfind[:, :],
                     lnpr2[:, q * 64:q * 64 + 64], start=True, stop=True)
            for q in range(2)][-1])(),
            reads=[buf("lnpr2"), buf("ones")], writes=[bank[0]],
            deps=[gpr, gw1])

        def wsq(e, pp=pp):
            for q in range(2):
                e.tensor_scalar(tmpa[:, q * 64:q * 64 + 64],
                                psum[:, q * 128:q * 128 + 64], -2.0,
                                Lnpair[:, pp + q:pp + q + 1],
                                OP.mult, OP.add)
            for q in range(2):
                e.tensor_tensor(tmpa[:, q * 64:q * 64 + 64],
                                tmpa[:, q * 64:q * 64 + 64],
                                psum[:, 256 + q * 64:256 + q * 64 + 64],
                                OP.add)
            last = None
            for q in range(2):
                last = e.tensor_scalar(tmpa[:, q * 64:q * 64 + 64],
                                       tmpa[:, q * 64:q * 64 + 64], 0.0,
                                       negr, OP.max, OP.mult)
            return last
        g("dve", wsq, reads=[bank[0], buf("lnpair"), buf("temps")],
          writes=[buf("wsq")])
        gw4 = g("act", (lambda pp=pp: lambda e: [
            e.activation(asb[:, (pp + q) * 132:(pp + q) * 132 + 64],
                         tmpa[:, q * 64:q * 64 + 64], AF.Exp,
                         bias=0.0, scale=1.0) for q in range(2)][-1])(),
            reads=[buf("wsq")],
            writes=[buf("asbW%d" % pp), buf("asbW%d" % (pp + 1))])
        g("dve", (lambda pp=pp: lambda e: [
            e.tensor_tensor(asb[:, (pp + q) * 132:(pp + q) * 132 + 64],
                            asb[:, (pp + q) * 132:(pp + q) * 132 + 64],
                            iepsdual[:, :], OP.add) for q in range(2)][-1])(),
          reads=[buf("asbW%d" % pp), buf("asbW%d" % (pp + 1)), buf("idual")],
          writes=[buf("asbW%d" % pp), buf("asbW%d" % (pp + 1))])

    if LIM < 11:
        return _finish(locals())
    sched.phase = "A7phiK"
    wobuf = load_attn_w(wo_d, "o", awb, extra_w=[buf("awq")])
    for tc in range(NTC):
        pc = (tc % 2) * 2
        cb = [bank[pc], bank[pc + 1]]

        def cmm(e, tc=tc, pc=pc):
            last = None
            for dc in range(NCH):
                last = e.matmul(
                    psum[0:128, pc * 512 + 2 * dc * 64:
                         pc * 512 + 2 * dc * 64 + 64],
                    bft(XRES + 8192 + dc * 1024, 2048)
                    [:, tc * 128:tc * 128 + 128],
                    Lfmze[:, dc * 64:dc * 64 + 64],
                    start=True, stop=True)
                last = e.matmul(
                    psum[0:128, pc * 512 + (2 * dc + 1) * 64:
                         pc * 512 + (2 * dc + 1) * 64 + 64],
                    bft(XRES + 8192 + dc * 1024, 2048)
                    [:, tc * 128:tc * 128 + 128],
                    Lfmzo[:, dc * 64:dc * 64 + 64],
                    start=True, stop=True)
            return last
        g("pe", cmm, reads=[buf("lfmz")] + kfm_bufs, writes=cb)
        if SUB < 1:
            continue
        gsb = g("dve", (lambda tc=tc, pc=pc: lambda e: e.tensor_tensor(
            tmpa[:, 0:1024].rearrange("p (h d) -> p h d", d=64),
            psum[:, pc * 512:pc * 512 + 1024].rearrange(
                "p (h d) -> p h d", d=64),
            knh[:, tc * NH:(tc + 1) * NH].rearrange(
                "p (h o) -> p h o", o=1).broadcast_to([128, NH, 64]),
            OP.subtract))(),
            reads=cb + [buf("knh")], writes=[buf("ck")])
        gex = g("act", (lambda tc=tc: lambda e: e.activation(
            bftc[:, 0:1024], tmpa[:, 0:1024], AF.Exp, bias=0.0,
            scale=rcol2))(),
            reads=[buf("ck"), buf("temps")], writes=[buf("phik")])
        if SUB < 2:
            continue
        for h in range(NH):
            g("pe", (lambda tc=tc, h=h: lambda e: [
                e.matmul(psum[0:64, 3072 + h * 64:3072 + h * 64 + 64],
                         bftc[:, h * 64:h * 64 + 64],
                         bft(WBLK + tc * 512, 1024)[:, h * 64:h * 64 + 64],
                         start=(tc == 0 and h % 8 == 0),
                         stop=(tc == NTC - 1 and h % 8 == 7)),
                e.matmul(psum[0:64, 2560 + h:2560 + h + 1],
                         bftc[:, h * 64:h * 64 + 64], onesb[:, 0:1],
                         start=(tc == 0 and h == 0),
                         stop=(tc == NTC - 1 and h == NH - 1))][-1])(),
              reads=[buf("phik"), vtm_bufs[tc], buf("ones")],
              writes=[bank[6], bank[7], bank[5]])

    if SUB < 3:
        return _finish(locals())

    def zstage(e):
        e.tensor_tensor(cc2st[:, 0:1024].rearrange("p (h d) -> p h d", d=64),
                        psum[0:64, 3072:4096].rearrange(
                            "p (h d) -> p h d", d=64),
                        elcol[:, :].rearrange("p (h o) -> p h o", o=1)
                        .broadcast_to([64, NH, 64]), OP.mult)
        return e.tensor_tensor(cc2st[:, 1024:1040], psum[0:64, 2560:2576],
                               elcol[:, :], OP.mult)
    g("dve", zstage, reads=[bank[5], bank[6], bank[7], buf("eL")],
      writes=[buf("cc2st")])
    dmag("pool", lambda e: [e.dma_start(cc2_in[:, :], cc2st)], 1,
         reads=[buf("cc2st")])

    if SUB < 4:
        return _finish(locals())

    def cc2(e):
        ins = e.collective_compute(
            "AllReduce", OP.add, replica_groups=RG,
            ins=[cc2_in[:, :].opt()], outs=[cc2_out[:, :].opt()])
        ins.then_inc(ccsem2, 1)
        e.wait_ge(ccsem2, 1)
        return e.engine_nop()
    gcc2 = g("pool", cc2, deps=[sched.by_eng["pool"][-1]])

    def zload(e):
        ops = []
        for h in range(NH):
            p0 = (h % 2) * 64
            ops.append(e.dma_start(
                asb[p0:p0 + 64, (h // 2) * 132 + 64:(h // 2) * 132 + 128],
                cc2_out[0:64, h * 64:(h + 1) * 64]))
            ops.append(e.dma_start(
                asb[p0:p0 + 64, (h // 2) * 132 + 128:(h // 2) * 132 + 129],
                cc2_out[0:64, 1024 + h:1024 + h + 1],
                allow_slow_non_contiguous=True))
        return ops
    dmag("pool", zload, 32, writes=[buf("asbZ")], deps=[gcc2])
    sched.barrier()

    if LIM < 12:
        return _finish(locals())
    sched.phase = "A8solve"
    asb_buf = buf("asball")
    sel = f32(TMPB + 1920, 128)
    for k in range(NL):
        extra = ([buf("asbZ")] + [buf("asbW%d" % p) for p in range(8)]
                 if k == 0 else [asb_buf])
        gsel = g("dve", (lambda k=k: lambda e: [
            e.tensor_scalar(sel[:, 0:64], iota_c.broadcast_to([128, 64]),
                            float(k), None, OP.is_equal),
            e.tensor_scalar(sel[:, 64:128], iota_c.broadcast_to([128, 64]),
                            float(64 + k), None, OP.is_equal)][-1])(),
            reads=[buf("iota")], writes=[buf("sel")])
        gbc = g("pe", (lambda k=k: lambda e: [
            e.matmul(psum[:, 0:512], sel, asb[:, 0:512],
                     start=True, stop=True),
            e.matmul(psum[:, 512:1024], sel, asb[:, 512:1024],
                     start=True, stop=True),
            e.matmul(psum[:, 1024:1056], sel, asb[:, 1024:1056],
                     start=True, stop=True)][-1])(),
            reads=extra + [buf("sel")], writes=[bank[0], bank[1], bank[2]],
            deps=[gsel])

        def solstep(e, k=k):
            for q in range(2):
                e.reciprocal(sm[:, 32 + q * 4:36 + q * 4],
                             psum[:, k:1056:132][:, q * 4:q * 4 + 4])
            for q in range(2):
                e.tensor_tensor(sm[:, 40 + q * 4:44 + q * 4],
                                identdual[:, k:k + 1].broadcast_to([128, 4]),
                                asb[:, k::132][:, q * 4:q * 4 + 4],
                                OP.subtract)
            for q in range(2):
                e.tensor_tensor(sm[:, 48 + q * 4:52 + q * 4],
                                sm[:, 40 + q * 4:44 + q * 4],
                                sm[:, 32 + q * 4:36 + q * 4], OP.mult)
            for q in range(2):
                e.tensor_tensor(
                    f32(TMPA + q * 528, 528).rearrange(
                        "p (b n) -> p b n", n=132),
                    psum[:, q * 528:q * 528 + 528].rearrange(
                        "p (b n) -> p b n", n=132),
                    sm[:, 48 + q * 4:52 + q * 4].rearrange(
                        "p (b o) -> p b o", o=1)
                    .broadcast_to([128, 4, 132]), OP.mult)
            last = None
            for q in range(2):
                last = e.tensor_tensor(asb[:, q * 528:q * 528 + 528],
                                       asb[:, q * 528:q * 528 + 528],
                                       f32(TMPA + q * 528, 528), OP.add)
            return last
        g("dve", solstep, reads=[bank[0], bank[1], bank[2], buf("idual")],
          writes=[asb_buf])
        if k % 8 == 7:
            p = k // 8
            cqb = [bank[3], bank[4], bank[5], bank[6]]

            def cqmm_e(e, p=p):
                last = None
                for nb in range(NTB):
                    last = e.matmul(
                        psum[0:64, 1536 + nb * 512:1536 + nb * 512 + 512],
                        Lfmb[0:64, p * 64:p * 64 + 64],
                        bft(XRES + p * 1024, 2048)
                        [0:64, nb * 512:nb * 512 + 512],
                        start=True, stop=True)
                    last = e.matmul(
                        psum[64:128, 1536 + nb * 512:1536 + nb * 512 + 512],
                        Lfmb[64:128, p * 64:p * 64 + 64],
                        bft(XRES + p * 1024, 2048)
                        [64:128, nb * 512:nb * 512 + 512],
                        start=True, stop=True)
                return last
            g("pe", cqmm_e, reads=[buf("lfm"), qfm_bufs[p]], writes=cqb)
            for ch in range(4):
                gts = g("dve", (lambda p=p, ch=ch: lambda e: e.tensor_scalar(
                    f32(BFTC + (ch % 2) * 512, 512),
                    psum[:, 1536 + ch * 512:1536 + (ch + 1) * 512], 2.0,
                    Lnpair[:, p:p + 1], OP.mult, OP.subtract))(),
                    reads=[cqb[ch], buf("lnpair")],
                    writes=[buf("cqch%d" % (ch % 2))])
                g("act", (lambda p=p, ch=ch: lambda e: e.activation(
                    bft(XRES + 8192 + p * 1024 + ch * 256, 512),
                    f32(BFTC + (ch % 2) * 512, 512), AF.Exp,
                    bias=0.0, scale=rcol))(),
                  reads=[buf("cqch%d" % (ch % 2)), buf("temps")],
                  writes=[buf("phiq%d" % p)], deps=[gts])
    def ycopy(e):
        e.memset(sm[:, 9:10], 0.0)
        return e.tensor_copy(
            ybsb[:, :].rearrange("p (b n) -> p b n", n=65),
            asb[:, :].rearrange("p (b n) -> p b n", n=132)[:, :, 64:129])
    g("dve", ycopy, reads=[asb_buf], writes=[buf("yb")])
    sched.barrier()

    if LIM < 13:
        return _finish(locals())
    if LIM < 14:
        return _finish(locals())
    sched.phase = "A10phiQ"
    for p in range(8):
        phiqp = bft(XRES + 8192 + p * 1024, 2048)
        for nb in range(NTB):
            g("pe", (lambda p=p, nb=nb, phiqp=phiqp: lambda e: [
                e.matmul(psum[0:64, (4 + nb) * 512:(4 + nb) * 512 + 512],
                         ybsb[0:64, p * 65:p * 65 + 64],
                         phiqp[0:64, nb * 512:nb * 512 + 512],
                         start=True, stop=True),
                e.matmul(psum[64:128, (4 + nb) * 512:(4 + nb) * 512 + 512],
                         ybsb[64:128, p * 65:p * 65 + 64],
                         phiqp[64:128, nb * 512:nb * 512 + 512],
                         start=True, stop=True),
                e.matmul(psum[0:1, nb * 512:nb * 512 + 512],
                         ybsb[0:64, p * 65 + 64:p * 65 + 65],
                         phiqp[0:64, nb * 512:nb * 512 + 512],
                         start=True, stop=True),
                e.matmul(psum[64:65, nb * 512:nb * 512 + 512],
                         ybsb[64:128, p * 65 + 64:p * 65 + 65],
                         phiqp[64:128, nb * 512:nb * 512 + 512],
                         start=True, stop=True)][-1])(),
              reads=[buf("yb"), buf("phiq%d" % p)],
              writes=[bank[4 + nb], bank[nb]])
        g("dve", (lambda p=p: lambda e: e.tensor_copy(
            bft(XRES + 8192 + p * 1024, 2048), psum[:, 2048:4096]))(),
          reads=[bank[4], bank[5], bank[6], bank[7]],
          writes=[buf("attnb%d" % p), buf("phiq%d" % p)])
        gnv = g("dve", (lambda p=p: lambda e: [
            e.tensor_copy(AR[0:1, TMPB:TMPB + 2048], psum[0:1, 0:2048]),
            e.tensor_copy(AR[64:65, TMPB:TMPB + 2048], psum[64:65, 0:2048])][-1])(),
            reads=[bank[0], bank[1], bank[2], bank[3]], writes=[buf("nv")])
        dmag("pool", (lambda p=p: lambda e: [
            e.dma_start(norm16[2 * p:2 * p + 1, :], AR[0:1, TMPB:TMPB + 2048]),
            e.dma_start(norm16[2 * p + 1:2 * p + 2, :],
                        AR[64:65, TMPB:TMPB + 2048])])(), 2,
            reads=[buf("nv")], writes=[buf("n16_%d" % p)])

    def rncomp(e):
        e.tensor_scalar(norm16[:, :], norm16[:, :], 1e-10, None, OP.max)
        e.reciprocal(norm16[:, :], norm16[:, :])
        return e.tensor_copy(rnb16[:, :], norm16[:, :])
    g("dve", rncomp, reads=[buf("n16_%d" % p) for p in range(8)],
      writes=[buf("rn")])
    for p in range(8):
        dmag("pool", (lambda p=p: lambda e: [
            e.dma_start(ARB[0:1, 2 * TMPB:2 * TMPB + 2048],
                        rnb16[2 * p:2 * p + 1, :]),
            e.dma_start(ARB[64:65, 2 * TMPB:2 * TMPB + 2048],
                        rnb16[2 * p + 1:2 * p + 2, :])])(), 2,
            reads=[buf("rn")], writes=[buf("rnp0")])
        for nb in range(NTB):
            gbb = g("pe", (lambda nb=nb: lambda e: [
                e.matmul(psum[0:64, nb * 512:nb * 512 + 512],
                         onesrb[0:1, 0:64],
                         ARB[0:1, 2 * TMPB + nb * 512:2 * TMPB + nb * 512 + 512],
                         start=True, stop=True),
                e.matmul(psum[64:128, nb * 512:nb * 512 + 512],
                         ones2[64:65, :],
                         ARB[64:65, 2 * TMPB + nb * 512:
                             2 * TMPB + nb * 512 + 512],
                         start=True, stop=True)][-1])(),
                reads=[buf("rnp0"), buf("ones")], writes=[bank[nb]])
            gcp = g("dve", (lambda nb=nb: lambda e: e.tensor_copy(
                bftc[:, nb * 512:nb * 512 + 512],
                psum[:, nb * 512:nb * 512 + 512]))(),
                reads=[bank[nb]], writes=[buf("rnbc")], deps=[gbb])
            g("dve", (lambda p=p, nb=nb: lambda e: e.tensor_tensor(
                bft(XRES + 8192 + p * 1024, 2048)[:, nb * 512:nb * 512 + 512],
                bft(XRES + 8192 + p * 1024, 2048)[:, nb * 512:nb * 512 + 512],
                bftc[:, nb * 512:nb * 512 + 512], OP.mult))(),
              reads=[buf("attnb%d" % p), buf("rnbc")],
              writes=[buf("attnb%d" % p)], deps=[gcp])
    sched.barrier()

    if LIM < 15:
        return _finish(locals())
    sched.phase = "A11wo"
    x2_fn = lambda dc: (f32(XHAT + dc * 2048, 2048) if dc < 4 else
                        f32(XRES + (dc - 4) * 2048, 2048))
    x2_bufs = [buf("x2_%d" % dc) for dc in range(NCH)]
    x2b_fn = lambda dc: bft(WBLK + dc * 1024, 2048)
    x2b_bufs = [buf("x2b_%d" % dc) for dc in range(NCH)]
    for oc in range(NCH):
        pb = (oc % 2) * 4
        pbufs = [bank[pb + i] for i in range(4)]

        def mmo(e, oc=oc, pb=pb):
            last = None
            for kc in range(NCH):
                for nb in range(NTB):
                    last = e.matmul(
                        psum[:, (pb + nb) * 512:(pb + nb) * 512 + 512],
                        awb[:, kc * 1024 + oc * 128:kc * 1024 + oc * 128 + 128],
                        bft(XRES + 8192 + kc * 1024, 2048)
                        [:, nb * 512:nb * 512 + 512],
                        start=(kc == 0), stop=(kc == NCH - 1))
            return last
        g("pe", mmo, reads=[wobuf] + [buf("attnb%d" % p) for p in range(8)],
          writes=pbufs)
        gx1 = dmag("sp", (lambda oc=oc: lambda e: [e.dma_start(
            tmpb[:, :], x1_d[oc * 128:(oc + 1) * 128, :])])(), 1,
            writes=[buf("x1t")])
        g("act", (lambda oc=oc, pb=pb: lambda e: e.activation(
            tmpa[:, :], psum[:, pb * 512:pb * 512 + 2048],
            AF.Identity, bias=bosb[:, oc:oc + 1], scale=1.0))(),
          reads=pbufs + [buf("consts")], writes=[buf("at")])
        g("dve", (lambda oc=oc: lambda e: e.tensor_tensor(
            x2_fn(oc), tmpa[:, :], tmpb[:, :], OP.add))(),
          reads=[buf("at"), buf("x1t")], writes=[x2_bufs[oc]])
        g("dve", (lambda oc=oc: lambda e: e.tensor_copy(
            x2b_fn(oc), x2_fn(oc)))(),
          reads=[x2_bufs[oc]], writes=[x2b_bufs[oc]])
    sched.barrier()

    if LIM < 16:
        return _finish(locals())
    sched.phase = "LN3"
    gst3 = ln_stats(x2b_fn, x2b_bufs, "L3")
    xhat3_fn = lambda dc: bft(XRES + 8192 + dc * 1024, 2048)
    xhat3_b = [buf("xh3_%d" % dc) for dc in range(NCH)]
    ln_apply(x2_fn, x2_bufs, xhat3_fn, xhat3_b, "L3", gst3)
    sched.barrier()
    sched.phase = "FFN3"
    ffn_mm1(xhat3_fn, xhat3_b, "F3", False)
    ffn_mm2(x2_fn, x2_bufs, None, None, out_d, "F3", False)
    sched.barrier()

    return _finish(locals())


def _finish(L):
    nc = L["nc"]; sched = L["sched"]; sems = L["sems"]; dslot = L["dslot"]
    ctx = L["ctx"]; ENGINES_ = ENGINES
    if L["LIM"] < 16:
        f32 = L["f32"]; XRES = L["XRES"]; out_d = L["out_d"]
        dmagf = L["dmag"]
        dmagf("pool", lambda e: [e.dma_start(
            out_d[c * 128:(c + 1) * 128, :], f32(XRES + c * T, T))
            for c in range(NCH)], NCH)
        dbg_d = L["dbg_d"]
        import os as _os
        _skip_dbg = _os.environ.get("KDBG", "1") == "0"
        dumps = []
        col = [0]
        def dump(ap, w):
            c0 = col[0]
            dumps.append((ap, c0, w))
            col[0] += w
        dump(L["Lfmb"][:, :].bitcast(F32), 256)
        dump(L["Lnpair"][:, :], 8)
        dump(L["knh"][:, :], 256)
        dump(L["asb"], 1056)
        dump(L["ybsb"][:, :].bitcast(F32), 260)
        dump(L["elcol"][:, :], 16)
        dump(L["lncol"][:, :], 16)
        dump(L["AR"][:, L["WSTG"]:L["WSTG"] + 1024], 1024)
        dump(L["AR"][:, L["WBLK"]:L["WBLK"] + 2048], 2048)
        dump(L["Esb"][:, :].bitcast(F32), 512)
        dump(L["sm"][:, :], 256)
        def dofl(e):
            return [e.dma_start(dbg_d[0:a.shape[0], c0:c0 + w], a)
                    for (a, c0, w) in dumps]
        if not _skip_dbg:
            dmagf("pool", dofl, len(dumps))
    for grp in sched.groups:
        for d in grp.deps:
            if d.dma or d.eng != grp.eng:
                d.need_inc = True
    NSLOT_ = 10
    for e in ENGINES:
        cum = 0
        slot_tot = [0] * NSLOT_
        didx = 0
        for grp in sched.by_eng[e]:
            if grp.dma:
                s = didx % NSLOT_
                grp.slot = s
                grp.slot_prev = slot_tot[s]
                slot_tot[s] += 16 * grp.ndma
                grp.cum = slot_tot[s]
                didx += 1
            else:
                if grp.need_inc:
                    cum += 1
                grp.cum = cum

    with nc.Block() as block:
        def mkrun(eng_name):
            def body(eng):
                waited = {}
                for grp in sched.by_eng[eng_name]:
                    for d in sorted(grp.deps, key=lambda gg: gg.gid):
                        if d.dma:
                            sem, tgt = dslot[d.eng][d.slot], d.cum
                        elif d.eng != eng_name:
                            sem, tgt = sems[d.eng], d.cum
                        else:
                            continue
                        key = id(sem)
                        if waited.get(key, -1) < tgt:
                            eng.wait_ge(sem, tgt)
                            waited[key] = tgt
                    if grp.dma:
                        ssem = dslot[eng_name][grp.slot]
                        if grp.slot_prev > 0 and waited.get(id(ssem), -1) < grp.slot_prev:
                            eng.wait_ge(ssem, grp.slot_prev)
                            waited[id(ssem)] = grp.slot_prev
                        with nc.named_scope(grp.phase):
                            for ins in grp.emit(eng):
                                ins.then_inc(ssem, 16)
                    else:
                        with nc.named_scope(grp.phase):
                            last = grp.emit(eng)
                        if grp.need_inc:
                            last.then_inc(sems[eng_name], 1)
                if eng_name == "pool":
                    for i, s in enumerate(dslot["pool"]):
                        tot = 0
                        for x in sched.by_eng["pool"]:
                            if x.dma and x.slot == i:
                                tot = x.cum
                        if tot > 0:
                            eng.wait_ge(s, tot)
            return body
        block.tensor(mkrun("pe"))
        block.vector(mkrun("dve"))
        block.scalar(mkrun("act"))
        block.gpsimd(mkrun("pool"))
        block.sync(mkrun("sp"))
    ctx.close()
    from concourse.library_overlay import lower_extended_insts
    lower_extended_insts(nc)
    return nc


_NC_CACHE = None
_LAST_RES = None


def kernel(**inputs):
    global _NC_CACHE
    x = np.ascontiguousarray(np.asarray(inputs["L0_attn_query"], np.float32))
    Wq = np.asarray(inputs["L0_attn_Wq"], np.float32)
    Wk = np.asarray(inputs["L0_attn_Wk"], np.float32)
    Wv = np.asarray(inputs["L0_attn_Wv"], np.float32)
    Wo = np.asarray(inputs["L0_attn_Wo"], np.float32)
    bq = np.asarray(inputs["L0_attn_bq"], np.float32)
    bk = np.asarray(inputs["L0_attn_bk"], np.float32)
    bv = np.asarray(inputs["L0_attn_bv"], np.float32)
    bo = np.asarray(inputs["L0_attn_bo"], np.float32)
    temp = np.asarray(inputs["L0_attn_temperature"], np.float32).reshape(1, 1)
    lidx = np.asarray(inputs["L0_attn_landmark_idx"], np.int32).reshape(1, NL)
    w1 = np.asarray(inputs["L0_ffn_w1"], np.float32)
    b1 = np.asarray(inputs["L0_ffn_b1"], np.float32)
    w2 = np.asarray(inputs["L0_ffn_w2"], np.float32)
    b2 = np.asarray(inputs["L0_ffn_b2"], np.float32)

    def colmajor(b):
        return np.ascontiguousarray(b.reshape(-1, 128).T)

    base = {
        "w1": np.ascontiguousarray(w1),
        "w2": np.ascontiguousarray(w2),
        "wqt": np.ascontiguousarray(Wq.T),
        "wkt": np.ascontiguousarray(Wk.T),
        "wvt": np.ascontiguousarray(Wv.T),
        "wot": np.ascontiguousarray(Wo.T),
        "b1": colmajor(b1),
        "b2": colmajor(b2),
        "bq": colmajor(bq),
        "bk": colmajor(bk),
        "bv": colmajor(bv),
        "bo": colmajor(bo),
        "bkrow": np.ascontiguousarray(bk.reshape(1, -1)),
        "bvrow": np.ascontiguousarray(bv.reshape(1, -1)),
        "lidx": lidx,
        "temp": temp,
        "hind": np.float32([[1.0] * 64 + [0.0] * 64,
                            [0.0] * 64 + [1.0] * 64]),
    }
    if _NC_CACHE is None:
        _NC_CACHE = build_nc()
    nc = _NC_CACHE
    in_maps = []
    for c in range(8):
        b, h = c // 2, c % 2
        m = dict(base)
        m["x"] = np.ascontiguousarray(x[b, h * T:(h + 1) * T, :].T)
        m["coff"] = np.float32([[h * T]])
        in_maps.append(m)
    res = run_bass_kernel_spmd(nc, in_maps, core_ids=list(range(8)))
    global _LAST_RES
    _LAST_RES = res
    out = np.empty((4, 2 * T, D), np.float32)
    for c in range(8):
        b, h = c // 2, c % 2
        out[b, h * T:(h + 1) * T, :] = np.asarray(res.results[c]["out"]).T
    return out


# version 16 (final)
# speedup vs baseline: 1.0281x; 1.0033x over previous
import math
import numpy as np

import concourse.bass as bass
import concourse.mybir as mybir
from concourse.bass_utils import run_bass_kernel_spmd

F32 = mybir.dt.float32
BF16 = mybir.dt.bfloat16
I32 = mybir.dt.int32
AF = mybir.ActivationFunctionType
OP = mybir.AluOpType

D, FF, T = 1024, 4096, 2048
NCH, NF, NTC, NTB = 8, 32, 16, 4
NH, DH, NL = 16, 64, 64
MAGIC = float(np.float32(3 * 2 ** 14))
EPS_LN = 1e-5
REG = 1e-6
ENGINES = ("pe", "dve", "act", "pool", "sp")


def _build_pwl():
    X = None
    try:
        import jax
        import jax.numpy as jnp
        with jax.default_device(jax.devices("cpu")[0]):
            Xj = jnp.linspace(-4.0, 4.0, 33, dtype=jnp.float32)
            Yj = 0.5 * Xj * (1.0 + jax.lax.erf(Xj / jnp.sqrt(jnp.float32(2.0))))
            X = np.asarray(Xj, np.float32)
            Y = np.asarray(Yj, np.float32)
    except Exception:
        X = None
    if X is None:
        X = np.linspace(-4.0, 4.0, 33).astype(np.float32)
        Y = np.float32([0.5 * float(v) * (1.0 + math.erf(float(v) / math.sqrt(2.0)))
                        for v in X])
    X64, Y64 = X.astype(np.float64), Y.astype(np.float64)
    s = (Y64[1:] - Y64[:-1]) / (X64[1:] - X64[:-1])
    cp = np.empty(33, np.float64)
    cp[0] = s[0]
    cp[1:32] = s[1:] - s[:-1]
    cp[32] = 1.0 - s[31]
    XK = np.concatenate([X64[:32], [4.0]])
    kconst = float(np.float32(Y64[0] - np.sum(cp * XK)))
    terms = [(float(np.float32(cp[j])), float(np.float32(cp[j] * XK[j])),
              bool(cp[j] > 0)) for j in range(33)]
    knots = [float(np.float32(XK[j])) for j in range(33)]
    return terms, kconst, knots


PWL_TERMS, PWL_K, PWL_KNOTS = _build_pwl()

M4 = 196608.0


def _ref_t2i(in0, in1, s0, s1, imm2):
    return np.maximum(in0, s0) * s1 + np.maximum(in0, s0 + 1.0) * imm2


def _ref_t2a(in0, in1, s0, s1, imm2):
    return (in1 + np.maximum(in0, s0) * s1) + np.maximum(in0, s0 + 1.0) * imm2


def _ref_t1ak(in0, in1, s0, s1, imm2):
    return (in1 + np.maximum(in0, s0) * s1) + imm2


def _install_pwl_ops():
    from concourse import dve_ops as _dvo
    from concourse.dve_op import DveOp
    from concourse.dve_spec import Spec, Src0, Src1, C0, C1, C2, One, maxx
    from concourse.dve_spec import lower as _lower
    if any(op.name == "PWL_T2A" for op in _dvo.OPS):
        return {op.name: op for op in _dvo.OPS
                if op.name in ("PWL_T2I", "PWL_T2A", "PWL_T1AK")}
    defs = [
        ("PWL_T2I",
         Spec(body=maxx(Src0, C0) * C1 + maxx(Src0, C0 + One) * C2),
         _ref_t2i, "append"),
        ("PWL_T2A",
         Spec(body=(Src1 + maxx(Src0, C0) * C1)
              + maxx(Src0, C0 + One) * C2),
         _ref_t2a, "append"),
        ("PWL_T1AK",
         Spec(body=(Src1 + maxx(Src0, C0) * C1) + C2),
         _ref_t1ak, 0),
    ]
    out = {}
    for name, spec, ref, slot in defs:
        sha = {"v3": _lower(spec, ver="v3", perf_en=False, name=name).sha("v3")}
        op = DveOp(name, spec, subdim=False, uops_sha=sha, reference=ref)
        if slot == "append":
            _dvo.OPS.append(op)
            row = _dvo._CUSTOM_DVE_ROW_BASE + len(_dvo.OPS) - 1
        else:
            old = _dvo.OPS[slot]
            _dvo.OPS[slot] = op
            _dvo.CUSTOM_DVE_SPECS.pop(old.name, None)
            row = _dvo._CUSTOM_DVE_ROW_BASE + slot
        assert row < 0x20
        _dvo.CUSTOM_DVE_SPECS[name] = spec
        _dvo.CUSTOM_DVE_REFS[name] = ref
        _dvo._SUB_OPCODE_FOR_NAME[name] = row
        out[name] = op
    return out


_PWL_OPS = _install_pwl_ops()

_PWL4 = [(float(np.float32(4.0) * np.float32(PWL_KNOTS[_j])),
          float(np.float32(PWL_TERMS[_j][0]) / 4.0)) for _j in range(33)]
for _j in range(32):
    assert _PWL4[_j + 1][0] == _PWL4[_j][0] + 1.0, (_j, _PWL4[_j], _PWL4[_j + 1])


class Buf:
    __slots__ = ("w", "r", "name")

    def __init__(self, name=""):
        self.w, self.r, self.name = None, [], name


class Group:
    __slots__ = ("eng", "emit", "deps", "dma", "need_inc", "cum", "gid", "ndma",
                 "slot", "slot_prev", "phase")

    def __init__(self, eng, emit, deps, dma, ndma):
        self.eng, self.emit, self.deps = eng, emit, deps
        self.dma, self.ndma = dma, ndma
        self.need_inc, self.cum, self.gid = False, 0, -1


class Sched:
    def __init__(self):
        self.groups = []
        self.by_eng = {e: [] for e in ENGINES}
        self._bar = {}
        self.phase = "init"

    def add(self, eng, emit, reads=(), writes=(), deps=(), dma=False, ndma=0):
        ds = set()
        for b in reads:
            if b.w is not None:
                ds.add(b.w)
        for b in writes:
            if b.w is not None:
                ds.add(b.w)
            ds.update(b.r)
        for gg in deps:
            if gg is not None:
                ds.add(gg)
        ds.update(self._bar.values())
        grp = Group(eng, emit, ds, dma, ndma)
        grp.phase = self.phase
        grp.gid = len(self.groups)
        self.groups.append(grp)
        self.by_eng[eng].append(grp)
        for b in reads:
            b.r.append(grp)
        for b in writes:
            b.w, b.r = grp, []
        return grp

    def barrier(self):
        self._bar = {e: self.by_eng[e][-1] for e in ENGINES if self.by_eng[e]}


def build_nc():
    import os
    LIM = int(os.environ.get("KLIM", "99"))
    SUB = int(os.environ.get("KSUB", "9"))
    nc = bass.Bass("TRN2", target_bir_lowering=False,
                   detect_race_conditions=False)
    sched = Sched()
    B = {}

    def buf(name):
        if name not in B:
            B[name] = Buf(name)
        return B[name]

    x_in = nc.dram_tensor("x", [D, T], F32, kind="ExternalInput")
    w1_d = nc.dram_tensor("w1", [D, FF], F32, kind="ExternalInput")
    w2_d = nc.dram_tensor("w2", [FF, D], F32, kind="ExternalInput")
    wq_d = nc.dram_tensor("wqt", [D, D], F32, kind="ExternalInput")
    wk_d = nc.dram_tensor("wkt", [D, D], F32, kind="ExternalInput")
    wv_d = nc.dram_tensor("wvt", [D, D], F32, kind="ExternalInput")
    wo_d = nc.dram_tensor("wot", [D, D], F32, kind="ExternalInput")
    b1_d = nc.dram_tensor("b1", [128, NF], F32, kind="ExternalInput")
    b2_d = nc.dram_tensor("b2", [128, NCH], F32, kind="ExternalInput")
    bq_d = nc.dram_tensor("bq", [128, NCH], F32, kind="ExternalInput")
    bk_d = nc.dram_tensor("bk", [128, NCH], F32, kind="ExternalInput")
    bv_d = nc.dram_tensor("bv", [128, NCH], F32, kind="ExternalInput")
    bo_d = nc.dram_tensor("bo", [128, NCH], F32, kind="ExternalInput")
    bkrow_d = nc.dram_tensor("bkrow", [1, D], F32, kind="ExternalInput")
    bvrow_d = nc.dram_tensor("bvrow", [1, D], F32, kind="ExternalInput")
    lidx_d = nc.dram_tensor("lidx", [1, NL], I32, kind="ExternalInput")
    hind_d = nc.dram_tensor("hind", [2, 128], F32, kind="ExternalInput")
    temp_d = nc.dram_tensor("temp", [1, 1], F32, kind="ExternalInput")
    coff_d = nc.dram_tensor("coff", [1, 1], F32, kind="ExternalInput")
    out_d = nc.dram_tensor("out", [D, T], F32, kind="ExternalOutput")
    dbg_d = nc.dram_tensor("dbg", [128, 8192], F32, kind="ExternalOutput")

    x1_d = nc.dram_tensor("x1scr", [D, T], F32)
    yq_d = nc.dram_tensor("yqscr", [FF, T], BF16)
    w1b_d = nc.dram_tensor("w1bscr", [128, NCH * FF], BF16)
    w2b_d = nc.dram_tensor("w2bscr", [128, NCH * FF], BF16)
    cc1_in = nc.dram_tensor("cc1in", [NL, D], F32)
    cc1_out = nc.dram_tensor("cc1out", [NL, D], F32)
    cc2_in = nc.dram_tensor("cc2in", [NL, NH * 65], F32)
    cc2_out = nc.dram_tensor("cc2out", [NL, NH * 65], F32)
    RG = [[0, 1], [2, 3], [4, 5], [6, 7]]

    import contextlib
    ctx = contextlib.ExitStack()

    def sbt(name, shape, dt):
        return ctx.enter_context(nc.sbuf_tensor(name, shape, dt))

    AR = sbt("AR", [128, 49152], F32)
    ARB = AR[:, :].bitcast(BF16)

    b1sb = sbt("b1sb", [128, NF], F32)
    b1q4 = sbt("b1q4", [128, NF], F32)
    b2sb = sbt("b2sb", [128, NCH], F32)
    bqsb = sbt("bqsb", [128, NCH], F32)
    bksb = sbt("bksb", [128, NCH], F32)
    bosb = sbt("bosb", [128, NCH], F32)
    sm = sbt("sm", [128, 256], F32)
    ident = sbt("ident", [128, 128], F32)
    identb = sbt("identb", [128, 128], BF16)
    identdual = sbt("identdual", [128, 64], F32)
    iepsdual = sbt("iepsdual", [128, 64], F32)
    halfind = sbt("halfind", [2, 128], F32)
    onesb = sbt("onesb", [128, 4], BF16)
    onesrb = sbt("onesrb", [1, 128], BF16)
    Esb = sbt("Esb", [128, NTC * NL], BF16)
    idxsb = sbt("idxsb", [1, NL], I32)
    Lfmb = sbt("Lfmb", [128, 8 * DH], BF16)
    Lfmze = sbt("Lfmze", [128, 8 * DH], BF16)
    Lfmzo = sbt("Lfmzo", [128, 8 * DH], BF16)
    ones2 = sbt("ones2", [128, 64], BF16)
    Lnpair = sbt("Lnpair", [128, 8], F32)
    elcol = sbt("elcol", [64, NH], F32)
    lncol = sbt("lncol", [64, NH], F32)
    lnrows = sbt("lnrows", [16, 64], F32)
    lnpr2 = sbt("lnpr2", [2, 128], F32)
    knh = sbt("knh", [128, NTC * NH], F32)
    ybsb = sbt("ybsb", [128, 8 * 65], BF16)
    trsc = sbt("trsc", [64, 64], BF16)
    indsb = sbt("indsb", [128, 2], F32)

    psum = ctx.enter_context(nc.psum_tensor("ps", [128, 4096], F32))
    sems = {e: ctx.enter_context(nc.semaphore("s_" + e)) for e in ENGINES}
    NSLOT = 10
    dslot = {q: [ctx.enter_context(nc.semaphore("d_%s%d" % (q, i)))
                 for i in range(NSLOT)] for q in ("sp", "pool")}
    ccsem1 = ctx.enter_context(nc.semaphore("cc1s"))
    ccsem2 = ctx.enter_context(nc.semaphore("cc2s"))

    XRES, XHAT = 0, 16384
    TMPA, TMPB, TMPC = 24576, 26624, 28672
    ROWS = 30720
    WBLK, YQIN = 34816, 38912
    WSTG = 43008
    BFTA, BFTC = 47104, 48128

    def f32(off, n, p0=0, p1=128):
        return AR[p0:p1, off:off + n]

    def bft(off_f32, nb, p0=0, p1=128):
        return ARB[p0:p1, 2 * off_f32:2 * off_f32 + nb]

    tmpa = f32(TMPA, 2048)
    tmpb = f32(TMPB, 2048)
    tmpc = f32(TMPC, 2048)
    bfta = bft(BFTA, 2048)
    bftc = bft(BFTC, 2048)

    iota_c = sm[:, 8:9]
    rcol = sm[:, 1:2]
    negr = sm[:, 2:3]
    rcol2 = sm[:, 3:4]
    coffcol = sm[:, 5:6]
    onesrow = sm[0:1, 128:256]

    nc.const_aps.register(F32, float(EPS_LN), sm[:, 11:12])
    nc.const_aps.register(F32, 0.0, sm[:, 12:13])
    nc.const_aps.register(F32, 1.0, sm[:, 13:14])
    nc.const_aps.register(F32, M4, sm[:, 6:7])
    nc.const_aps.register(F32, -M4, sm[:, 7:8])
    nc.const_aps.register(F32, MAGIC, sm[:, 14:15])
    nc.const_aps.register(F32, -MAGIC, sm[:, 15:16])

    bank = [buf("pbank%d" % i) for i in range(8)]

    def g(eng, emit, reads=(), writes=(), deps=()):
        return sched.add(eng, emit, reads, writes, deps)

    def dmag(eng, emit, n, reads=(), writes=(), deps=()):
        return sched.add(eng, emit, reads, writes, deps, dma=True, ndma=n)

    xres_b = [buf("xres%d" % c) for c in range(NCH)]
    xhat_b = [buf("xhat%d" % c) for c in range(NCH)]

    dmag("sp", lambda e: [e.dma_start(f32(XRES + c * T, T),
                                      x_in[c * 128:(c + 1) * 128, :])
                          for c in range(NCH)], NCH, writes=xres_b)
    dmag("sp", lambda e: [e.dma_start(b1sb[:, :], b1_d[:, :]),
                          e.dma_start(b2sb[:, :], b2_d[:, :]),
                          e.dma_start(bqsb[:, :], bq_d[:, :]),
                          e.dma_start(bksb[:, :], bk_d[:, :]),
                          e.dma_start(bosb[:, :], bo_d[:, :]),
                          e.dma_start(idxsb[:, :], lidx_d[:, :]),
                          e.dma_start(sm[0:1, 0:1], temp_d[:, :]),
                          e.dma_start(sm[0:1, 4:5], coff_d[:, :]),
                          e.dma_start(halfind[:, :], hind_d[:, :])],
         9, writes=[buf("consts"), buf("hind")])
    g("dve", lambda e: e.tensor_scalar(b1q4[:, :], b1sb[:, :], 4.0, None,
                                       OP.mult),
      reads=[buf("consts")], writes=[buf("b1q4")])

    def ph_iota(e):
        e.iota(iota_c, [[0, 1]], channel_multiplier=1,
               allow_small_or_imprecise_dtypes=True)
        return e.iota(ident[:, :], [[1, 128]], channel_multiplier=0,
                      allow_small_or_imprecise_dtypes=True)
    g("pool", ph_iota, writes=[buf("iota"), buf("identraw")])

    def ph_ones(e):
        e.memset(sm[:, 11:12], EPS_LN)
        e.memset(sm[:, 12:13], 0.0)
        e.memset(sm[:, 13:14], 1.0)
        e.memset(sm[:, 6:7], M4)
        e.memset(sm[:, 7:8], -M4)
        e.memset(sm[:, 14:15], MAGIC)
        e.memset(sm[:, 15:16], -MAGIC)
        e.memset(onesrow, 1.0)
        e.memset(onesb[:, :], 1.0)
        e.memset(ones2[:, :], 1.0)
        e.memset(Lfmze[:, :], 0.0)
        e.memset(Lfmzo[:, :], 0.0)
        return e.memset(onesrb[:, :], 1.0)
    g("dve", ph_ones, writes=[buf("ones")])

    def ph_ident(e):
        e.tensor_scalar(ident[:, :], ident[:, :], iota_c, None, OP.is_equal)
        e.tensor_scalar(indsb[:, 0:1], iota_c, 64.0, None, OP.is_lt)
        e.tensor_copy(identb[:, :], ident[:, :])
        e.tensor_scalar(indsb[:, 1:2], iota_c, 63.5, None, OP.is_gt)
        e.tensor_copy(identdual[0:64, :], ident[0:64, 0:64])
        return e.tensor_scalar(iepsdual[0:64, :], ident[0:64, 0:64], REG, None,
                               OP.mult)
    g("dve", ph_ident, reads=[buf("iota"), buf("identraw")],
      writes=[buf("ident"), buf("idual_t"), buf("ind")])
    dmag("pool", lambda e: [e.dma_start(identdual[64:128, :], identdual[0:64, :]),
                            e.dma_start(iepsdual[64:128, :], iepsdual[0:64, :])],
         2, reads=[buf("idual_t")], writes=[buf("idual")])

    g("pe", lambda e: e.matmul(psum[:, 0:1], onesrow, sm[0:1, 0:1],
                               start=True, stop=True),
      reads=[buf("consts"), buf("ones")], writes=[bank[0]])

    def ph_tder(e):
        e.reciprocal(rcol, psum[:, 0:1])
        e.tensor_scalar(sm[:, 10:11], psum[:, 0:1], -1.0, None, OP.mult)
        e.tensor_copy(sm[:, 0:1], psum[:, 0:1])
        e.tensor_scalar(rcol2, rcol, 2.0, None, OP.mult)
        return e.reciprocal(negr, sm[:, 10:11])
    gtd = g("dve", ph_tder, reads=[bank[0]], writes=[buf("temps")])

    g("pe", lambda e: e.matmul(psum[:, 2:3], onesrow, sm[0:1, 4:5],
                               start=True, stop=True),
      reads=[buf("consts"), buf("ones")], writes=[bank[0]], deps=[gtd])
    g("act", lambda e: e.activation(coffcol, psum[:, 2:3], AF.Identity,
                                    bias=0.0, scale=1.0),
      reads=[bank[0]], writes=[buf("coffcol")])

    def ph_idx(e):
        e.tensor_copy(sm[0:1, 16:16 + NL], idxsb[0:1, :])
        e.memset(sm[:, 9:10], 0.0)
        return e.tensor_scalar(sm[0:1, 16:16 + NL], sm[0:1, 16:16 + NL],
                               0.0, float(2 * T - 1), OP.max, OP.min)
    g("dve", ph_idx, reads=[buf("consts")], writes=[buf("idxrow")])
    g("pe", lambda e: e.matmul(psum[:, 512:512 + NL], onesrow,
                               sm[0:1, 16:16 + NL], start=True, stop=True),
      reads=[buf("idxrow"), buf("ones")], writes=[bank[1]])

    def ph_E(e):
        last = None
        for tc in range(NTC):
            e.tensor_scalar(tmpa[:, tc * NL:(tc + 1) * NL],
                            psum[:, 512:512 + NL],
                            float(-128 * tc), coffcol, OP.add, OP.subtract)
        for tc in range(NTC):
            last = e.tensor_scalar(Esb[:, tc * NL:(tc + 1) * NL],
                                   tmpa[:, tc * NL:(tc + 1) * NL], iota_c,
                                   None, OP.is_equal)
        return last
    g("dve", ph_E, reads=[bank[1], buf("iota"), buf("coffcol")],
      writes=[buf("E")])
    sched.barrier()

    def ln_stats(xb_fn, xbufs, tag):
        sqsl = lambda dc: (bfta if dc < 4 else bftc)[:, (dc % 4) * 512:
                                                     (dc % 4) * 512 + 512]
        for nb in range(NTB):
            g("pe", (lambda nb=nb: lambda e: [e.matmul(
                psum[0:1, 3072:3584], onesb[:, 0:1],
                xb_fn(dc)[:, nb * 512:nb * 512 + 512],
                start=(dc == 0), stop=(dc == NCH - 1))
                for dc in range(NCH)][-1])(),
              reads=list(xbufs) + [buf("ones")], writes=[bank[6]])
            g("dve", (lambda nb=nb: lambda e: e.tensor_copy(
                f32(ROWS + nb * 512, 512, 0, 1), psum[0:1, 3072:3584]))(),
              reads=[bank[6]], writes=[buf("sx" + tag)])
            gsq = g("dve", (lambda nb=nb: lambda e: [e.tensor_tensor(
                sqsl(dc),
                xb_fn(dc)[:, nb * 512:nb * 512 + 512],
                xb_fn(dc)[:, nb * 512:nb * 512 + 512], OP.mult)
                for dc in range(NCH)][-1])(),
                reads=list(xbufs), writes=[buf("sq" + tag)])
            g("pe", (lambda nb=nb: lambda e: [e.matmul(
                psum[0:1, 3584:4096], onesb[:, 0:1], sqsl(dc),
                start=(dc == 0), stop=(dc == NCH - 1))
                for dc in range(NCH)][-1])(),
              reads=[buf("sq" + tag), buf("ones")], writes=[bank[7]],
              deps=[gsq])
            g("dve", (lambda nb=nb: lambda e: e.tensor_copy(
                f32(ROWS + 2048 + nb * 512, 512, 0, 1), psum[0:1, 3584:4096]))(),
              reads=[bank[7]], writes=[buf("sx2" + tag)])

        def rowmath(e):
            mean = f32(ROWS, T, 0, 1)
            ex2 = f32(ROWS + 2048, T, 0, 1)
            e.tensor_scalar(mean, mean, 1.0 / D, None, OP.mult)
            e.tensor_scalar(ex2, ex2, 1.0 / D, None, OP.mult)
            e.tensor_tensor(tmpa[0:1, :], mean, mean, OP.mult)
            return e.tensor_tensor(ex2, ex2, tmpa[0:1, :], OP.subtract)
        g("dve", rowmath, reads=[buf("sx" + tag), buf("sx2" + tag)],
          writes=[buf("var" + tag), buf("sx" + tag)])
        g("act", lambda e: e.activation(f32(ROWS + 2048, T, 0, 1),
                                        f32(ROWS + 2048, T, 0, 1), AF.Sqrt,
                                        bias=EPS_LN, scale=1.0),
          reads=[buf("var" + tag)], writes=[buf("sd" + tag)])

        def rowmath2(e):
            mean = f32(ROWS, T, 0, 1)
            sd = f32(ROWS + 2048, T, 0, 1)
            e.reciprocal(tmpa[0:1, :], sd)
            e.tensor_tensor(tmpb[0:1, :], mean, tmpa[0:1, :], OP.mult)
            e.tensor_copy(mean, tmpa[0:1, :])
            return e.tensor_copy(sd, tmpb[0:1, :])
        return g("dve", rowmath2, reads=[buf("sd" + tag), buf("sx" + tag)],
                 writes=[buf("rows" + tag)])

    def ln_apply(src_fn, srcbufs, dst_fn, dstbufs, tag, gstat):
        for nb in range(NTB):
            gb1 = g("pe", (lambda nb=nb: lambda e: e.matmul(
                psum[:, 2048:2560], onesrow, f32(ROWS + nb * 512, 512, 0, 1),
                start=True, stop=True))(),
                reads=[buf("rows" + tag), buf("ones")], writes=[bank[4]],
                deps=[gstat])
            gb2 = g("pe", (lambda nb=nb: lambda e: e.matmul(
                psum[:, 2560:3072], onesrow,
                f32(ROWS + 2048 + nb * 512, 512, 0, 1),
                start=True, stop=True))(),
                reads=[buf("rows" + tag), buf("ones")], writes=[bank[5]],
                deps=[gstat])
            for dc in range(NCH):
                gm = g("dve", (lambda nb=nb, dc=dc: lambda e: e.tensor_tensor(
                    tmpa[:, nb * 512:nb * 512 + 512],
                    src_fn(dc)[:, nb * 512:nb * 512 + 512],
                    psum[:, 2048:2560], OP.mult))(),
                    reads=[srcbufs[dc], bank[4]], writes=[buf("lnt" + tag)],
                    deps=[gb1])
                g("dve", (lambda nb=nb, dc=dc: lambda e: e.tensor_tensor(
                    dst_fn(dc)[:, nb * 512:nb * 512 + 512],
                    tmpa[:, nb * 512:nb * 512 + 512],
                    psum[:, 2560:3072], OP.subtract))(),
                  reads=[buf("lnt" + tag), bank[5]], writes=[dstbufs[dc]],
                  deps=[gb2, gm])

    def ffn_mm1(xhat_fn, xhatbufs, tag, first):
        wb = buf("wblk" + tag)
        for fb in range(4):
            for kc in range(NCH):
                st = buf("wstg%d" % (kc % 2))
                if first:
                    gl = dmag("sp", (lambda kc=kc, fb=fb: lambda e: [e.dma_start(
                        f32(WSTG + (kc % 2) * 2048, 1024),
                        w1_d[kc * 128:(kc + 1) * 128,
                             fb * 1024:(fb + 1) * 1024])])(), 1, writes=[st])
                    gc = g("dve", (lambda kc=kc: lambda e: e.tensor_copy(
                        bft(WBLK, 8192)[:, kc * 1024:(kc + 1) * 1024],
                        f32(WSTG + (kc % 2) * 2048, 1024)))(),
                        reads=[st], writes=[wb], deps=[gl])
                    dmag("pool", (lambda kc=kc, fb=fb: lambda e: [e.dma_start(
                        w1b_d[:, kc * FF + fb * 1024:kc * FF + (fb + 1) * 1024],
                        bft(WBLK, 8192)[:, kc * 1024:(kc + 1) * 1024])])(),
                        1, reads=[wb], deps=[gc])
                else:
                    dmag("sp", (lambda kc=kc, fb=fb: lambda e: [e.dma_start(
                        bft(WBLK, 8192)[:, kc * 1024:(kc + 1) * 1024],
                        w1b_d[:, kc * FF + fb * 1024:kc * FF + (fb + 1) * 1024])])(),
                        1, writes=[wb])
            for fci in range(8):
                fc = fb * 8 + fci
                pb = (fc % 2) * 4
                pbufs = [bank[pb + i] for i in range(4)]

                def mm1(e, fci=fci, pb=pb):
                    last = None
                    for kc in range(NCH):
                        for nb in range(NTB):
                            last = e.matmul(
                                psum[:, (pb + nb) * 512:(pb + nb) * 512 + 512],
                                bft(WBLK, 8192)[:, kc * 1024 + fci * 128:
                                                kc * 1024 + fci * 128 + 128],
                                xhat_fn(kc)[:, nb * 512:nb * 512 + 512],
                                start=(kc == 0), stop=(kc == NCH - 1))
                    return last
                g("pe", mm1, reads=[wb] + list(xhatbufs), writes=pbufs)
                g("act", (lambda fc=fc, pb=pb: lambda e: e.activation(
                    tmpa[:, :], psum[:, pb * 512:pb * 512 + 2048], AF.Identity,
                    bias=b1q4[:, fc:fc + 1], scale=4.0))(),
                  reads=pbufs + [buf("b1q4")], writes=[buf("h" + tag)])
                g("dve", lambda e: e.tensor_scalar(
                    tmpb[:, :], tmpa[:, :], M4, M4, OP.add, OP.subtract),
                  reads=[buf("h" + tag)], writes=[buf("hq" + tag)])

                def pwl(e):
                    T2I, T2A, T1AK = (_PWL_OPS["PWL_T2I"], _PWL_OPS["PWL_T2A"],
                                      _PWL_OPS["PWL_T1AK"])
                    e._custom_dve(T2I, out=tmpc[:, :], in0=tmpb[:, :],
                                  s0=_PWL4[0][0], s1=_PWL4[0][1],
                                  imm2=_PWL4[1][1])
                    for j in range(2, 32, 2):
                        e._custom_dve(T2A, out=tmpc[:, :], in0=tmpb[:, :],
                                      in1=tmpc[:, :], s0=_PWL4[j][0],
                                      s1=_PWL4[j][1], imm2=_PWL4[j + 1][1])
                    e._custom_dve(T1AK, out=tmpc[:, :], in0=tmpb[:, :],
                                  in1=tmpc[:, :], s0=_PWL4[32][0],
                                  s1=_PWL4[32][1], imm2=PWL_K)
                    return e.tensor_scalar(bfta[:, :], tmpc[:, :], MAGIC, MAGIC,
                                           OP.add, OP.subtract)
                gp = g("dve", pwl, reads=[buf("hq" + tag)],
                       writes=[buf("yqo" + tag), buf("h" + tag),
                               buf("sq" + tag)])
                dmag("pool", (lambda fc=fc: lambda e: [e.dma_start(
                    yq_d[fc * 128:(fc + 1) * 128, :], bfta[:, :])])(), 1,
                    reads=[buf("yqo" + tag)],
                    writes=[buf("yqd%d" % fb)], deps=[gp])

    def ffn_mm2(res_fn, resbufs, cast_fn, castbufs, dma_out, tag, first):
        w2stA = bft(WSTG + 1024, 1024)
        w2stB = bft(WSTG + 1536, 1024)
        u2 = [f32(WSTG + 3072, 512), f32(WSTG + 3584, 512)]
        u2b_ = [buf("u2a"), buf("u2b")]
        if first:
            for dc in range(NCH):
                gl = dmag("sp", (lambda dc=dc: lambda e: [e.dma_start(
                    f32(WSTG, 4096).rearrange("p (c n) -> p c n", n=128),
                    w2_d[:, dc * 128:(dc + 1) * 128].rearrange(
                        "(c p) n -> p c n", p=128))])(), 1,
                    writes=[buf("wstgA"), buf("wstg0"), buf("wstg1"),
                            buf("w2q0"), buf("w2q1")] + u2b_)
                gc = g("dve", (lambda dc=dc: lambda e: [
                    e.tensor_copy(bft(YQIN, 2048), f32(WSTG, 2048)),
                    e.tensor_copy(bft(YQIN + 1024, 2048),
                                  f32(WSTG + 2048, 2048))][-1])(),
                    reads=[buf("wstgA"), buf("wstg0"), buf("wstg1"),
                           buf("w2q0"), buf("w2q1"), buf("u2a"), buf("u2b")],
                    writes=[buf("w2cst"), buf("yqq0"), buf("yqq1")],
                    deps=[gl])
                dmag("pool", (lambda dc=dc: lambda e: [
                    e.dma_start(w2b_d[:, dc * 4096:dc * 4096 + 2048],
                                bft(YQIN, 2048)),
                    e.dma_start(w2b_d[:, dc * 4096 + 2048:dc * 4096 + 4096],
                                bft(YQIN + 1024, 2048))])(), 2,
                    reads=[buf("w2cst"), buf("yqq0"), buf("yqq1"),
                           buf("yqguard")],
                    writes=[buf("w2bd%d" % dc)])
        for tb in range(NTB):
            for qtr in range(4):
                par = (tb * 4 + qtr) % 2
                yqb = buf("yqq%d" % par)
                gyq = dmag("sp", (lambda tb=tb, qtr=qtr, par=par:
                                  lambda e: [e.dma_start(
                    bft(YQIN + par * 2048, 4096)[:, k * 512:k * 512 + 512],
                    yq_d[(qtr * 8 + k) * 128:(qtr * 8 + k + 1) * 128,
                         tb * 512:tb * 512 + 512]) for k in range(8)])(),
                    8, reads=[buf("yqd%d" % qtr)],
                    writes=[yqb, buf("yqguard")])
                for dc in range(NCH):
                    wdb = buf("w2q%d" % (dc % 2))
                    wst = w2stA if dc % 2 == 0 else w2stB
                    gw = dmag("sp", (lambda dc=dc, qtr=qtr, wst=wst:
                                     lambda e: [e.dma_start(
                        wst[:, :],
                        w2b_d[:, dc * 4096 + qtr * 1024:
                              dc * 4096 + (qtr + 1) * 1024])])(), 1,
                        reads=[buf("w2bd%d" % dc)], writes=[wdb])

                    def mm2(e, dc=dc, qtr=qtr, par=par, wst=wst):
                        last = None
                        for k in range(8):
                            last = e.matmul(
                                psum[:, dc * 512:dc * 512 + 512],
                                wst[:, k * 128:k * 128 + 128],
                                bft(YQIN + par * 2048, 4096)
                                [:, k * 512:k * 512 + 512],
                                start=(qtr == 0 and k == 0),
                                stop=(qtr == 3 and k == 7))
                        return last
                    g("pe", mm2, reads=[wdb, yqb], writes=[bank[dc]],
                      deps=[gw, gyq])
            for dp in range(0, 8, 2):
                g("act", (lambda dp=dp: lambda e: [
                    e.activation(u2[0][:, :],
                                 psum[:, dp * 512:dp * 512 + 512],
                                 AF.Identity, bias=b2sb[:, dp:dp + 1],
                                 scale=1.0),
                    e.activation(u2[1][:, :],
                                 psum[:, (dp + 1) * 512:(dp + 1) * 512 + 512],
                                 AF.Identity, bias=b2sb[:, dp + 1:dp + 2],
                                 scale=1.0)][-1])(),
                  reads=[bank[dp], bank[dp + 1], buf("consts")],
                  writes=u2b_)
                g("act", lambda e: [
                    e.activation(u2[0][:, :], u2[0][:, :], AF.Identity,
                                 bias=MAGIC, scale=1.0),
                    e.activation(u2[1][:, :], u2[1][:, :], AF.Identity,
                                 bias=MAGIC, scale=1.0),
                    e.activation(u2[0][:, :], u2[0][:, :], AF.Identity,
                                 bias=-MAGIC, scale=1.0),
                    e.activation(u2[1][:, :], u2[1][:, :], AF.Identity,
                                 bias=-MAGIC, scale=1.0)][-1],
                  reads=u2b_, writes=u2b_)
                gy = g("dve", (lambda tb=tb, dp=dp: lambda e: [
                    e.scalar_tensor_tensor(
                        res_fn(dp + i)[:, tb * 512:tb * 512 + 512],
                        u2[i][:, :], 0.5,
                        res_fn(dp + i)[:, tb * 512:tb * 512 + 512],
                        OP.mult, OP.add) for i in range(2)][-1])(),
                    reads=u2b_ + [resbufs[dp], resbufs[dp + 1]],
                    writes=[resbufs[dp], resbufs[dp + 1]])
                if dma_out is not None:
                    dmag("pool", (lambda tb=tb, dp=dp: lambda e: [e.dma_start(
                        dma_out[(dp + i) * 128:(dp + i + 1) * 128,
                                tb * 512:tb * 512 + 512],
                        res_fn(dp + i)[:, tb * 512:tb * 512 + 512])
                        for i in range(2)])(),
                        2, reads=[resbufs[dp], resbufs[dp + 1]], deps=[gy])

    awb = bft(TMPC, 8192)
    bkr = f32(TMPA, D, 0, 1)
    bvr = f32(TMPA + 1024, D, 0, 1)
    ktm_bufs = [buf("ktm%d" % i) for i in range(NTC)]
    kfm_bufs = [buf("kfm%d" % i) for i in range(NCH)]
    vtm_bufs = [buf("vtm%d" % i) for i in range(NTC)]
    qfm_bufs = [buf("qfm%d" % i) for i in range(NCH)]
    Lsb = f32(WSTG, 1024, 0, 64)
    Lbb = bft(WSTG + 1024, 1024, 0, 64)
    lsq = f32(WSTG + 1536, 1024, 0, 64)
    cc2st = f32(WSTG + 2560, NH * 65, 0, 64)
    asb = f32(TMPB, 1056)
    norm16 = f32(WBLK, 2048, 0, 16)
    rnb16 = bft(WBLK + 2048, 2048, 0, 16)

    awk_ap = bft(WBLK, 8192)

    def load_attn_w(dram_h, tagc, wap, extra_w=(), ceng="pool"):
        wbuf = buf("aw" + tagc)
        for kc in range(NCH):
            st = buf("wstg%d" % (kc % 2))
            gl = dmag("sp", (lambda kc=kc: lambda e: [e.dma_start(
                f32(WSTG + (kc % 2) * 2048, 1024),
                dram_h[kc * 128:(kc + 1) * 128, :])])(), 1, writes=[st])
            g(ceng, (lambda kc=kc: lambda e: e.tensor_copy(
                wap[:, kc * 1024:(kc + 1) * 1024],
                f32(WSTG + (kc % 2) * 2048, 1024)))(),
              reads=[st], writes=[wbuf] + list(extra_w), deps=[gl])
        return wbuf

    def proj_tm(dst_off, dstbufs, wbuf, wap, brow_ap, tagp, extra_w=()):
        bias_sb = f32(BFTA, 1024)
        for half in range(2):
            gbb_ = g("pe", (lambda half=half: lambda e: e.matmul(
                psum[:, (4 + half) * 512:(4 + half) * 512 + 512], onesrow,
                brow_ap[0:1, half * 512:half * 512 + 512],
                start=True, stop=True))(),
              reads=[buf("brow"), buf("ones")], writes=[bank[4 + half]])
            g("dve", (lambda half=half: lambda e: e.tensor_copy(
                f32(BFTA + half * 512, 512),
                psum[:, (4 + half) * 512:(4 + half) * 512 + 512]))(),
              reads=[bank[4 + half]], writes=[buf("biasb" + tagp)],
              deps=[gbb_])
        for tm in range(NTC):
            pb = (tm % 2) * 2
            pbufs = [bank[pb], bank[pb + 1]]

            def mm(e, tm=tm, pb=pb):
                last = None
                for kc in range(NCH):
                    for half in range(2):
                        last = e.matmul(
                            psum[:, (pb + half) * 512:(pb + half) * 512 + 512],
                            bft(XHAT + kc * 1024, 2048)[:, tm * 128:tm * 128 + 128],
                            wap[:, kc * 1024 + half * 512:
                                kc * 1024 + half * 512 + 512],
                            start=(kc == 0), stop=(kc == NCH - 1))
                return last
            g("pe", mm, reads=[wbuf] + list(xhat_b), writes=pbufs)
            for half in range(2):
                g("dve", (lambda tm=tm, half=half, pb=pb: lambda e:
                          e.tensor_tensor(
                    bft(dst_off + tm * 512, 1024)[:, half * 512:half * 512 + 512],
                    psum[:, (pb + half) * 512:(pb + half) * 512 + 512],
                    f32(BFTA + half * 512, 512),
                    OP.add))(),
                  reads=[bank[pb + half], buf("biasb" + tagp)],
                  writes=[dstbufs[tm]] + list(extra_w))

    def proj_fm(dst_off, dstbufs, wbuf, wap, bias_sb, tagp, extra_w=()):
        for oc in range(NCH):
            pb = (oc % 2) * 4
            pbufs = [bank[pb + i] for i in range(4)]

            def mm(e, oc=oc, pb=pb):
                last = None
                for kc in range(NCH):
                    for nb in range(NTB):
                        last = e.matmul(
                            psum[:, (pb + nb) * 512:(pb + nb) * 512 + 512],
                            wap[:, kc * 1024 + oc * 128:kc * 1024 + oc * 128 + 128],
                            bft(XHAT + kc * 1024, 2048)[:, nb * 512:nb * 512 + 512],
                            start=(kc == 0), stop=(kc == NCH - 1))
                return last
            g("pe", mm, reads=[wbuf] + list(xhat_b), writes=pbufs)
            g("act", (lambda oc=oc, pb=pb: lambda e: e.activation(
                bft(dst_off + oc * 1024, 2048),
                psum[:, pb * 512:pb * 512 + 2048],
                AF.Identity, bias=bias_sb[:, oc:oc + 1], scale=1.0))(),
              reads=pbufs + [buf("consts")],
              writes=[dstbufs[oc]] + list(extra_w))

    if LIM < 1:
        return _finish(locals())
    sched.phase = "LN1"
    xb_fn = lambda dc: bft(XHAT + dc * 1024, 2048)
    xb1_bufs = [buf("xb1_%d" % dc) for dc in range(NCH)]
    for dc in range(NCH):
        g("dve", (lambda dc=dc: lambda e: e.tensor_copy(
            xb_fn(dc), f32(XRES + dc * T, T)))(),
          reads=[xres_b[dc]], writes=[xb1_bufs[dc]])
    gst1 = ln_stats(xb_fn, xb1_bufs, "L1")
    ln_apply(lambda dc: f32(XRES + dc * T, T), xres_b, xb_fn, xhat_b,
             "L1", gst1)
    sched.barrier()
    if LIM < 2:
        return _finish(locals())
    sched.phase = "FFN1"
    ffn_mm1(xb_fn, xhat_b, "F1", True)
    ffn_mm2(lambda dc: f32(XRES + dc * T, T), xres_b, None, None, x1_d,
            "F1", True)
    wkbuf = load_attn_w(wk_d, "k", awk_ap, extra_w=[buf("wblkF1")])
    sched.barrier()
    if LIM < 4:
        return _finish(locals())
    sched.phase = "LN2"
    xb2_bufs = [buf("xb2_%d" % dc) for dc in range(NCH)]
    for dc in range(NCH):
        g("dve", (lambda dc=dc: lambda e: e.tensor_copy(
            xb_fn(dc), f32(XRES + dc * T, T)))(),
          reads=[xres_b[dc]], writes=[xb2_bufs[dc]])
    gst2 = ln_stats(xb_fn, xb2_bufs, "L2")
    ln_apply(lambda dc: f32(XRES + dc * T, T), xres_b, xb_fn, xhat_b,
             "L2", gst2)
    sched.barrier()

    if LIM < 5:
        return _finish(locals())
    sched.phase = "A1ktm"
    dmag("sp", lambda e: [e.dma_start(bkr, bkrow_d[:, :]),
                          e.dma_start(bvr, bvrow_d[:, :])], 2,
         writes=[buf("brow")])
    wvbuf = load_attn_w(wv_d, "v", awb,
                        extra_w=[buf("yqaccF1"), buf("rowsL2")])
    proj_tm(XRES, ktm_bufs, wkbuf, awk_ap, bkr, "ktm")
    sched.barrier()

    if LIM < 6:
        return _finish(locals())
    sched.phase = "A2Lpart"
    for tc in range(NTC):
        g("pe", (lambda tc=tc: lambda e: [
            e.matmul(psum[0:64, 3072 + half * 512:3072 + half * 512 + 512],
                     Esb[:, tc * NL:(tc + 1) * NL],
                     bft(XRES + tc * 512, 1024)[:, half * 512:half * 512 + 512],
                     start=(tc == 0), stop=(tc == NTC - 1))
            for half in range(2)][-1])(),
          reads=[buf("E"), ktm_bufs[tc]], writes=[bank[6], bank[7]])
    g("dve", lambda e: e.tensor_copy(Lsb, psum[0:64, 3072:4096]),
      reads=[bank[6], bank[7]], writes=[buf("Lpart")])
    dmag("pool", lambda e: [e.dma_start(cc1_in[:, :], Lsb)], 1,
         reads=[buf("Lpart")])

    def cc1(e):
        ins = e.collective_compute(
            "AllReduce", OP.add, replica_groups=RG,
            ins=[cc1_in[:, :].opt()], outs=[cc1_out[:, :].opt()])
        ins.then_inc(ccsem1, 1)
        e.wait_ge(ccsem1, 1)
        return e.engine_nop()
    gcc1 = g("pool", cc1, deps=[sched.by_eng["pool"][-1]])
    sched.barrier()

    if LIM < 7:
        return _finish(locals())
    sched.phase = "A4kfm"
    proj_fm(XRES + 8192, kfm_bufs, wkbuf, awk_ap, bksb, "kfm")
    sched.barrier()

    if LIM < 8:
        return _finish(locals())
    sched.phase = "A3vtm"
    proj_tm(WBLK, vtm_bufs, wvbuf, awb, bvr, "vtm", extra_w=[wkbuf])
    sched.barrier()

    if LIM < 9:
        return _finish(locals())
    sched.phase = "A5L"
    wqbuf = load_attn_w(wq_d, "q", awb, extra_w=[buf("awv")])
    gLl = dmag("pool", lambda e: [e.dma_start(Lsb, cc1_out[:, :])],
               1, writes=[buf("L")], deps=[gcc1])

    for tc in range(NTC):
        so = (tc % 2) * 1024
        gsq = g("dve", (lambda tc=tc, so=so: lambda e: [e.tensor_tensor(
            tmpa[:, so + dc * 128:so + dc * 128 + 128],
            bft(XRES + 8192 + dc * 1024, 2048)[:, tc * 128:tc * 128 + 128],
            bft(XRES + 8192 + dc * 1024, 2048)[:, tc * 128:tc * 128 + 128],
            OP.mult) for dc in range(NCH)][-1])(),
            reads=list(kfm_bufs), writes=[buf("ksq%d" % (tc % 2))])
        pk = 4 + tc % 2
        g("pe", (lambda tc=tc, so=so, pk=pk: lambda e: [e.matmul(
            psum[:, pk * 512 + 2 * dc:pk * 512 + 2 * dc + 2],
            tmpa[:, so + dc * 128:so + dc * 128 + 128], indsb[:, :],
            start=True, stop=True) for dc in range(NCH)][-1])(),
          reads=[buf("ksq%d" % (tc % 2)), buf("ind")], writes=[bank[pk]],
          deps=[gsq])
        g("dve", (lambda tc=tc, pk=pk: lambda e: e.tensor_scalar(
            knh[:, tc * NH:(tc + 1) * NH], psum[:, pk * 512:pk * 512 + NH],
            0.5, None, OP.mult))(),
          reads=[bank[pk]], writes=[buf("knh")])

    def lder(e):
        e.tensor_copy(Lbb[:, :], Lsb)
        e.tensor_tensor(lsq, Lbb[:, :], Lbb[:, :], OP.mult)
        e.tensor_reduce(lncol[:, :], lsq.rearrange("p (h d) -> p h d", d=64),
                        mybir.AxisListType.X, OP.add)
        return e.engine_nop()
    gld = g("dve", lder, reads=[buf("L")], writes=[buf("Lb"), buf("LnC")])
    g("act", lambda e: e.activation(elcol[:, :], lncol[:, :], AF.Exp,
                                    bias=0.0, scale=negr[0:64, :]),
      reads=[buf("LnC"), buf("temps")], writes=[buf("eL")])
    gt1 = g("pe", lambda e: e.transpose(psum[0:16, 1536:1536 + 64],
                                        lncol[:, :], ident[0:64, 0:64]),
            reads=[buf("LnC"), buf("ident")], writes=[bank[3]])
    g("dve", lambda e: e.tensor_copy(lnrows[:, :], psum[0:16, 1536:1536 + 64]),
      reads=[bank[3]], writes=[buf("lnrows")])

    def lnpair_dma(e):
        ops = []
        for h in range(NH):
            p = h // 2
            dst = Lnpair[0:64, p:p + 1] if h % 2 == 0 else Lnpair[64:128, p:p + 1]
            ops.append(e.dma_start(dst, lncol[0:64, h:h + 1],
                                   allow_slow_non_contiguous=True))
        return ops
    dmag("pool", lnpair_dma, NH, reads=[buf("LnC")], writes=[buf("lnpair")])

    for h in range(NH):
        gt = g("pe", (lambda h=h: lambda e: e.transpose(
            psum[0:64, 1024 + (h % 2) * 64:1024 + (h % 2) * 64 + 64],
            Lsb[:, h * 64:(h + 1) * 64], ident[0:64, 0:64]))(),
            reads=[buf("L"), buf("ident")], writes=[bank[2]])
        if h % 2 == 0:
            g("dve", (lambda h=h: lambda e: e.tensor_copy(
                Lfmb[0:64, (h // 2) * 64:(h // 2) * 64 + 64],
                psum[0:64, 1024:1024 + 64]))(),
              reads=[bank[2]], writes=[buf("lfm")], deps=[gt])
        else:
            gcv = g("dve", (lambda h=h: lambda e: e.tensor_copy(
                trsc[:, :], psum[0:64, 1088:1088 + 64]))(),
                reads=[bank[2]], writes=[buf("lfmtmp")], deps=[gt])
            dmag("pool", (lambda h=h: lambda e: [e.dma_start(
                Lfmb[64:128, (h // 2) * 64:(h // 2) * 64 + 64], trsc[:, :])])(),
                1, reads=[buf("lfmtmp")], writes=[buf("lfm")])
    g("dve", lambda e: [
        e.tensor_copy(Lfmze[0:64, :], Lfmb[0:64, :]),
        e.tensor_copy(Lfmzo[64:128, :], Lfmb[64:128, :])][-1],
      reads=[buf("lfm")], writes=[buf("lfmz")])
    sched.barrier()

    if LIM < 10:
        return _finish(locals())
    sched.phase = "A9qfm"
    proj_fm(XRES, qfm_bufs, wqbuf, awb, bqsb, "qfm", extra_w=list(ktm_bufs))
    sched.barrier()
    sched.phase = "A6W"
    for pp in range(0, 8, 2):
        gw1 = g("pe", (lambda pp=pp: lambda e: [
            e.matmul(psum[0 if half == 0 else 64:64 if half == 0 else 128,
                          q * 128:q * 128 + 64],
                     Lfmb[0 if half == 0 else 64:64 if half == 0 else 128,
                          (pp + q) * 64:(pp + q) * 64 + 64],
                     Lfmb[0 if half == 0 else 64:64 if half == 0 else 128,
                          (pp + q) * 64:(pp + q) * 64 + 64],
                     start=True, stop=True)
            for q in range(2) for half in range(2)][-1])(),
            reads=[buf("lfm")], writes=[bank[0]])
        gpr = dmag("pool", (lambda pp=pp: lambda e: [
            e.dma_start(lnpr2[i:i + 1, q * 64:q * 64 + 64],
                        lnrows[2 * (pp + q) + i:2 * (pp + q) + i + 1, :])
            for q in range(2) for i in range(2)])(), 4,
            reads=[buf("lnrows")], writes=[buf("lnpr2")])
        gw2 = g("pe", (lambda pp=pp: lambda e: [
            e.matmul(psum[:, 256 + q * 64:256 + q * 64 + 64], halfind[:, :],
                     lnpr2[:, q * 64:q * 64 + 64], start=True, stop=True)
            for q in range(2)][-1])(),
            reads=[buf("lnpr2"), buf("ones")], writes=[bank[0]],
            deps=[gpr, gw1])

        def wsq(e, pp=pp):
            for q in range(2):
                e.tensor_scalar(tmpa[:, q * 64:q * 64 + 64],
                                psum[:, q * 128:q * 128 + 64], -2.0,
                                Lnpair[:, pp + q:pp + q + 1],
                                OP.mult, OP.add)
            for q in range(2):
                e.tensor_tensor(tmpa[:, q * 64:q * 64 + 64],
                                tmpa[:, q * 64:q * 64 + 64],
                                psum[:, 256 + q * 64:256 + q * 64 + 64],
                                OP.add)
            last = None
            for q in range(2):
                last = e.tensor_scalar(tmpa[:, q * 64:q * 64 + 64],
                                       tmpa[:, q * 64:q * 64 + 64], 0.0,
                                       negr, OP.max, OP.mult)
            return last
        g("dve", wsq, reads=[bank[0], buf("lnpair"), buf("temps")],
          writes=[buf("wsq")])
        gw4 = g("act", (lambda pp=pp: lambda e: [
            e.activation(asb[:, (pp + q) * 132:(pp + q) * 132 + 64],
                         tmpa[:, q * 64:q * 64 + 64], AF.Exp,
                         bias=0.0, scale=1.0) for q in range(2)][-1])(),
            reads=[buf("wsq")],
            writes=[buf("asbW%d" % pp), buf("asbW%d" % (pp + 1))])
        g("dve", (lambda pp=pp: lambda e: [
            e.tensor_tensor(asb[:, (pp + q) * 132:(pp + q) * 132 + 64],
                            asb[:, (pp + q) * 132:(pp + q) * 132 + 64],
                            iepsdual[:, :], OP.add) for q in range(2)][-1])(),
          reads=[buf("asbW%d" % pp), buf("asbW%d" % (pp + 1)), buf("idual")],
          writes=[buf("asbW%d" % pp), buf("asbW%d" % (pp + 1))])

    if LIM < 11:
        return _finish(locals())
    sched.phase = "A7phiK"
    wobuf = load_attn_w(wo_d, "o", awb, extra_w=[buf("awq")])
    for tc in range(NTC):
        pc = (tc % 2) * 2
        cb = [bank[pc], bank[pc + 1]]

        def cmm(e, tc=tc, pc=pc):
            last = None
            for dc in range(NCH):
                last = e.matmul(
                    psum[0:128, pc * 512 + 2 * dc * 64:
                         pc * 512 + 2 * dc * 64 + 64],
                    bft(XRES + 8192 + dc * 1024, 2048)
                    [:, tc * 128:tc * 128 + 128],
                    Lfmze[:, dc * 64:dc * 64 + 64],
                    start=True, stop=True)
                last = e.matmul(
                    psum[0:128, pc * 512 + (2 * dc + 1) * 64:
                         pc * 512 + (2 * dc + 1) * 64 + 64],
                    bft(XRES + 8192 + dc * 1024, 2048)
                    [:, tc * 128:tc * 128 + 128],
                    Lfmzo[:, dc * 64:dc * 64 + 64],
                    start=True, stop=True)
            return last
        g("pe", cmm, reads=[buf("lfmz")] + kfm_bufs, writes=cb)
        if SUB < 1:
            continue
        gsb = g("dve", (lambda tc=tc, pc=pc: lambda e: e.tensor_tensor(
            tmpa[:, 0:1024].rearrange("p (h d) -> p h d", d=64),
            psum[:, pc * 512:pc * 512 + 1024].rearrange(
                "p (h d) -> p h d", d=64),
            knh[:, tc * NH:(tc + 1) * NH].rearrange(
                "p (h o) -> p h o", o=1).broadcast_to([128, NH, 64]),
            OP.subtract))(),
            reads=cb + [buf("knh")], writes=[buf("ck")])
        gex = g("act", (lambda tc=tc: lambda e: e.activation(
            bftc[:, 0:1024], tmpa[:, 0:1024], AF.Exp, bias=0.0,
            scale=rcol2))(),
            reads=[buf("ck"), buf("temps")], writes=[buf("phik")])
        if SUB < 2:
            continue
        for h in range(NH):
            g("pe", (lambda tc=tc, h=h: lambda e: [
                e.matmul(psum[0:64, 3072 + h * 64:3072 + h * 64 + 64],
                         bftc[:, h * 64:h * 64 + 64],
                         bft(WBLK + tc * 512, 1024)[:, h * 64:h * 64 + 64],
                         start=(tc == 0 and h % 8 == 0),
                         stop=(tc == NTC - 1 and h % 8 == 7)),
                e.matmul(psum[0:64, 2560 + h:2560 + h + 1],
                         bftc[:, h * 64:h * 64 + 64], onesb[:, 0:1],
                         start=(tc == 0 and h == 0),
                         stop=(tc == NTC - 1 and h == NH - 1))][-1])(),
              reads=[buf("phik"), vtm_bufs[tc], buf("ones")],
              writes=[bank[6], bank[7], bank[5]])

    if SUB < 3:
        return _finish(locals())

    def zstage(e):
        e.tensor_tensor(cc2st[:, 0:1024].rearrange("p (h d) -> p h d", d=64),
                        psum[0:64, 3072:4096].rearrange(
                            "p (h d) -> p h d", d=64),
                        elcol[:, :].rearrange("p (h o) -> p h o", o=1)
                        .broadcast_to([64, NH, 64]), OP.mult)
        return e.tensor_tensor(cc2st[:, 1024:1040], psum[0:64, 2560:2576],
                               elcol[:, :], OP.mult)
    g("dve", zstage, reads=[bank[5], bank[6], bank[7], buf("eL")],
      writes=[buf("cc2st")])
    dmag("pool", lambda e: [e.dma_start(cc2_in[:, :], cc2st)], 1,
         reads=[buf("cc2st")])

    if SUB < 4:
        return _finish(locals())

    def cc2(e):
        ins = e.collective_compute(
            "AllReduce", OP.add, replica_groups=RG,
            ins=[cc2_in[:, :].opt()], outs=[cc2_out[:, :].opt()])
        ins.then_inc(ccsem2, 1)
        e.wait_ge(ccsem2, 1)
        return e.engine_nop()
    gcc2 = g("pool", cc2, deps=[sched.by_eng["pool"][-1]])

    def zload(e):
        ops = []
        for h in range(NH):
            p0 = (h % 2) * 64
            ops.append(e.dma_start(
                asb[p0:p0 + 64, (h // 2) * 132 + 64:(h // 2) * 132 + 128],
                cc2_out[0:64, h * 64:(h + 1) * 64]))
            ops.append(e.dma_start(
                asb[p0:p0 + 64, (h // 2) * 132 + 128:(h // 2) * 132 + 129],
                cc2_out[0:64, 1024 + h:1024 + h + 1],
                allow_slow_non_contiguous=True))
        return ops
    dmag("pool", zload, 32, writes=[buf("asbZ")], deps=[gcc2])
    sched.barrier()

    if LIM < 12:
        return _finish(locals())
    sched.phase = "A8solve"
    asb_buf = buf("asball")
    sel = f32(TMPB + 1920, 128)
    for k in range(NL):
        extra = ([buf("asbZ")] + [buf("asbW%d" % p) for p in range(8)]
                 if k == 0 else [asb_buf])
        gsel = g("dve", (lambda k=k: lambda e: [
            e.tensor_scalar(sel[:, 0:64], iota_c.broadcast_to([128, 64]),
                            float(k), None, OP.is_equal),
            e.tensor_scalar(sel[:, 64:128], iota_c.broadcast_to([128, 64]),
                            float(64 + k), None, OP.is_equal)][-1])(),
            reads=[buf("iota")], writes=[buf("sel")])
        gbc = g("pe", (lambda k=k: lambda e: [
            e.matmul(psum[:, 0:512], sel, asb[:, 0:512],
                     start=True, stop=True),
            e.matmul(psum[:, 512:1024], sel, asb[:, 512:1024],
                     start=True, stop=True),
            e.matmul(psum[:, 1024:1056], sel, asb[:, 1024:1056],
                     start=True, stop=True)][-1])(),
            reads=extra + [buf("sel")], writes=[bank[0], bank[1], bank[2]],
            deps=[gsel])

        def solstep(e, k=k):
            for q in range(2):
                e.reciprocal(sm[:, 32 + q * 4:36 + q * 4],
                             psum[:, k:1056:132][:, q * 4:q * 4 + 4])
            for q in range(2):
                e.tensor_tensor(sm[:, 40 + q * 4:44 + q * 4],
                                identdual[:, k:k + 1].broadcast_to([128, 4]),
                                asb[:, k::132][:, q * 4:q * 4 + 4],
                                OP.subtract)
            for q in range(2):
                e.tensor_tensor(sm[:, 48 + q * 4:52 + q * 4],
                                sm[:, 40 + q * 4:44 + q * 4],
                                sm[:, 32 + q * 4:36 + q * 4], OP.mult)
            for q in range(2):
                e.tensor_tensor(
                    f32(TMPA + q * 528, 528).rearrange(
                        "p (b n) -> p b n", n=132)[:, :, k:132],
                    psum[:, q * 528:q * 528 + 528].rearrange(
                        "p (b n) -> p b n", n=132)[:, :, k:132],
                    sm[:, 48 + q * 4:52 + q * 4].rearrange(
                        "p (b o) -> p b o", o=1)
                    .broadcast_to([128, 4, 132])[:, :, k:132], OP.mult)
            last = None
            for q in range(2):
                last = e.tensor_tensor(
                    asb[:, q * 528:q * 528 + 528].rearrange(
                        "p (b n) -> p b n", n=132)[:, :, k:132],
                    asb[:, q * 528:q * 528 + 528].rearrange(
                        "p (b n) -> p b n", n=132)[:, :, k:132],
                    f32(TMPA + q * 528, 528).rearrange(
                        "p (b n) -> p b n", n=132)[:, :, k:132], OP.add)
            return last
        g("dve", solstep, reads=[bank[0], bank[1], bank[2], buf("idual")],
          writes=[asb_buf])
        if k % 8 == 7:
            p = k // 8
            cqb = [bank[3], bank[4], bank[5], bank[6]]

            def cqmm_e(e, p=p):
                last = None
                for nb in range(NTB):
                    last = e.matmul(
                        psum[0:64, 1536 + nb * 512:1536 + nb * 512 + 512],
                        Lfmb[0:64, p * 64:p * 64 + 64],
                        bft(XRES + p * 1024, 2048)
                        [0:64, nb * 512:nb * 512 + 512],
                        start=True, stop=True)
                    last = e.matmul(
                        psum[64:128, 1536 + nb * 512:1536 + nb * 512 + 512],
                        Lfmb[64:128, p * 64:p * 64 + 64],
                        bft(XRES + p * 1024, 2048)
                        [64:128, nb * 512:nb * 512 + 512],
                        start=True, stop=True)
                return last
            g("pe", cqmm_e, reads=[buf("lfm"), qfm_bufs[p]], writes=cqb)
            for ch in range(4):
                gts = g("dve", (lambda p=p, ch=ch: lambda e: e.tensor_scalar(
                    f32(BFTC + (ch % 2) * 512, 512),
                    psum[:, 1536 + ch * 512:1536 + (ch + 1) * 512], 2.0,
                    Lnpair[:, p:p + 1], OP.mult, OP.subtract))(),
                    reads=[cqb[ch], buf("lnpair")],
                    writes=[buf("cqch%d" % (ch % 2))])
                g("act", (lambda p=p, ch=ch: lambda e: e.activation(
                    bft(XRES + 8192 + p * 1024 + ch * 256, 512),
                    f32(BFTC + (ch % 2) * 512, 512), AF.Exp,
                    bias=0.0, scale=rcol))(),
                  reads=[buf("cqch%d" % (ch % 2)), buf("temps")],
                  writes=[buf("phiq%d" % p)], deps=[gts])
    def ycopy(e):
        e.memset(sm[:, 9:10], 0.0)
        return e.tensor_copy(
            ybsb[:, :].rearrange("p (b n) -> p b n", n=65),
            asb[:, :].rearrange("p (b n) -> p b n", n=132)[:, :, 64:129])
    g("dve", ycopy, reads=[asb_buf], writes=[buf("yb")])
    sched.barrier()

    if LIM < 13:
        return _finish(locals())
    if LIM < 14:
        return _finish(locals())
    sched.phase = "A10phiQ"
    for p in range(8):
        phiqp = bft(XRES + 8192 + p * 1024, 2048)
        for nb in range(NTB):
            g("pe", (lambda p=p, nb=nb, phiqp=phiqp: lambda e: [
                e.matmul(psum[0:64, (4 + nb) * 512:(4 + nb) * 512 + 512],
                         ybsb[0:64, p * 65:p * 65 + 64],
                         phiqp[0:64, nb * 512:nb * 512 + 512],
                         start=True, stop=True),
                e.matmul(psum[64:128, (4 + nb) * 512:(4 + nb) * 512 + 512],
                         ybsb[64:128, p * 65:p * 65 + 64],
                         phiqp[64:128, nb * 512:nb * 512 + 512],
                         start=True, stop=True),
                e.matmul(psum[0:1, nb * 512:nb * 512 + 512],
                         ybsb[0:64, p * 65 + 64:p * 65 + 65],
                         phiqp[0:64, nb * 512:nb * 512 + 512],
                         start=True, stop=True),
                e.matmul(psum[64:65, nb * 512:nb * 512 + 512],
                         ybsb[64:128, p * 65 + 64:p * 65 + 65],
                         phiqp[64:128, nb * 512:nb * 512 + 512],
                         start=True, stop=True)][-1])(),
              reads=[buf("yb"), buf("phiq%d" % p)],
              writes=[bank[4 + nb], bank[nb]])
        g("dve", (lambda p=p: lambda e: e.tensor_copy(
            bft(XRES + 8192 + p * 1024, 2048), psum[:, 2048:4096]))(),
          reads=[bank[4], bank[5], bank[6], bank[7]],
          writes=[buf("attnb%d" % p), buf("phiq%d" % p)])
        gnv = g("dve", (lambda p=p: lambda e: [
            e.tensor_copy(AR[0:1, TMPB:TMPB + 2048], psum[0:1, 0:2048]),
            e.tensor_copy(AR[64:65, TMPB:TMPB + 2048], psum[64:65, 0:2048])][-1])(),
            reads=[bank[0], bank[1], bank[2], bank[3]], writes=[buf("nv")])
        dmag("pool", (lambda p=p: lambda e: [
            e.dma_start(norm16[2 * p:2 * p + 1, :], AR[0:1, TMPB:TMPB + 2048]),
            e.dma_start(norm16[2 * p + 1:2 * p + 2, :],
                        AR[64:65, TMPB:TMPB + 2048])])(), 2,
            reads=[buf("nv")], writes=[buf("n16_%d" % p)])

    def rncomp(e):
        e.tensor_scalar(norm16[:, :], norm16[:, :], 1e-10, None, OP.max)
        e.reciprocal(norm16[:, :], norm16[:, :])
        return e.tensor_copy(rnb16[:, :], norm16[:, :])
    g("dve", rncomp, reads=[buf("n16_%d" % p) for p in range(8)],
      writes=[buf("rn")])
    for p in range(8):
        dmag("pool", (lambda p=p: lambda e: [
            e.dma_start(ARB[0:1, 2 * TMPB:2 * TMPB + 2048],
                        rnb16[2 * p:2 * p + 1, :]),
            e.dma_start(ARB[64:65, 2 * TMPB:2 * TMPB + 2048],
                        rnb16[2 * p + 1:2 * p + 2, :])])(), 2,
            reads=[buf("rn")], writes=[buf("rnp0")])
        for nb in range(NTB):
            gbb = g("pe", (lambda nb=nb: lambda e: [
                e.matmul(psum[0:64, nb * 512:nb * 512 + 512],
                         onesrb[0:1, 0:64],
                         ARB[0:1, 2 * TMPB + nb * 512:2 * TMPB + nb * 512 + 512],
                         start=True, stop=True),
                e.matmul(psum[64:128, nb * 512:nb * 512 + 512],
                         ones2[64:65, :],
                         ARB[64:65, 2 * TMPB + nb * 512:
                             2 * TMPB + nb * 512 + 512],
                         start=True, stop=True)][-1])(),
                reads=[buf("rnp0"), buf("ones")], writes=[bank[nb]])
            gcp = g("dve", (lambda nb=nb: lambda e: e.tensor_copy(
                bftc[:, nb * 512:nb * 512 + 512],
                psum[:, nb * 512:nb * 512 + 512]))(),
                reads=[bank[nb]], writes=[buf("rnbc")], deps=[gbb])
            g("dve", (lambda p=p, nb=nb: lambda e: e.tensor_tensor(
                bft(XRES + 8192 + p * 1024, 2048)[:, nb * 512:nb * 512 + 512],
                bft(XRES + 8192 + p * 1024, 2048)[:, nb * 512:nb * 512 + 512],
                bftc[:, nb * 512:nb * 512 + 512], OP.mult))(),
              reads=[buf("attnb%d" % p), buf("rnbc")],
              writes=[buf("attnb%d" % p)], deps=[gcp])
    sched.barrier()

    if LIM < 15:
        return _finish(locals())
    sched.phase = "A11wo"
    x2_fn = lambda dc: (f32(XHAT + dc * 2048, 2048) if dc < 4 else
                        f32(XRES + (dc - 4) * 2048, 2048))
    x2_bufs = [buf("x2_%d" % dc) for dc in range(NCH)]
    x2b_fn = lambda dc: bft(WBLK + dc * 1024, 2048)
    x2b_bufs = [buf("x2b_%d" % dc) for dc in range(NCH)]
    for oc in range(NCH):
        pb = (oc % 2) * 4
        pbufs = [bank[pb + i] for i in range(4)]

        def mmo(e, oc=oc, pb=pb):
            last = None
            for kc in range(NCH):
                for nb in range(NTB):
                    last = e.matmul(
                        psum[:, (pb + nb) * 512:(pb + nb) * 512 + 512],
                        awb[:, kc * 1024 + oc * 128:kc * 1024 + oc * 128 + 128],
                        bft(XRES + 8192 + kc * 1024, 2048)
                        [:, nb * 512:nb * 512 + 512],
                        start=(kc == 0), stop=(kc == NCH - 1))
            return last
        g("pe", mmo, reads=[wobuf] + [buf("attnb%d" % p) for p in range(8)],
          writes=pbufs)
        gx1 = dmag("sp", (lambda oc=oc: lambda e: [e.dma_start(
            tmpb[:, :], x1_d[oc * 128:(oc + 1) * 128, :])])(), 1,
            writes=[buf("x1t")])
        g("act", (lambda oc=oc, pb=pb: lambda e: e.activation(
            tmpa[:, :], psum[:, pb * 512:pb * 512 + 2048],
            AF.Identity, bias=bosb[:, oc:oc + 1], scale=1.0))(),
          reads=pbufs + [buf("consts")], writes=[buf("at")])
        g("dve", (lambda oc=oc: lambda e: e.tensor_tensor(
            x2_fn(oc), tmpa[:, :], tmpb[:, :], OP.add))(),
          reads=[buf("at"), buf("x1t")], writes=[x2_bufs[oc]])
        g("dve", (lambda oc=oc: lambda e: e.tensor_copy(
            x2b_fn(oc), x2_fn(oc)))(),
          reads=[x2_bufs[oc]], writes=[x2b_bufs[oc]])
    sched.barrier()

    if LIM < 16:
        return _finish(locals())
    sched.phase = "LN3"
    gst3 = ln_stats(x2b_fn, x2b_bufs, "L3")
    xhat3_fn = lambda dc: bft(XRES + 8192 + dc * 1024, 2048)
    xhat3_b = [buf("xh3_%d" % dc) for dc in range(NCH)]
    ln_apply(x2_fn, x2_bufs, xhat3_fn, xhat3_b, "L3", gst3)
    sched.barrier()
    sched.phase = "FFN3"
    ffn_mm1(xhat3_fn, xhat3_b, "F3", False)
    ffn_mm2(x2_fn, x2_bufs, None, None, out_d, "F3", False)
    sched.barrier()

    return _finish(locals())


def _finish(L):
    nc = L["nc"]; sched = L["sched"]; sems = L["sems"]; dslot = L["dslot"]
    ctx = L["ctx"]; ENGINES_ = ENGINES
    if L["LIM"] < 16:
        f32 = L["f32"]; XRES = L["XRES"]; out_d = L["out_d"]
        dmagf = L["dmag"]
        dmagf("pool", lambda e: [e.dma_start(
            out_d[c * 128:(c + 1) * 128, :], f32(XRES + c * T, T))
            for c in range(NCH)], NCH)
        dbg_d = L["dbg_d"]
        import os as _os
        _skip_dbg = _os.environ.get("KDBG", "1") == "0"
        dumps = []
        col = [0]
        def dump(ap, w):
            c0 = col[0]
            dumps.append((ap, c0, w))
            col[0] += w
        dump(L["Lfmb"][:, :].bitcast(F32), 256)
        dump(L["Lnpair"][:, :], 8)
        dump(L["knh"][:, :], 256)
        dump(L["asb"], 1056)
        dump(L["ybsb"][:, :].bitcast(F32), 260)
        dump(L["elcol"][:, :], 16)
        dump(L["lncol"][:, :], 16)
        dump(L["AR"][:, L["WSTG"]:L["WSTG"] + 1024], 1024)
        dump(L["AR"][:, L["WBLK"]:L["WBLK"] + 2048], 2048)
        dump(L["Esb"][:, :].bitcast(F32), 512)
        dump(L["sm"][:, :], 256)
        def dofl(e):
            return [e.dma_start(dbg_d[0:a.shape[0], c0:c0 + w], a)
                    for (a, c0, w) in dumps]
        if not _skip_dbg:
            dmagf("pool", dofl, len(dumps))
    for grp in sched.groups:
        for d in grp.deps:
            if d.dma or d.eng != grp.eng:
                d.need_inc = True
    NSLOT_ = 10
    for e in ENGINES:
        cum = 0
        slot_tot = [0] * NSLOT_
        didx = 0
        for grp in sched.by_eng[e]:
            if grp.dma:
                s = didx % NSLOT_
                grp.slot = s
                grp.slot_prev = slot_tot[s]
                slot_tot[s] += 16 * grp.ndma
                grp.cum = slot_tot[s]
                didx += 1
            else:
                if grp.need_inc:
                    cum += 1
                grp.cum = cum

    with nc.Block() as block:
        def mkrun(eng_name):
            def body(eng):
                waited = {}
                for grp in sched.by_eng[eng_name]:
                    for d in sorted(grp.deps, key=lambda gg: gg.gid):
                        if d.dma:
                            sem, tgt = dslot[d.eng][d.slot], d.cum
                        elif d.eng != eng_name:
                            sem, tgt = sems[d.eng], d.cum
                        else:
                            continue
                        key = id(sem)
                        if waited.get(key, -1) < tgt:
                            eng.wait_ge(sem, tgt)
                            waited[key] = tgt
                    if grp.dma:
                        ssem = dslot[eng_name][grp.slot]
                        if grp.slot_prev > 0 and waited.get(id(ssem), -1) < grp.slot_prev:
                            eng.wait_ge(ssem, grp.slot_prev)
                            waited[id(ssem)] = grp.slot_prev
                        with nc.named_scope(grp.phase):
                            for ins in grp.emit(eng):
                                ins.then_inc(ssem, 16)
                    else:
                        with nc.named_scope(grp.phase):
                            last = grp.emit(eng)
                        if grp.need_inc:
                            last.then_inc(sems[eng_name], 1)
                if eng_name == "pool":
                    for i, s in enumerate(dslot["pool"]):
                        tot = 0
                        for x in sched.by_eng["pool"]:
                            if x.dma and x.slot == i:
                                tot = x.cum
                        if tot > 0:
                            eng.wait_ge(s, tot)
            return body
        block.tensor(mkrun("pe"))
        block.vector(mkrun("dve"))
        block.scalar(mkrun("act"))
        block.gpsimd(mkrun("pool"))
        block.sync(mkrun("sp"))
    ctx.close()
    from concourse.library_overlay import lower_extended_insts
    lower_extended_insts(nc)
    return nc


_NC_CACHE = None
_LAST_RES = None


def kernel(**inputs):
    global _NC_CACHE
    x = np.ascontiguousarray(np.asarray(inputs["L0_attn_query"], np.float32))
    Wq = np.asarray(inputs["L0_attn_Wq"], np.float32)
    Wk = np.asarray(inputs["L0_attn_Wk"], np.float32)
    Wv = np.asarray(inputs["L0_attn_Wv"], np.float32)
    Wo = np.asarray(inputs["L0_attn_Wo"], np.float32)
    bq = np.asarray(inputs["L0_attn_bq"], np.float32)
    bk = np.asarray(inputs["L0_attn_bk"], np.float32)
    bv = np.asarray(inputs["L0_attn_bv"], np.float32)
    bo = np.asarray(inputs["L0_attn_bo"], np.float32)
    temp = np.asarray(inputs["L0_attn_temperature"], np.float32).reshape(1, 1)
    lidx = np.asarray(inputs["L0_attn_landmark_idx"], np.int32).reshape(1, NL)
    w1 = np.asarray(inputs["L0_ffn_w1"], np.float32)
    b1 = np.asarray(inputs["L0_ffn_b1"], np.float32)
    w2 = np.asarray(inputs["L0_ffn_w2"], np.float32)
    b2 = np.asarray(inputs["L0_ffn_b2"], np.float32)

    def colmajor(b):
        return np.ascontiguousarray(b.reshape(-1, 128).T)

    base = {
        "w1": np.ascontiguousarray(w1),
        "w2": np.ascontiguousarray(w2),
        "wqt": np.ascontiguousarray(Wq.T),
        "wkt": np.ascontiguousarray(Wk.T),
        "wvt": np.ascontiguousarray(Wv.T),
        "wot": np.ascontiguousarray(Wo.T),
        "b1": colmajor(b1),
        "b2": colmajor(b2),
        "bq": colmajor(bq),
        "bk": colmajor(bk),
        "bv": colmajor(bv),
        "bo": colmajor(bo),
        "bkrow": np.ascontiguousarray(bk.reshape(1, -1)),
        "bvrow": np.ascontiguousarray(bv.reshape(1, -1)),
        "lidx": lidx,
        "temp": temp,
        "hind": np.float32([[1.0] * 64 + [0.0] * 64,
                            [0.0] * 64 + [1.0] * 64]),
    }
    if _NC_CACHE is None:
        _NC_CACHE = build_nc()
    nc = _NC_CACHE
    in_maps = []
    for c in range(8):
        b, h = c // 2, c % 2
        m = dict(base)
        m["x"] = np.ascontiguousarray(x[b, h * T:(h + 1) * T, :].T)
        m["coff"] = np.float32([[h * T]])
        in_maps.append(m)
    res = run_bass_kernel_spmd(nc, in_maps, core_ids=list(range(8)))
    global _LAST_RES
    _LAST_RES = res
    out = np.empty((4, 2 * T, D), np.float32)
    for c in range(8):
        b, h = c // 2, c % 2
        out[b, h * T:(h + 1) * T, :] = np.asarray(res.results[c]["out"]).T
    return out
